# version 02
# baseline (speedup 1.0000x reference)
import math
import numpy as np
import ml_dtypes
import concourse.bass as bass
import concourse.mybir as mybir
from concourse.bass_utils import run_bass_kernel_spmd

F32 = mybir.dt.float32
BF16 = mybir.dt.bfloat16
AF = mybir.ActivationFunctionType
ALU = mybir.AluOpType
AX = mybir.AxisListType

ENGINES = ("sync", "gpsimd", "tensor", "vector", "scalar")


class Prog:

    def __init__(self, nc):
        self.nc = nc
        self.steps = []
        self.dmcnt = {e: 0 for e in ENGINES}
        self.cccnt = 0
        self.cur_eng = None
        self.sems = {}

    def step(self, eng, fn):
        self.steps.append((eng, fn))

    def dma(self, e, out, in_, **kw):
        n = self.cur_eng
        self.dmcnt[n] += 1
        return e.dma_start(out=out, in_=in_, **kw).then_inc(self.sems["dm_" + n], 16)

    def dmaT(self, e, out, in_):
        n = self.cur_eng
        self.dmcnt[n] += 1
        return e.dma_start_transpose(out, in_).then_inc(self.sems["dm_" + n], 16)

    def cc(self, e, kind, op, groups, ins, outs):
        self.cccnt += 1
        return e.collective_compute(kind, op, groups,
                                    [a.opt() for a in ins],
                                    [a.opt() for a in outs]).then_inc(self.sems["cc"])

    def emit(self, blk, sems):
        self.sems = sems
        ch = sems["ch"]
        cc_before_step = []
        def make_section(name):
            def sec(eng):
                self.cur_eng = name
                for i, (ename, fn) in enumerate(self.steps):
                    if ename != name:
                        continue
                    eng.wait_ge(ch, i)
                    dm_before = self.dmcnt[name]
                    cc_before = self.cccnt
                    last = fn(eng)
                    need_tail = False
                    if self.dmcnt[name] != dm_before:
                        eng.wait_ge(sems["dm_" + name], 16 * self.dmcnt[name])
                        need_tail = True
                    if self.cccnt != cc_before:
                        eng.wait_ge(sems["cc"], self.cccnt)
                        need_tail = True
                    if need_tail:
                        eng.nop().then_inc(ch, 1)
                    else:
                        assert last is not None, f"step {i} ({name}) returned None with no dma/cc"
                        last.then_inc(ch, 1)
            return sec
        blk.sync(make_section("sync"))
        blk.gpsimd(make_section("gpsimd"))
        blk.tensor(make_section("tensor"))
        blk.vector(make_section("vector"))
        blk.scalar(make_section("scalar"))
        return len(self.steps)


def ceil_div(a, b):
    return (a + b - 1) // b



def load_vec_as_cols(P, sync_eng, sb_cols, dram_vec, n):
    nch = n // 128
    P.dma(sync_eng, sb_cols[:, 0:nch],
          dram_vec.rearrange("(c p) -> p c", p=128),
          allow_slow_non_contiguous=True)


class G:
    pass


def alloc_globals(P, nc, ex):
    g = G()
    g.ones_row = ex.enter_context(nc.sbuf_tensor("g_ones", [1, 128], F32))
    g.eps_col = ex.enter_context(nc.sbuf_tensor("g_eps", [128, 1], F32))
    g.negk0_col = ex.enter_context(nc.sbuf_tensor("g_negk0", [128, 1], F32))
    g.ones_colb = ex.enter_context(nc.sbuf_tensor("g_onescb", [128, 1], BF16))
    g.identb2 = ex.enter_context(nc.sbuf_tensor("g_identb2", [128, 128], BF16))
    def init(e):
        e.memset(g.ones_row[:], 1.0)
        e.memset(g.negk0_col[:], -8.0)
        e.memset(g.ones_colb[:], 1.0)
        return e.memset(g.eps_col[:], 1e-5)
    P.step("vector", init)
    def initg(e):
        return e.iota(g.identb2[:], [[1, 128]], base=0, channel_multiplier=-1,
                      allow_small_or_imprecise_dtypes=True)
    P.step("gpsimd", initg)
    def initg2(e):
        return e.tensor_single_scalar(g.identb2[:], g.identb2[:], 0.0,
                                      ALU.is_equal)
    P.step("vector", initg2)
    return g


def emit_postln(P, g, oacc, stats, out_sb, D=1024, scr=None):
    ssum = stats[:, 0:1]; negm = stats[:, 1:2]; ssq = stats[:, 2:3]
    lnv = stats[:, 3:4]; rstd = stats[:, 4:5]
    P.step("vector", lambda e: e.reduce_sum(ssum, oacc[:], axis=AX.X))
    P.step("vector", lambda e: e.tensor_scalar_mul(negm, ssum, -1.0 / D))
    def s2(e):
        return e.activation(scr[:, 0:D], oacc[:], AF.Square, bias=negm,
                            accum_out=ssq)
    P.step("scalar", s2)
    P.step("scalar", lambda e: e.activation(lnv, ssq, AF.Ln, scale=1.0 / D,
                                            bias=g.eps_col[:]))
    P.step("scalar", lambda e: e.activation(rstd, lnv, AF.Exp, scale=-0.5))
    def s4(e):
        return e.tensor_scalar(out_sb[:], oacc[:], negm, rstd,
                               ALU.add, ALU.mult)
    P.step("vector", s4)


def emit_l0_attn(P, nc, ex, g, io, T):
    D = 1024
    NT = T // 128
    NSUP = T // 1024
    NQB = T // 512

    from contextlib import ExitStack
    exB = ExitStack()
    sb = lambda n, s, d: ex.enter_context(nc.sbuf_tensor(n, s, d))
    ps = lambda n, s, d: ex.enter_context(nc.psum_tensor(n, s, d))
    sbB = lambda n, s, d: exB.enter_context(nc.sbuf_tensor(n, s, d))
    psB = lambda n, s, d: exB.enter_context(nc.psum_tensor(n, s, d))

    qhatT = sb("qhatT", [128, T], BF16)
    kT = sb("kT", [128, T], BF16)
    oscol = sb("oscol", [128, 4], F32)
    bout_row = sb("bout_row", [1, 1024], F32)
    sqscr = sb("sqscr", [128, 1024], F32)
    Whg = sbB("Whg", [128, 8, 4096], BF16)
    Wqkg = sbB("Wqkg", [128, 8, 128], BF16)
    beff_h = sbB("beff_h", [1, 4096], F32)
    beff_qk = sbB("beff_qk", [1, 128], F32)
    lng_c = sbB("lng_c", [128, 8], F32)
    lnb_c = sbB("lnb_c", [128, 8], BF16)
    bh_row = sbB("bh_row", [1, 4096], F32)
    bqk_row = sbB("bqk_row", [1, 128], F32)
    riota = sbB("riota", [1, 128], F32)
    pr0 = psB("pr0", [1, 512], F32)
    pr1 = psB("pr1", [1, 512], F32)

    def p0_sync(e):
        load_vec_as_cols(P, e, lng_c, io["ln_g"], D)
        P.dma(e, oscol[:, 0:1], io["os_gamma"][0:1, :].rearrange("a p -> p a"),
              allow_slow_non_contiguous=True)
        P.dma(e, oscol[:, 1:2], io["os_gamma"][1:2, :].rearrange("a p -> p a"),
              allow_slow_non_contiguous=True)
        P.dma(e, oscol[:, 2:3], io["os_beta"][0:1, :].rearrange("a p -> p a"),
              allow_slow_non_contiguous=True)
        P.dma(e, oscol[:, 3:4], io["os_beta"][1:2, :].rearrange("a p -> p a"),
              allow_slow_non_contiguous=True)
        P.dma(e, bh_row[:], io["b_hidden"].rearrange("(a f) -> a f", a=1))
        P.dma(e, bqk_row[:], io["b_qk"].rearrange("(a f) -> a f", a=1))
        P.dma(e, bout_row[:], io["b_out"].rearrange("(a f) -> a f", a=1))
    P.step("sync", p0_sync)

    def p0_gps(e):
        load_vec_as_cols(P, e, lnb_c, io["ln_b"], D)
        for c in range(8):
            P.dma(e, Whg[:, c, :], io["W_hidden"][c * 128:(c + 1) * 128, :])
            P.dma(e, Wqkg[:, c, :], io["W_qk"][c * 128:(c + 1) * 128, :])
        return None
    P.step("gpsimd", p0_gps)


    for fb in range(8):
        def tmm(e, fb=fb):
            p = pr0 if fb % 2 == 0 else pr1
            last = None
            for c in range(8):
                last = e.matmul(p[:], lnb_c[:, c:c + 1],
                                Whg[:, c, fb * 512:(fb + 1) * 512],
                                start=(c == 0), stop=(c == 7))
            return last
        def tev(e, fb=fb):
            p = pr0 if fb % 2 == 0 else pr1
            return e.tensor_add(beff_h[:, fb * 512:(fb + 1) * 512], p[:],
                                bh_row[:, fb * 512:(fb + 1) * 512])
        P.step("tensor", tmm)
        P.step("vector", tev)
    def tmm_qk(e):
        last = None
        for c in range(8):
            last = e.matmul(pr0[:, 0:128], lnb_c[:, c:c + 1], Wqkg[:, c, :],
                            start=(c == 0), stop=(c == 7))
        return last
    P.step("tensor", tmm_qk)
    def tev_qk(e):
        return e.tensor_add(beff_qk[:], pr0[:, 0:128], bqk_row[:])
    P.step("vector", tev_qk)

    def wscale(e):
        last = None
        for c in range(8):
            e.tensor_scalar_mul(Whg[:, c, :], Whg[:, c, :], lng_c[:, c:c + 1])
            last = e.tensor_scalar_mul(Wqkg[:, c, :], Wqkg[:, c, :],
                                       lng_c[:, c:c + 1])
        return last
    P.step("vector", wscale)

    exA = ExitStack()
    sbA = lambda n, s, d: exA.enter_context(nc.sbuf_tensor(n, s, d))
    NA = 4
    sxA = sbA("sxA", [128, NA, 1024], F32)
    snrmA = sbA("snrmA", [128, NA, 1024], BF16)
    statA = sbA("statA", [128, 32], F32)
    sq8 = sbA("sq8", [128, NA, 1024], F32)
    for sup in range(T // (128 * NA)):
        ts = sup * 128 * NA
        def a1(e, ts=ts):
            P.dma(e, sxA[:], io["x"][ts:ts + 128 * NA, :]
                  .rearrange("(a p) d -> p a d", p=128))
        P.step("sync", a1)
        P.step("vector", lambda e: e.tensor_reduce(statA[:, 0:NA], sxA[:],
                                                   AX.X, ALU.add))
        P.step("vector", lambda e: e.tensor_scalar_mul(statA[:, 8:8 + NA],
                                                       statA[:, 0:NA], -1.0 / D))
        def a3(e):
            last = None
            for a in range(NA):
                last = e.activation(sq8[:, a, :], sxA[:, a, :], AF.Square,
                                    bias=statA[:, 8 + a:9 + a],
                                    accum_out=statA[:, 16 + a:17 + a])
            return last
        P.step("scalar", a3)
        P.step("scalar", lambda e: e.activation(statA[:, 24:24 + NA],
                                                statA[:, 16:16 + NA], AF.Ln,
                                                scale=1.0 / D,
                                                bias=g.eps_col[:]))
        P.step("scalar", lambda e: e.activation(statA[:, 24:24 + NA],
                                                statA[:, 24:24 + NA], AF.Exp,
                                                scale=-0.5))
        def a5(e):
            last = None
            for a in range(NA):
                last = e.tensor_scalar(snrmA[:, a, :], sxA[:, a, :],
                                       statA[:, 8 + a:9 + a],
                                       statA[:, 24 + a:25 + a],
                                       ALU.add, ALU.mult)
            return last
        P.step("vector", a5)
        def a6(e, ts=ts):
            P.dma(e, io["nrm_hbm"][ts:ts + 128 * NA, :]
                  .rearrange("(a p) d -> p a d", p=128), snrmA[:])
        P.step("sync", a6)
    exA.close()
    nfm = sbB("nfm", [128, 8, 512], BF16)
    hid_tok = sbB("hid_tok", [128, 4096], BF16)
    qkf = sbB("qkf", [128, 128], F32)
    qtmp = sbB("qtmp", [128, 128], F32)
    pb = [psB(f"pb{i}", [128, 512], F32) for i in range(4)]
    pq = psB("pq", [128, 128], F32)
    pbc = psB("pbc", [128, 128], F32)

    for sup in range(T // 512):
        ts = sup * 512
        def b1(e, ts=ts):
            for c in range(8):
                P.dmaT(e, nfm[:, c, :],
                       io["nrm_hbm"][ts:ts + 512, c * 128:(c + 1) * 128])
        P.step("sync", b1)
        for tsub in range(4):
            tg = ts + tsub * 128
            for half in range(2):
                def bmm(e, tsub=tsub, half=half):
                    last = None
                    for fb in range(half * 4, half * 4 + 4):
                        p = pb[fb % 4]
                        for c in range(8):
                            e.matmul(p[:], nfm[:, c, tsub * 128:(tsub + 1) * 128],
                                     Whg[:, c, fb * 512:(fb + 1) * 512],
                                     start=(c == 0), stop=False)
                        last = e.matmul(p[:], g.ones_row[:],
                                        beff_h[:, fb * 512:(fb + 1) * 512],
                                        start=False, stop=True)
                    return last
                P.step("tensor", bmm)
                def bev(e, half=half):
                    last = None
                    for fb in range(half * 4, half * 4 + 4):
                        last = e.activation(hid_tok[:, fb * 512:(fb + 1) * 512],
                                            pb[fb % 4][:], AF.Silu)
                    return last
                P.step("scalar", bev)
            def biot(e, tg=tg):
                return e.iota(riota[:], [[1, 128]], base=1 + tg,
                              channel_multiplier=0,
                              allow_small_or_imprecise_dtypes=True)
            P.step("gpsimd", biot)
            P.step("vector", lambda e: e.reciprocal(riota[:], riota[:]))
            def bqk(e, tsub=tsub, tg=tg):
                for c in range(8):
                    e.matmul(pq[:], nfm[:, c, tsub * 128:(tsub + 1) * 128],
                             Wqkg[:, c, :], start=(c == 0), stop=False)
                e.matmul(pq[:], g.ones_row[:], beff_qk[:],
                         start=False, stop=True)
                return e.matmul(pbc[:], g.ones_row[:], riota[:],
                                start=True, stop=True)
            P.step("tensor", bqk)
            def bqe(e):
                return e.activation(qkf[:], pq[:], AF.Silu)
            P.step("scalar", bqe)
            def bqv(e, tg=tg):
                e.tensor_scalar(qtmp[:], qkf[:], oscol[:, 0:1], oscol[:, 2:3],
                                ALU.mult, ALU.add)
                return e.tensor_scalar(kT[:, tg:tg + 128], qkf[:],
                                       oscol[:, 1:2], oscol[:, 3:4],
                                       ALU.mult, ALU.add)
            P.step("vector", bqv)
            def bqv2(e, tg=tg):
                return e.tensor_mul(qhatT[:, tg:tg + 128], qtmp[:], pbc[:])
            P.step("vector", bqv2)
            def bhw(e, tg=tg):
                P.dma(e, io["hid_hbm"][tg:tg + 128, :], hid_tok[:])
            P.step("sync", bhw)

    exB.close()
    attnT = sb("attnT", [128, NT, 512], BF16)
    relu_s = [sb(f"relu{i}", [128, 512], F32) for i in range(2)]
    vslice = sb("vslice", [128, NT, 128], BF16)
    gatef = sb("gatef", [128, 512], BF16)
    gated = sb("gated", [128, 16, 512], BF16)
    Woutb = sb("Woutb", [128, 16, 1024], BF16)
    xres = sb("xres", [128, 4, 1024], F32)
    oacc = sb("oacc", [128, 1024], F32)
    onrm = sb("onrm", [128, 1024], F32)
    statC = sb("statC", [128, 8], F32)
    ps_sim = [ps(f"psim{i}", [128, 512], F32) for i in range(2)]
    ps_o = ps("ps_o", [128, 512], F32)
    ps_w = ps("ps_w", [128, 512], F32)

    def c0(e):
        for c in range(16):
            P.dma(e, Woutb[:, c, :], io["W_out"][c * 128:(c + 1) * 128, :])
    P.step("gpsimd", c0)

    for qb in range(NQB):
        nk = 4 * (qb + 1)
        qs = qb * 512
        for k0 in range(0, nk, 2):
            kk = [k for k in (k0, k0 + 1) if k < nk]
            def csim(e, kk=kk, qs=qs):
                for j, k in enumerate(kk):
                    last = e.matmul(ps_sim[j][:], kT[:, k * 128:(k + 1) * 128],
                                    qhatT[:, qs:qs + 512], start=True, stop=True)
                return last
            P.step("tensor", csim)
            def crel(e, kk=kk):
                last = None
                for j, k in enumerate(kk):
                    last = e.activation(relu_s[j][:], ps_sim[j][:], AF.Relu)
                return last
            P.step("scalar", crel)
            def csq(e, kk=kk):
                last = None
                for j, k in enumerate(kk):
                    last = e.tensor_mul(attnT[:, k, :], relu_s[j][:],
                                        relu_s[j][:])
                return last
            P.step("vector", csq)
        def cdiag(e, qb=qb, nk=nk):
            last = None
            for k in range(nk - 4, nk):
                rel = k * 128 - qb * 512
                last = e.affine_select(attnT[:, k, :], attnT[:, k, :],
                                       [[1, 512]], ALU.is_ge, 0.0,
                                       base=-rel, channel_multiplier=-1)
            return last
        P.step("gpsimd", cdiag)

        for vf in range(16):
            def cvl(e, vf=vf, nk=nk, qs=qs):
                P.dma(e, vslice[:, 0:nk, :],
                      io["hid_hbm"][0:nk * 128, vf * 128:(vf + 1) * 128]
                      .rearrange("(k p) f -> p k f", p=128))
                P.dmaT(e, gatef[:],
                       io["hid_hbm"][qs:qs + 512,
                                     2048 + vf * 128:2048 + (vf + 1) * 128])
            P.step("sync", cvl)
            def cav(e, vf=vf, nk=nk):
                last = None
                for k in range(nk):
                    last = e.matmul(ps_o[:], vslice[:, k, :], attnT[:, k, :],
                                    start=(k == 0), stop=(k == nk - 1))
                return last
            P.step("tensor", cav)
            def cgt(e, vf=vf):
                return e.tensor_mul(gated[:, vf, :], ps_o[:], gatef[:])
            P.step("vector", cgt)

        def cxl(e, qs=qs):
            P.dma(e, xres[:], io["x"][qs:qs + 512, :]
                  .rearrange("(a p) d -> p a d", p=128))
        P.step("sync", cxl)
        for tsub in range(4):
            for dh in range(2):
                def cwm(e, tsub=tsub, dh=dh):
                    for c in range(16):
                        e.matmul(ps_w[:], gated[:, c, tsub * 128:(tsub + 1) * 128],
                                 Woutb[:, c, dh * 512:(dh + 1) * 512],
                                 start=(c == 0), stop=False)
                    return e.matmul(ps_w[:], g.ones_row[:],
                                    bout_row[:, dh * 512:(dh + 1) * 512],
                                    start=False, stop=True)
                P.step("tensor", cwm)
                def cwe(e, tsub=tsub, dh=dh):
                    return e.scalar_tensor_tensor(
                        oacc[:, dh * 512:(dh + 1) * 512], xres[:, tsub, dh * 512:(dh + 1) * 512],
                        2.0, ps_w[:], ALU.mult, ALU.add)
                P.step("vector", cwe)
            emit_postln(P, g, oacc, statC, onrm, D=D, scr=sqscr)
            def cxw(e, tsub=tsub, qs=qs):
                P.dma(e, io["x1"][qs + tsub * 128:qs + (tsub + 1) * 128, :],
                      onrm[:])
            P.step("sync", cxw)


def emit_rms_spill(P, nc, g, io_in, io_out, T, gbc, scale_ln_col, tag):
    from contextlib import ExitStack
    ex = ExitStack()
    D = 1024
    NA = 4
    sx = ex.enter_context(nc.sbuf_tensor(f"rms_sx{tag}", [128, NA, 1024], F32))
    so = ex.enter_context(nc.sbuf_tensor(f"rms_so{tag}", [128, NA, 1024], BF16))
    st = ex.enter_context(nc.sbuf_tensor(f"rms_st{tag}", [128, 16], F32))
    sq = ex.enter_context(nc.sbuf_tensor(f"rms_sq{tag}", [128, NA, 1024], F32))
    for sup in range(T // (128 * NA)):
        ts = sup * 128 * NA
        def r1(e, ts=ts):
            P.dma(e, sx[:], io_in[ts:ts + 128 * NA, :]
                  .rearrange("(a p) d -> p a d", p=128))
        P.step("sync", r1)
        def r2(e):
            last = None
            for a in range(NA):
                last = e.activation(sq[:, a, :], sx[:, a, :], AF.Square,
                                    accum_out=st[:, a:a + 1])
            return last
        P.step("scalar", r2)
        P.step("vector", lambda e: e.tensor_scalar_max(st[:, 0:NA],
                                                       st[:, 0:NA], 1e-24))
        P.step("scalar", lambda e: e.activation(st[:, 8:8 + NA], st[:, 0:NA],
                                                AF.Ln))
        P.step("scalar", lambda e: e.activation(st[:, 8:8 + NA], st[:, 8:8 + NA],
                                                AF.Exp, scale=-0.5,
                                                bias=scale_ln_col[:]))
        def r5(e):
            last = None
            for a in range(NA):
                last = e.tensor_scalar_mul(sq[:, a, :], sx[:, a, :],
                                           st[:, 8 + a:9 + a])
            return last
        P.step("vector", r5)
        def r6(e):
            last = None
            for a in range(NA):
                last = e.tensor_mul(so[:, a, :], sq[:, a, :], gbc[:])
            return last
        P.step("vector", r6)
        def r7(e, ts=ts):
            P.dma(e, io_out[ts:ts + 128 * NA, :]
                  .rearrange("(a p) d -> p a d", p=128), so[:])
        P.step("sync", r7)
    ex.close()


def emit_bcast_row(P, nc, g, dram_vec, out_tile, psb, n):
    row = out_tile
    def s1(e):
        P.dma(e, g.row_scr[:, 0:n], dram_vec.rearrange("(a f) -> a f", a=1))
    P.step("sync", s1)
    for j0 in range(0, n, 512):
        w = min(512, n - j0)
        def s2(e, j0=j0, w=w):
            return e.matmul(psb[:, 0:w], g.ones_row[:], g.row_scr[:, j0:j0 + w],
                            start=True, stop=True)
        P.step("tensor", s2)
        def s3(e, j0=j0, w=w):
            return e.tensor_copy(out_tile[:, j0:j0 + w], psb[:, 0:w])
        P.step("vector", s3)


def emit_l0_ffn(P, nc, ex, g, io, T):
    from contextlib import ExitStack
    D = 1024
    ES = 4096
    NT = T // 128

    sb = lambda n, s, d: ex.enter_context(nc.sbuf_tensor(n, s, d))
    ps = lambda n, s, d: ex.enter_context(nc.psum_tensor(n, s, d))

    ex0 = ExitStack()
    gxbc = sb("gxbc", [128, 1024], F32)
    gsbc = sb("gsbc", [128, 1024], F32)
    ln32 = sb("ln32", [128, 1], F32)
    Mn_all = sb("Mn_all", [128, 64], F32)
    Sn_all = sb("Sn_all", [128, 64], F32)
    MnT = sb("MnT", [64, 128], F32)
    identf = sb("identf", [128, 128], F32)
    ps_pro = ex0.enter_context(nc.psum_tensor("ps_pro", [128, 512], F32))
    P.step("vector", lambda e: e.memset(ln32[:], math.log(32.0)))
    emit_bcast_row(P, nc, g, io["norm_gamma"], gxbc, ps_pro, 1024)
    emit_bcast_row(P, nc, g, io["slot_norm_gamma"], gsbc, ps_pro, 1024)
    def idgen(e):
        return e.iota(identf[:], [[1, 128]], base=0, channel_multiplier=-1,
                      allow_small_or_imprecise_dtypes=True)
    P.step("gpsimd", idgen)
    def idgen2(e):
        return e.tensor_single_scalar(identf[:], identf[:], 0.0, ALU.is_equal)
    P.step("vector", idgen2)

    emit_rms_spill(P, nc, g, io["x"], io["xn_hbm"], T, gxbc, ln32, "x")
    se_flat = io["slot_embeds"].rearrange("e s d -> (e s) d")
    emit_rms_spill(P, nc, g, se_flat, io["sn_hbm"], 4096, gsbc, ln32, "s")
    ex0.close()

    exF = ExitStack()
    sbF = lambda n, s, d: exF.enter_context(nc.sbuf_tensor(n, s, d))
    psF = lambda n, s, d: exF.enter_context(nc.psum_tensor(n, s, d))
    sn_fm = sbF("sn_fm", [128, 8, ES], BF16)
    xnf = sbF("xnf", [128, 8, 512], BF16)
    lgt = sbF("lgt", [128, ES], BF16)
    lgl = sbF("lgl", [128, ES], BF16)
    mcols = sbF("mcols", [128, 8], F32)
    plg = [psF(f"plg{i}", [128, 512], F32) for i in range(2)]
    def f3(e):
        for c in range(8):
            P.dmaT(e, sn_fm[:, c, :], io["sn_hbm"][:, c * 128:(c + 1) * 128])
    P.step("sync", f3)
    for sup in range(T // 512):
        ts = sup * 512
        def f4a(e, ts=ts):
            for c in range(8):
                P.dmaT(e, xnf[:, c, :], io["xn_hbm"][ts:ts + 512,
                                                     c * 128:(c + 1) * 128])
        P.step("sync", f4a)
        for nsub in range(4):
            nt = (ts + nsub * 128) // 128
            for esb in range(8):
                def f4m(e, nsub=nsub, esb=esb):
                    p = plg[esb % 2]
                    last = None
                    for c in range(8):
                        last = e.matmul(p[:], xnf[:, c, nsub * 128:(nsub + 1) * 128],
                                        sn_fm[:, c, esb * 512:(esb + 1) * 512],
                                        start=(c == 0), stop=(c == 7))
                    return last
                P.step("tensor", f4m)
                def f4e(e, esb=esb):
                    e.tensor_copy(lgt[:, esb * 512:(esb + 1) * 512],
                                  plg[esb % 2][:])
                    return e.reduce_max(mcols[:, esb:esb + 1], plg[esb % 2][:],
                                        axis=AX.X)
                P.step("vector", f4e)
                def f4l(e, esb=esb):
                    return e.tensor_sub(lgl[:, esb * 512:(esb + 1) * 512],
                                        plg[esb % 2][:],
                                        lgt[:, esb * 512:(esb + 1) * 512])
                P.step("vector", f4l)
            def f4M(e, nt=nt):
                return e.reduce_max(Mn_all[:, nt:nt + 1], mcols[:], axis=AX.X)
            P.step("vector", f4M)
            def f4w(e, nt=nt):
                P.dma(e, io["lg_hbm"][nt * 128:(nt + 1) * 128, :], lgt[:])
                P.dma(e, io["lglo_hbm"][nt * 128:(nt + 1) * 128, :], lgl[:])
            P.step("sync", f4w)
    exF.close()

    exG = ExitStack()
    sbG = lambda n, s, d: exG.enter_context(nc.sbuf_tensor(n, s, d))
    lgtT = sbG("lgtT", [128, T], BF16)
    lgtTl = sbG("lgtTl", [128, T], BF16)
    lgtTf = sbG("lgtTf", [128, T], F32)
    dspT = sbG("dspT", [128, T], BF16)
    mes = sbG("mes", [128, 4], F32)
    for et in range(32):
        def f5a(e, et=et):
            P.dmaT(e, lgtT[:], io["lg_hbm"][:, et * 128:(et + 1) * 128])
            P.dmaT(e, lgtTl[:], io["lglo_hbm"][:, et * 128:(et + 1) * 128])
        P.step("sync", f5a)
        P.step("vector", lambda e: e.tensor_add(lgtTf[:], lgtT[:], lgtTl[:]))
        P.step("vector", lambda e: e.reduce_max(mes[:, 0:1], lgtTf[:], axis=AX.X))
        P.step("vector", lambda e: e.tensor_scalar_mul(mes[:, 1:2], mes[:, 0:1],
                                                       -1.0))
        def f5e(e):
            return e.activation(dspT[:], lgtTf[:], AF.Exp, bias=mes[:, 1:2],
                                accum_out=mes[:, 2:3])
        P.step("scalar", f5e)
        P.step("vector", lambda e: e.reciprocal(mes[:, 3:4], mes[:, 2:3]))
        P.step("vector", lambda e: e.tensor_scalar_mul(dspT[:], dspT[:],
                                                       mes[:, 3:4]))
        def f5d(e, et=et):
            P.dma(e, io["disp_hbm"][et * 128:(et + 1) * 128, :], dspT[:])
        P.step("sync", f5d)
    exG.close()

    exH = ExitStack()
    sbH = lambda n, s, d: exH.enter_context(nc.sbuf_tensor(n, s, d))
    psH = lambda n, s, d: exH.enter_context(nc.psum_tensor(n, s, d))
    xtk = sbH("xtk", [128, 1024], BF16)
    dtk = sbH("dtk", [128, 512], BF16)
    slo = sbH("slo", [128, 8, 512], BF16)
    psl = [psH(f"psl{i}", [128, 512], F32) for i in range(8)]
    for esb in range(8):
        for n in range(NT):
            def f6a(e, esb=esb, n=n):
                P.dma(e, xtk[:], io["xn_hbm"][n * 128:(n + 1) * 128, :])
                P.dmaT(e, dtk[:], io["disp_hbm"][esb * 512:(esb + 1) * 512,
                                                 n * 128:(n + 1) * 128])
            P.step("sync", f6a)
            def f6m(e, esb=esb, n=n):
                last = None
                for dc in range(8):
                    last = e.matmul(psl[dc][:], xtk[:, dc * 128:(dc + 1) * 128],
                                    dtk[:], start=(n == 0), stop=(n == NT - 1))
                return last
            P.step("tensor", f6m)
        def f6e(e, esb=esb):
            last = None
            for dc in range(8):
                last = e.tensor_copy(slo[:, dc, :], psl[dc][:])
            return last
        P.step("vector", f6e)
        def f6w(e, esb=esb):
            P.dma(e, io["slots_hbm"].rearrange("(dc p) es -> p dc es", p=128)
                  [:, :, esb * 512:(esb + 1) * 512], slo[:])
        P.step("sync", f6w)
    exH.close()

    exI = ExitStack()
    sbI = lambda n, s, d: exI.enter_context(nc.sbuf_tensor(n, s, d))
    psI = lambda n, s, d: exI.enter_context(nc.psum_tensor(n, s, d))
    w1e = sbI("w1e", [128, 8, 4096], BF16)
    slte = sbI("slte", [128, 8, 512], BF16)
    hfm = sbI("hfm", [128, 32, 512], BF16)
    w2c = [sbI(f"w2c{i}", [128, 1024], BF16) for i in range(2)]
    b1c = sbI("b1c", [128, 32], F32)
    b2r = sbI("b2r", [1, 1024], F32)
    eo_all = sbI("eo_all", [128, 32, 1024], BF16)
    for ee in range(8):
        exE = ExitStack()
        ph = [exE.enter_context(nc.psum_tensor(f"ph{ee}_{i}", [128, 512], F32))
              for i in range(2)]
        def f7a(e, ee=ee):
            for dc in range(8):
                P.dma(e, w1e[:, dc, :], io["w1"][ee, dc * 128:(dc + 1) * 128, :])
            P.dma(e, slte[:],
                  io["slots_hbm"].rearrange("(dc p) es -> p dc es", p=128)
                  [:, :, ee * 512:(ee + 1) * 512])
            load_vec_as_cols(P, e, b1c, io["b1"][ee, :], 4096)
        P.step("gpsimd", f7a)
        def f7b(e, ee=ee):
            P.dma(e, b2r[:], io["b2"][ee:ee + 1, :])
        P.step("sync", f7b)
        for hc in range(32):
            def f7m(e, hc=hc):
                p = ph[hc % 2]
                last = None
                for dc in range(8):
                    last = e.matmul(p[:], w1e[:, dc, hc * 128:(hc + 1) * 128],
                                    slte[:, dc, :], start=(dc == 0),
                                    stop=(dc == 7))
                return last
            P.step("tensor", f7m)
            def f7g(e, hc=hc):
                return e.activation(hfm[:, hc, :], ph[hc % 2][:], AF.Gelu,
                                    bias=b1c[:, hc:hc + 1])
            P.step("scalar", f7g)
        for p in ph:
            pass
        exE.close()
        exE = ExitStack()
        peo = [exE.enter_context(nc.psum_tensor(f"peo{ee}_{i}", [128, 512], F32))
               for i in range(8)]
        for hc in range(32):
            def f7w2(e, ee=ee, hc=hc):
                P.dma(e, w2c[hc % 2][:], io["w2"][ee, hc * 128:(hc + 1) * 128, :])
            P.step("gpsimd", f7w2)
            def f7e(e, hc=hc):
                last = None
                for ssub in range(4):
                    for dh in range(2):
                        last = e.matmul(
                            peo[ssub * 2 + dh][:],
                            hfm[:, hc, ssub * 128:(ssub + 1) * 128],
                            w2c[hc % 2][:, dh * 512:(dh + 1) * 512],
                            start=(hc == 0), stop=False)
                return last
            P.step("tensor", f7e)
        def f7bias(e):
            last = None
            for ssub in range(4):
                for dh in range(2):
                    last = e.matmul(peo[ssub * 2 + dh][:], g.ones_row[:],
                                    b2r[:, dh * 512:(dh + 1) * 512],
                                    start=False, stop=True)
            return last
        P.step("tensor", f7bias)
        def f7v(e, ee=ee):
            last = None
            for ssub in range(4):
                for dh in range(2):
                    last = e.tensor_copy(
                        eo_all[:, ee * 4 + ssub, dh * 512:(dh + 1) * 512],
                        peo[ssub * 2 + dh][:])
            return last
        P.step("vector", f7v)
        exE.close()
    exI.close()

    exJ = ExitStack()
    sbJ = lambda n, s, d: exJ.enter_context(nc.sbuf_tensor(n, s, d))
    psJ = lambda n, s, d: exJ.enter_context(nc.psum_tensor(n, s, d))
    lgn = sbJ("lgn", [128, ES], BF16)
    lgnl = sbJ("lgnl", [128, ES], BF16)
    lgnf = sbJ("lgnf", [128, ES], F32)
    cet = sbJ("cet", [128, ES], BF16)
    nmn = sbJ("nmn", [128, 1], F32)
    for n in range(NT):
        def h1(e, n=n):
            P.dma(e, lgn[:], io["lg_hbm"][n * 128:(n + 1) * 128, :])
            P.dma(e, lgnl[:], io["lglo_hbm"][n * 128:(n + 1) * 128, :])
        P.step("sync", h1)
        P.step("vector", lambda e: e.tensor_add(lgnf[:], lgn[:], lgnl[:]))
        def h2(e, n=n):
            return e.tensor_scalar_mul(nmn[:], Mn_all[:, n:n + 1], -1.0)
        P.step("vector", h2)
        def h3(e, n=n):
            return e.activation(cet[:], lgnf[:], AF.Exp, bias=nmn[:],
                                accum_out=Sn_all[:, n:n + 1])
        P.step("scalar", h3)
        def h4(e, n=n):
            P.dma(e, io["ce_hbm"][n * 128:(n + 1) * 128, :], cet[:])
        P.step("sync", h4)
    exJ.close()

    exK = ExitStack()
    sbK = lambda n, s, d: exK.enter_context(nc.sbuf_tensor(n, s, d))
    psK = lambda n, s, d: exK.enter_context(nc.psum_tensor(n, s, d))
    ceT_sup = sbK("ceT_sup", [128, 32, 1024], BF16)
    xre = sbK("xre", [128, 1024], F32)
    oac = sbK("oac", [128, 1024], F32)
    onr = sbK("onr", [128, 1024], F32)
    stJ = sbK("stJ", [128, 8], F32)
    snc = sbK("snc", [128, 1], F32)
    scrK = sbK("scrK", [128, 1024], F32)
    pb8 = [psK(f"pb8{i}", [128, 512], F32) for i in range(2)]
    for nsup in range(ceil_div(NT, 8)):
      ns0 = nsup * 1024
      nw = min(1024, T - ns0)
      def k0(e, ns0=ns0, nw=nw):
        for k in range(32):
            P.dmaT(e, ceT_sup[:, k, 0:nw],
                   io["ce_hbm"][ns0:ns0 + nw, k * 128:(k + 1) * 128])
      P.step("sync", k0)
      for n in range(ns0 // 128, (ns0 + nw) // 128):
        def k1(e, n=n):
            P.dma(e, xre[:], io["x"][n * 128:(n + 1) * 128, :])
        P.step("sync", k1)
        def k2(e, n=n):
            return e.reciprocal(snc[:], Sn_all[:, n:n + 1])
        P.step("vector", k2)
        for dh in range(2):
            def k3(e, n=n, ns0=ns0, dh=dh):
                nn0 = n * 128 - ns0
                last = None
                for k in range(32):
                    last = e.matmul(pb8[dh][:],
                                    ceT_sup[:, k, nn0:nn0 + 128],
                                    eo_all[:, k, dh * 512:(dh + 1) * 512],
                                    start=(k == 0), stop=(k == 31))
                return last
            P.step("tensor", k3)
        def k4(e):
            e.tensor_scalar(oac[:, 0:512], pb8[0][:], snc[:], None, ALU.mult)
            return e.tensor_scalar(oac[:, 512:1024], pb8[1][:], snc[:], None,
                                   ALU.mult)
        P.step("vector", k4)
        P.step("vector", lambda e: e.tensor_add(oac[:], oac[:], xre[:]))
        emit_postln(P, g, oac, stJ, onr, D=D, scr=scrK)
        def k5(e, n=n):
            P.dma(e, io["x2"][n * 128:(n + 1) * 128, :], onr[:])
        P.step("sync", k5)
    exK.close()


def emit_l1_attn(P, nc, ex, g, io, T):
    from contextlib import ExitStack
    D = 1024
    H = 8
    NT = T // 128
    K0 = 8.0
    SC = 1.0 / math.sqrt(128.0)

    sb = lambda n, s, d: ex.enter_context(nc.sbuf_tensor(n, s, d))

    gnw_bc = sb("gnw_bc", [128, 1024], F32)
    g1_fm = sb("g1_fm", [128, T], BF16)
    Wg2T = sb("Wg2T", [128, 1024], BF16)
    identb = sb("identb", [128, 128], BF16)
    ln1 = sb("ln1", [128, 1], F32)

    ex0 = ExitStack()
    ps_pro = ex0.enter_context(nc.psum_tensor("ps_pro1", [128, 512], F32))
    emit_bcast_row(P, nc, g, io["g_norm_weight"], gnw_bc, ps_pro, 1024)
    def pid(e):
        return e.iota(ln1[:], [[1, 1]], base=0, channel_multiplier=0,
                      allow_small_or_imprecise_dtypes=True)
    P.step("gpsimd", pid)
    scp = ex0.enter_context(nc.sbuf_tensor("scp", [128, 512], F32))
    def pidb(e):
        return e.iota(scp[:, 0:128], [[1, 128]], base=0, channel_multiplier=-1,
                      allow_small_or_imprecise_dtypes=True)
    P.step("gpsimd", pidb)
    P.step("vector", lambda e: e.tensor_single_scalar(scp[:, 0:128],
                                                      scp[:, 0:128], 0.0,
                                                      ALU.is_equal))
    P.step("vector", lambda e: e.tensor_copy(identb[:], scp[:, 0:128]))
    def spill(e):
        for r in range(0, 1024, 128):
            P.dma(e, io["wq_bf"][r:r + 128, :], io["Wq"][r:r + 128, :])
            P.dma(e, io["wk_bf"][r:r + 128, :], io["Wk"][r:r + 128, :])
            P.dma(e, io["wv_bf"][r:r + 128, :], io["Wv"][r:r + 128, :])
            P.dma(e, io["wo_bf"][r:r + 128, :], io["Wo"][r:r + 128, :])
        P.dma(e, io["wg1_bf"][:, :], io["Wg1"][:, :])
        for r in range(0, 1024, 128):
            P.dma(e, io["wg2_bf"][r:r + 128, :], io["Wg2"][r:r + 128, :])
        for t0 in range(0, T, 512):
            P.dma(e, io["x_bf"][t0:t0 + 512, :], io["x"][t0:t0 + 512, :])
    P.step("gpsimd", spill)
    ex0.close()

    exB = ExitStack()
    sbB = lambda n, s, d: exB.enter_context(nc.sbuf_tensor(n, s, d))
    psB = lambda n, s, d: exB.enter_context(nc.psum_tensor(n, s, d))
    WqT = sbB("WqT", [128, 8, 1024], BF16)
    WkT = sbB("WkT", [128, 8, 1024], BF16)
    WvT = sbB("WvT", [128, 8, 1024], BF16)
    Wg1T = sbB("Wg1T", [128, 8, 128], BF16)
    xf = sbB("xf1", [128, 8, 512], BF16)
    prj = sbB("prj", [128, 1024], BF16)
    pp = [psB(f"pp1{i}", [128, 512], F32) for i in range(2)]
    def l1w(e):
        for c in range(8):
            for ob in range(2):
                P.dmaT(e, WqT[:, c, ob * 512:(ob + 1) * 512],
                       io["wq_bf"][ob * 512:(ob + 1) * 512, c * 128:(c + 1) * 128])
                P.dmaT(e, WkT[:, c, ob * 512:(ob + 1) * 512],
                       io["wk_bf"][ob * 512:(ob + 1) * 512, c * 128:(c + 1) * 128])
                P.dmaT(e, WvT[:, c, ob * 512:(ob + 1) * 512],
                       io["wv_bf"][ob * 512:(ob + 1) * 512, c * 128:(c + 1) * 128])
            P.dmaT(e, Wg1T[:, c, :], io["wg1_bf"][:, c * 128:(c + 1) * 128])
        for ob in range(2):
            P.dmaT(e, Wg2T[:, ob * 512:(ob + 1) * 512],
                   io["wg2_bf"][ob * 512:(ob + 1) * 512, :])
    P.step("sync", l1w)
    for sup in range(T // 512):
        ts = sup * 512
        def m1(e, ts=ts):
            for c in range(8):
                P.dmaT(e, xf[:, c, :], io["x_bf"][ts:ts + 512,
                                                  c * 128:(c + 1) * 128])
        P.step("sync", m1)
        for hh in range(8):
            def mq(e, hh=hh, ts=ts):
                for c in range(8):
                    e.matmul(pp[0][:], WqT[:, c, hh * 128:(hh + 1) * 128],
                             xf[:, c, :], start=(c == 0), stop=(c == 7))
                last = None
                for c in range(8):
                    last = e.matmul(pp[1][:], WkT[:, c, hh * 128:(hh + 1) * 128],
                                    xf[:, c, :], start=(c == 0), stop=(c == 7))
                return last
            P.step("tensor", mq)
            def mqe(e, hh=hh, ts=ts):
                return e.activation(prj[:, 0:512], pp[0][:], AF.Silu)
            P.step("scalar", mqe)
            def mqv(e, hh=hh, ts=ts):
                return e.tensor_copy(prj[:, 512:1024], pp[1][:])
            P.step("vector", mqv)
            def mqw(e, hh=hh, ts=ts):
                P.dma(e, io["qf_hbm"][hh * 128:(hh + 1) * 128, ts:ts + 512],
                      prj[:, 0:512])
                P.dma(e, io["kf_hbm"][hh * 128:(hh + 1) * 128, ts:ts + 512],
                      prj[:, 512:1024])
            P.step("sync", mqw)
        def mg(e, ts=ts):
            last = None
            for c in range(8):
                last = e.matmul(pp[0][:], Wg1T[:, c, :], xf[:, c, :],
                                start=(c == 0), stop=(c == 7))
            return last
        P.step("tensor", mg)
        def mge(e, ts=ts):
            return e.tensor_copy(g1_fm[:, ts:ts + 512], pp[0][:])
        P.step("vector", mge)
        for tsub in range(4):
            def mv(e, ts=ts, tsub=tsub):
                last = None
                for ob in range(2):
                    for c in range(8):
                        last = e.matmul(pp[ob][:],
                                        xf[:, c, tsub * 128:(tsub + 1) * 128],
                                        WvT[:, c, ob * 512:(ob + 1) * 512],
                                        start=(c == 0), stop=(c == 7))
                return last
            P.step("tensor", mv)
            def mve(e):
                e.tensor_copy(prj[:, 0:512], pp[0][:])
                return e.tensor_copy(prj[:, 512:1024], pp[1][:])
            P.step("vector", mve)
            def mvw(e, ts=ts, tsub=tsub):
                P.dma(e, io["vtok_hbm"][ts + tsub * 128:ts + (tsub + 1) * 128, :],
                      prj[:])
            P.step("sync", mvw)
    exB.close()

    exC = ExitStack()
    sbC = lambda n, s, d: exC.enter_context(nc.sbuf_tensor(n, s, d))
    psC = lambda n, s, d: exC.enter_context(nc.psum_tensor(n, s, d))
    kfh = sbC("kfh", [128, T], BF16)
    kap = sbC("kap", [128, T], BF16)
    csc = sbC("csc", [128, T], F32)
    qhh = sbC("qhh", [128, T], BF16)
    ktk = sbC("ktk", [128, NT, 128], BF16)
    vth = sbC("vth", [128, NT, 128], BF16)
    oth = sbC("oth", [128, NT, 128], BF16)
    zb = sbC("zb", [128, T], BF16)
    Sf = sbC("Sf", [128, 128], F32)
    Sb = sbC("Sb", [128, 128], BF16)
    ssc = sbC("ssc", [128, 128], BF16)
    ps1 = psC("ps1l", [128, 512], F32)
    ps2 = psC("ps2l", [128, 512], F32)
    psb = psC("psbl", [128, 512], BF16)
    P.step("vector", lambda e: e.memset(zb[:], 0.0))
    for hh in range(H):
        def n1(e, hh=hh):
            P.dma(e, kfh[:], io["kf_hbm"][hh * 128:(hh + 1) * 128, :])
            P.dma(e, qhh[:], io["qf_hbm"][hh * 128:(hh + 1) * 128, :])
            P.dma(e, vth[:], io["vtok_hbm"][:, hh * 128:(hh + 1) * 128]
                  .rearrange("(k p) f -> p k f", p=128))
        P.step("sync", n1)
        def n2(e):
            return e.activation(kap[:], kfh[:], AF.Exp, bias=g.negk0_col[:])
        P.step("scalar", n2)
        def n3(e):
            return e.tensor_tensor_scan(csc[:], kap[:], zb[:], 0.0,
                                        ALU.add, ALU.add)
        P.step("vector", n3)
        P.step("vector", lambda e: e.reciprocal(csc[:], csc[:]))
        def n5(e):
            return e.scalar_tensor_tensor(qhh[:], qhh[:], SC, csc[:],
                                          ALU.mult, ALU.mult)
        P.step("vector", n5)
        for c0 in range(0, NT, 4):
            def n6(e, c0=c0):
                last = None
                for j in range(4):
                    if c0 + j < NT:
                        last = e.transpose(psb[:, j * 128:(j + 1) * 128],
                                           kap[:, (c0 + j) * 128:(c0 + j + 1) * 128],
                                           identb[:])
                return last
            P.step("tensor", n6)
            def n7(e, c0=c0):
                last = None
                for j in range(4):
                    if c0 + j < NT:
                        last = e.tensor_copy(ktk[:, c0 + j, :],
                                             psb[:, j * 128:(j + 1) * 128])
                return last
            P.step("vector", n7)
        P.step("vector", lambda e: e.memset(Sf[:], 0.0))
        P.step("vector", lambda e: e.memset(Sb[:], 0.0))
        for ci in range(NT):
            def o1(e, ci=ci):
                return e.matmul(ps1[:, 0:128], kap[:, ci * 128:(ci + 1) * 128],
                                qhh[:, ci * 128:(ci + 1) * 128],
                                start=True, stop=True)
            P.step("tensor", o1)
            def o2(e, ci=ci):
                return e.tensor_copy(ssc[:], ps1[:, 0:128])
            P.step("vector", o2)
            def o3(e, ci=ci):
                return e.affine_select(ssc[:], ssc[:], [[1, 128]], ALU.is_ge,
                                       0.0, base=0, channel_multiplier=-1)
            P.step("gpsimd", o3)
            def o4(e, ci=ci):
                e.matmul(ps2[:, 0:128], ssc[:], vth[:, ci, :],
                         start=True, stop=False)
                e.matmul(ps2[:, 0:128], qhh[:, ci * 128:(ci + 1) * 128], Sb[:],
                         start=False, stop=True)
                return e.matmul(ps1[:, 128:256], ktk[:, ci, :], vth[:, ci, :],
                                start=True, stop=True)
            P.step("tensor", o4)
            def o5(e, ci=ci):
                e.tensor_copy(oth[:, ci, :], ps2[:, 0:128])
                return e.tensor_add(Sf[:], Sf[:], ps1[:, 128:256])
            P.step("vector", o5)
            def o6(e, ci=ci):
                return e.tensor_copy(Sb[:], Sf[:])
            P.step("vector", o6)
        def n8(e, hh=hh):
            P.dma(e, io["o_hbm"][:, hh * 128:(hh + 1) * 128]
                  .rearrange("(k p) f -> p k f", p=128), oth[:])
        P.step("sync", n8)
    exC.close()

    exD = ExitStack()
    sbD = lambda n, s, d: exD.enter_context(nc.sbuf_tensor(n, s, d))
    psD = lambda n, s, d: exD.enter_context(nc.psum_tensor(n, s, d))
    ost = sbD("ost", [128, 4, 1024], BF16)
    ogs = sbD("ogs", [128, 4, 1024], BF16)
    onn = sbD("onn", [128, 1024], F32)
    sgs = sbD("sgs", [128, 1024], F32)
    geff = sbD("geff", [128, 1024], F32)
    stD = sbD("stD", [128, 16], F32)
    sqD = sbD("sqD", [128, 4, 1024], F32)
    pg = [psD(f"pg{i}", [128, 512], F32) for i in range(2)]
    for sup in range(T // 512):
        ts = sup * 512
        def q1(e, ts=ts):
            P.dma(e, ost[:], io["o_hbm"][ts:ts + 512, :]
                  .rearrange("(a p) d -> p a d", p=128))
        P.step("sync", q1)
        def q2(e):
            last = None
            for a in range(4):
                last = e.activation(sqD[:, a, :], ost[:, a, :], AF.Square,
                                    accum_out=stD[:, a:a + 1])
            return last
        P.step("scalar", q2)
        P.step("scalar", lambda e: e.activation(stD[:, 8:12], stD[:, 0:4],
                                                AF.Ln, scale=1.0 / 1024,
                                                bias=g.eps_col[:]))
        P.step("scalar", lambda e: e.activation(stD[:, 8:12], stD[:, 8:12],
                                                AF.Exp, scale=-0.5))
        for a in range(4):
            tg = ts + a * 128
            def q3(e, a=a, tg=tg):
                last = None
                for ob in range(2):
                    last = e.matmul(pg[ob][:], g1_fm[:, tg:tg + 128],
                                    Wg2T[:, ob * 512:(ob + 1) * 512],
                                    start=True, stop=True)
                return last
            P.step("tensor", q3)
            def q4(e, a=a):
                e.activation(sgs[:, 0:512], pg[0][:], AF.Sigmoid)
                return e.activation(sgs[:, 512:1024], pg[1][:], AF.Sigmoid)
            P.step("scalar", q4)
            def q5(e, a=a):
                e.tensor_mul(geff[:, 0:512], pg[0][:], sgs[:, 0:512])
                return e.tensor_mul(geff[:, 512:1024], pg[1][:],
                                    sgs[:, 512:1024])
            P.step("vector", q5)
            def q6(e, a=a):
                return e.tensor_scalar_mul(onn[:], ost[:, a, :],
                                           stD[:, 8 + a:9 + a])
            P.step("vector", q6)
            def q7(e, a=a):
                return e.tensor_mul(onn[:], onn[:], gnw_bc[:])
            P.step("vector", q7)
            def q8(e, a=a):
                return e.tensor_mul(ogs[:, a, :], onn[:], geff[:])
            P.step("vector", q8)
        def q9(e, ts=ts):
            P.dma(e, io["og_hbm"][ts:ts + 512, :]
                  .rearrange("(a p) d -> p a d", p=128), ogs[:])
        P.step("sync", q9)
    exD.close()

    exE = ExitStack()
    sbE = lambda n, s, d: exE.enter_context(nc.sbuf_tensor(n, s, d))
    psE = lambda n, s, d: exE.enter_context(nc.psum_tensor(n, s, d))
    WoT = sbE("WoT", [128, 8, 1024], BF16)
    ogf = sbE("ogf", [128, 8, 512], BF16)
    xrE = sbE("xrE", [128, 1024], F32)
    oaE = sbE("oaE", [128, 1024], F32)
    onE = sbE("onE", [128, 1024], F32)
    stE = sbE("stE", [128, 8], F32)
    sqE = sbE("sqE", [128, 1024], F32)
    po = [psE(f"po{i}", [128, 512], F32) for i in range(2)]
    def r0(e):
        for c in range(8):
            for ob in range(2):
                P.dmaT(e, WoT[:, c, ob * 512:(ob + 1) * 512],
                       io["wo_bf"][ob * 512:(ob + 1) * 512,
                                   c * 128:(c + 1) * 128])
    P.step("sync", r0)
    for sup in range(T // 512):
        ts = sup * 512
        def r1(e, ts=ts):
            for c in range(8):
                P.dmaT(e, ogf[:, c, :], io["og_hbm"][ts:ts + 512,
                                                     c * 128:(c + 1) * 128])
        P.step("sync", r1)
        for tsub in range(4):
            tg = ts + tsub * 128
            def r2(e, tg=tg):
                P.dma(e, xrE[:], io["x"][tg:tg + 128, :])
            P.step("sync", r2)
            def r3(e, tsub=tsub):
                last = None
                for ob in range(2):
                    for c in range(8):
                        last = e.matmul(po[ob][:],
                                        ogf[:, c, tsub * 128:(tsub + 1) * 128],
                                        WoT[:, c, ob * 512:(ob + 1) * 512],
                                        start=(c == 0), stop=(c == 7))
                return last
            P.step("tensor", r3)
            def r4(e):
                e.tensor_add(oaE[:, 0:512], po[0][:], xrE[:, 0:512])
                return e.tensor_add(oaE[:, 512:1024], po[1][:],
                                    xrE[:, 512:1024])
            P.step("vector", r4)
            emit_postln(P, g, oaE, stE, onE, D=D, scr=sqE)
            def r5(e, tg=tg):
                P.dma(e, io["x3"][tg:tg + 128, :], onE[:])
            P.step("sync", r5)
    exE.close()


def emit_l1_ffn(P, nc, ex, g, io, T):
    from contextlib import ExitStack
    D = 1024
    H = 8
    NT = T // 128
    SC = 1.0 / math.sqrt(128.0)

    sb = lambda n, s, d: ex.enter_context(nc.sbuf_tensor(n, s, d))

    gv_all = sb("gv_all", [128, NT, 64], F32)
    go_all = sb("go_all", [128, NT, 64], BF16)
    rs_col = sb("rs_col", [128, 1], F32)

    ex0 = ExitStack()
    ps_pro = ex0.enter_context(nc.psum_tensor("ps_pro2", [128, 512], F32))
    emit_bcast_row(P, nc, g, io["route_scale"], rs_col, ps_pro, 1)
    def spill(e):
        for r in range(0, 1024, 128):
            P.dma(e, io["qw_bf"][r:r + 128, :], io["q_w"][r:r + 128, :])
            P.dma(e, io["kw_bf"][r:r + 128, :], io["k_w"][r:r + 128, :])
        P.dma(e, io["selv_bf"][:, :], io["sel_v_w"][:, :])
        P.dma(e, io["selo_bf"][:, :], io["sel_o_w"][:, :])
        for t0 in range(0, T, 512):
            P.dma(e, io["x_bf"][t0:t0 + 512, :], io["x"][t0:t0 + 512, :])
        for he in range(64):
            P.dma(e, io["ow_bf"][he, :, :], io["o_w"][he, :, :])
    P.step("gpsimd", spill)
    ex0.close()

    exB = ExitStack()
    sbB = lambda n, s, d: exB.enter_context(nc.sbuf_tensor(n, s, d))
    psB = lambda n, s, d: exB.enter_context(nc.psum_tensor(n, s, d))
    QwT = sbB("QwT", [128, 8, 1024], BF16)
    KwT = sbB("KwT", [128, 8, 1024], BF16)
    SvT = sbB("SvT", [128, 8, 64], BF16)
    SoT = sbB("SoT", [128, 8, 64], BF16)
    vwh = sbB("vwh", [128, 8, 1024], BF16)
    xf = sbB("xf2", [128, 8, 512], BF16)
    prj = sbB("prj2", [128, 1024], BF16)
    probs = sbB("probs2", [128, 64], F32)
    gsc = sbB("gsc", [128, 64], F32)
    m12 = sbB("m12", [128, 32], F32)
    wvt = sbB("wvt", [128, 1024], F32)
    vtt = sbB("vtt", [128, 1024], BF16)
    vtf = sbB("vtf", [128, 128], F32)
    pp = [psB(f"pp2{i}", [128, 512], F32) for i in range(2)]
    psl = [psB(f"psl2{i}", [128, 64], F32) for i in range(2)]
    def s1w(e):
        for c in range(8):
            for ob in range(2):
                P.dmaT(e, QwT[:, c, ob * 512:(ob + 1) * 512],
                       io["qw_bf"][ob * 512:(ob + 1) * 512, c * 128:(c + 1) * 128])
                P.dmaT(e, KwT[:, c, ob * 512:(ob + 1) * 512],
                       io["kw_bf"][ob * 512:(ob + 1) * 512, c * 128:(c + 1) * 128])
            P.dmaT(e, SvT[:, c, :], io["selv_bf"][:, c * 128:(c + 1) * 128])
            P.dmaT(e, SoT[:, c, :], io["selo_bf"][:, c * 128:(c + 1) * 128])
    P.step("sync", s1w)
    for sup in range(T // 512):
        ts = sup * 512
        def s2a(e, ts=ts):
            for c in range(8):
                P.dmaT(e, xf[:, c, :], io["x_bf"][ts:ts + 512,
                                                  c * 128:(c + 1) * 128])
        P.step("sync", s2a)
        for hh in range(8):
            def s2q(e, hh=hh):
                for c in range(8):
                    e.matmul(pp[0][:], QwT[:, c, hh * 128:(hh + 1) * 128],
                             xf[:, c, :], start=(c == 0), stop=(c == 7))
                last = None
                for c in range(8):
                    last = e.matmul(pp[1][:], KwT[:, c, hh * 128:(hh + 1) * 128],
                                    xf[:, c, :], start=(c == 0), stop=(c == 7))
                return last
            P.step("tensor", s2q)
            def s2e(e, hh=hh):
                e.tensor_scalar_mul(prj[:, 0:512], pp[0][:], SC)
                return e.tensor_copy(prj[:, 512:1024], pp[1][:])
            P.step("vector", s2e)
            def s2w(e, hh=hh, ts=ts):
                P.dma(e, io["qf2_hbm"][hh * 128:(hh + 1) * 128, ts:ts + 512],
                      prj[:, 0:512])
                P.dma(e, io["kf2_hbm"][hh * 128:(hh + 1) * 128, ts:ts + 512],
                      prj[:, 512:1024])
            P.step("sync", s2w)
        for tsub in range(4):
            nt = (ts + tsub * 128) // 128
            def s3m(e, tsub=tsub):
                for c in range(8):
                    e.matmul(psl[0][:], xf[:, c, tsub * 128:(tsub + 1) * 128],
                             SvT[:, c, :], start=(c == 0), stop=(c == 7))
                last = None
                for c in range(8):
                    last = e.matmul(psl[1][:], xf[:, c, tsub * 128:(tsub + 1) * 128],
                                    SoT[:, c, :], start=(c == 0), stop=(c == 7))
                return last
            P.step("tensor", s3m)
            for which, psi in (("v", 0), ("o", 1)):
                gall = gv_all if which == "v" else go_all
                def s4a(e, psi=psi):
                    return e.activation(probs[:], psl[psi][:], AF.Sigmoid)
                P.step("scalar", s4a)
                def s4b(e):
                    p3 = probs[:].rearrange("q (h e2) -> q h e2", e2=8)
                    return e.tensor_reduce(m12[:, 0:8], p3, AX.X, ALU.max)
                P.step("vector", s4b)
                def s4c(e):
                    m1b = m12[:, 0:8].rearrange("q (h o) -> q h o", o=1) \
                        .broadcast_to([128, 8, 8])
                    p3 = probs[:].rearrange("q (h e2) -> q h e2", e2=8)
                    return e.tensor_tensor(gsc[:].rearrange("q (h e2) -> q h e2", e2=8),
                                           p3, m1b, ALU.is_ge)
                P.step("vector", s4c)
                def s4d(e):
                    return e.scalar_tensor_tensor(gsc[:], gsc[:], -1e30,
                                                  probs[:], ALU.mult, ALU.add)
                P.step("vector", s4d)
                def s4e(e):
                    g3 = gsc[:].rearrange("q (h e2) -> q h e2", e2=8)
                    return e.tensor_reduce(m12[:, 8:16], g3, AX.X, ALU.max)
                P.step("vector", s4e)
                def s4f(e):
                    m2b = m12[:, 8:16].rearrange("q (h o) -> q h o", o=1) \
                        .broadcast_to([128, 8, 8])
                    p3 = probs[:].rearrange("q (h e2) -> q h e2", e2=8)
                    return e.tensor_tensor(gsc[:].rearrange("q (h e2) -> q h e2", e2=8),
                                           p3, m2b, ALU.is_ge)
                P.step("vector", s4f)
                def s4g(e):
                    return e.tensor_add(m12[:, 16:24], m12[:, 0:8],
                                        m12[:, 8:16])
                P.step("vector", s4g)
                P.step("vector", lambda e: e.tensor_scalar_max(
                    m12[:, 16:24], m12[:, 16:24], 1e-9))
                P.step("vector", lambda e: e.reciprocal(m12[:, 24:32],
                                                        m12[:, 16:24]))
                def s4h(e):
                    return e.tensor_scalar_mul(m12[:, 24:32], m12[:, 24:32],
                                               rs_col[:])
                P.step("vector", s4h)
                P.step("vector", lambda e: e.tensor_mul(gsc[:], gsc[:],
                                                        probs[:]))
                def s4i(e, nt=nt, gall=gall):
                    fb = m12[:, 24:32].rearrange("q (h o) -> q h o", o=1) \
                        .broadcast_to([128, 8, 8])
                    return e.tensor_tensor(
                        gall[:, nt, :].rearrange("q (h e2) -> q h e2", e2=8),
                        gsc[:].rearrange("q (h e2) -> q h e2", e2=8), fb,
                        ALU.mult)
                P.step("vector", s4i)
        for hh in range(8):
            def s5a(e, hh=hh):
                for dc in range(8):
                    for ee in range(8):
                        P.dma(e, vwh[:, dc, ee * 128:(ee + 1) * 128],
                              io["v_w"][hh * 8 + ee, dc * 128:(dc + 1) * 128, :])
            P.step("gpsimd", s5a)
            for tsub in range(4):
                nt = (ts + tsub * 128) // 128
                def s5m(e, hh=hh, tsub=tsub):
                    last = None
                    for ob in range(2):
                        for c in range(8):
                            last = e.matmul(pp[ob][:],
                                            xf[:, c, tsub * 128:(tsub + 1) * 128],
                                            vwh[:, c, ob * 512:(ob + 1) * 512],
                                            start=(c == 0), stop=(c == 7))
                    return last
                P.step("tensor", s5m)
                def s5g(e, hh=hh, nt=nt):
                    gb = gv_all[:, nt, hh * 8:(hh + 1) * 8] \
                        .rearrange("q (e2 o) -> q e2 o", o=1) \
                        .broadcast_to([128, 8, 128])
                    e.tensor_tensor(wvt[:, 0:512].rearrange("q (e2 p2) -> q e2 p2", p2=128),
                                    pp[0][:].rearrange("q (e2 p2) -> q e2 p2", p2=128),
                                    gb[:, 0:4, :], ALU.mult)
                    return e.tensor_tensor(
                        wvt[:, 512:1024].rearrange("q (e2 p2) -> q e2 p2", p2=128),
                        pp[1][:].rearrange("q (e2 p2) -> q e2 p2", p2=128),
                        gb[:, 4:8, :], ALU.mult)
                P.step("vector", s5g)
                def s5r(e):
                    wv3 = wvt[:].rearrange("q (e2 p2) -> q p2 e2", p2=128)
                    return e.tensor_reduce(vtf[:], wv3, AX.X, ALU.add)
                P.step("vector", s5r)
                P.step("vector", lambda e: e.tensor_copy(vtt[:, 0:128], vtf[:]))
                def s5w(e, hh=hh, ts=ts, tsub=tsub):
                    P.dma(e, io["vtok2_hbm"][ts + tsub * 128:ts + (tsub + 1) * 128,
                                             hh * 128:(hh + 1) * 128],
                          vtt[:, 0:128])
                P.step("sync", s5w)
    exB.close()

    exC = ExitStack()
    sbC = lambda n, s, d: exC.enter_context(nc.sbuf_tensor(n, s, d))
    psC = lambda n, s, d: exC.enter_context(nc.psum_tensor(n, s, d))
    kfh = sbC("kfh2", [128, T], BF16)
    qfh = sbC("qfh2", [128, T], BF16)
    vth = sbC("vth2", [128, NT, 128], BF16)
    expT = sbC("expT2", [128, NT, 512], BF16)
    srow = sbC("srow2", [1, 512], F32)
    ctxs = sbC("ctxs", [128, 512], F32)
    ctxb = sbC("ctxb", [128, 512], BF16)
    psc = [psC(f"psc2{i}", [128, 512], F32) for i in range(4)]
    pctx = psC("pctx2", [128, 512], F32)
    psum_s = psC("psum_s2", [1, 512], F32)
    pbc2 = psC("pbc2", [128, 512], F32)
    for hh in range(H):
        def t1(e, hh=hh):
            P.dma(e, kfh[:], io["kf2_hbm"][hh * 128:(hh + 1) * 128, :])
            P.dma(e, qfh[:], io["qf2_hbm"][hh * 128:(hh + 1) * 128, :])
            P.dma(e, vth[:], io["vtok2_hbm"][:, hh * 128:(hh + 1) * 128]
                  .rearrange("(k p) f -> p k f", p=128))
        P.step("sync", t1)
        for qb in range(T // 512):
            qs = qb * 512
            for k0 in range(0, NT, 4):
                kk = [k for k in range(k0, min(k0 + 4, NT))]
                def t2(e, kk=kk, qs=qs):
                    last = None
                    for j, k in enumerate(kk):
                        last = e.matmul(psc[j][:], kfh[:, k * 128:(k + 1) * 128],
                                        qfh[:, qs:qs + 512],
                                        start=True, stop=True)
                    return last
                P.step("tensor", t2)
                def t3(e, kk=kk):
                    last = None
                    for j, k in enumerate(kk):
                        last = e.activation(expT[:, k, :], psc[j][:], AF.Exp)
                    return last
                P.step("scalar", t3)
            def t4(e, qs=qs):
                for k in range(NT):
                    e.matmul(pctx[:], vth[:, k, :], expT[:, k, :],
                             start=(k == 0), stop=(k == NT - 1))
                last = None
                for k in range(NT):
                    last = e.matmul(psum_s[:], g.ones_colb[:], expT[:, k, :],
                                    start=(k == 0), stop=(k == NT - 1))
                return last
            P.step("tensor", t4)
            P.step("vector", lambda e: e.reciprocal(srow[:], psum_s[:]))
            def t5(e):
                return e.matmul(pbc2[:], g.ones_row[:], srow[:],
                                start=True, stop=True)
            P.step("tensor", t5)
            def t6(e):
                return e.tensor_copy(ctxs[:], pctx[:])
            P.step("vector", t6)
            def t7(e):
                return e.tensor_mul(ctxb[:], ctxs[:], pbc2[:])
            P.step("vector", t7)
            def t8(e, hh=hh, qs=qs):
                P.dma(e, io["ctx_hbm"][hh * 128:(hh + 1) * 128, qs:qs + 512],
                      ctxb[:])
            P.step("sync", t8)
    exC.close()

    exD = ExitStack()
    sbD = lambda n, s, d: exD.enter_context(nc.sbuf_tensor(n, s, d))
    psD = lambda n, s, d: exD.enter_context(nc.psum_tensor(n, s, d))
    goT = sbD("goT", [64, T], BF16)
    Estr = sbD("Estr", [64, 8192], BF16)
    ctxsup = sbD("ctxsup", [128, 8, 512], BF16)
    wctx = sbD("wctx", [128, 64, 512], BF16)
    owt = [sbD(f"owt{i}", [128, 1024], BF16) for i in range(2)]
    xrD = sbD("xrD", [128, 1024], F32)
    oaD = sbD("oaD", [128, 1024], F32)
    onD = sbD("onD", [128, 1024], F32)
    stD2 = sbD("stD2", [128, 8], F32)
    sqD2 = sbD("sqD2", [128, 1024], F32)
    pgT = psD("pgT", [64, 128], BF16)
    pwc = psD("pwc", [128, 512], F32)
    pO = [psD(f"pO{i}", [128, 512], F32) for i in range(4)]
    def u0(e):
        return e.iota(Estr[:], [[-1, 64], [0, 128]], base=0,
                      channel_multiplier=1,
                      allow_small_or_imprecise_dtypes=True)
    P.step("gpsimd", u0)
    P.step("vector", lambda e: e.tensor_single_scalar(Estr[:], Estr[:], 0.0,
                                                      ALU.is_equal))
    for n in range(NT):
        def u1(e, n=n):
            return e.transpose(pgT[:], go_all[:, n, :], g.identb2[:])
        P.step("tensor", u1)
        def u2(e, n=n):
            return e.tensor_copy(goT[:, n * 128:(n + 1) * 128], pgT[:])
        P.step("vector", u2)
    for tb in range(T // 512):
        ts = tb * 512
        def u3(e, ts=ts):
            for hh2 in range(8):
                P.dma(e, ctxsup[:, hh2, :],
                      io["ctx_hbm"][hh2 * 128:(hh2 + 1) * 128, ts:ts + 512])
        P.step("sync", u3)
        for he in range(64):
            def u5(e, he=he, ts=ts):
                return e.matmul(pwc[:], Estr[:, he * 128:he * 128 + 128],
                                goT[:, ts:ts + 512], start=True, stop=True)
            P.step("tensor", u5)
            def u6(e, he=he):
                return e.tensor_mul(wctx[:, he, :], ctxsup[:, he // 8, :],
                                    pwc[:])
            P.step("vector", u6)
        for tsub in range(4):
            tg = ts + tsub * 128
            def u7(e, tg=tg):
                P.dma(e, xrD[:], io["x"][tg:tg + 128, :])
            P.step("sync", u7)
            for he in range(64):
                def u8(e, he=he):
                    P.dma(e, owt[he % 2][:], io["ow_bf"][he, :, :])
                P.step("gpsimd", u8)
                def u9(e, he=he, tsub=tsub):
                    e.matmul(pO[0][:], wctx[:, he, tsub * 128:(tsub + 1) * 128],
                             owt[he % 2][:, 0:512], start=(he == 0),
                             stop=(he == 63))
                    return e.matmul(pO[1][:],
                                    wctx[:, he, tsub * 128:(tsub + 1) * 128],
                                    owt[he % 2][:, 512:1024], start=(he == 0),
                                    stop=(he == 63))
                P.step("tensor", u9)
            def u10(e):
                e.tensor_add(oaD[:, 0:512], pO[0][:], xrD[:, 0:512])
                return e.tensor_add(oaD[:, 512:1024], pO[1][:],
                                    xrD[:, 512:1024])
            P.step("vector", u10)
            emit_postln(P, g, oaD, stD2, onD, D=D, scr=sqD2)
            def u11(e, tg=tg):
                P.dma(e, io["x4"][tg:tg + 128, :], onD[:])
            P.step("sync", u11)
    exD.close()


from contextlib import ExitStack as _ExitStack

_T, _D = 8192, 1024


def _decl(nc, names, T=_T):
    io = {}
    io["x"] = nc.dram_tensor("x", [T, 1024], F32, kind="ExternalInput").ap()
    for nm, shp in names["in"]:
        io[nm] = nc.dram_tensor(nm, shp, F32, kind="ExternalInput").ap()
    io[names["out"]] = nc.dram_tensor(names["out"], [T, 1024], F32,
                                      kind="ExternalOutput").ap()
    for nm, shp in names["scratch"]:
        io[nm] = nc.dram_tensor(nm, shp, BF16, kind="Internal").ap()
    return io


_L0A_NAMES = {
    "in": [("ln_g", [1024]), ("ln_b", [1024]), ("W_hidden", [1024, 4096]),
           ("b_hidden", [4096]), ("W_qk", [1024, 128]), ("b_qk", [128]),
           ("os_gamma", [2, 128]), ("os_beta", [2, 128]),
           ("W_out", [2048, 1024]), ("b_out", [1024])],
    "out": "x1",
    "scratch": [("nrm_hbm", [_T, 1024]), ("hid_hbm", [_T, 4096])],
}
_L0F_NAMES = {
    "in": [("norm_gamma", [1024]), ("slot_norm_gamma", [1024]),
           ("slot_embeds", [8, 512, 1024]), ("w1", [8, 1024, 4096]),
           ("b1", [8, 4096]), ("w2", [8, 4096, 1024]), ("b2", [8, 1024])],
    "out": "x2",
    "scratch": [("xn_hbm", [_T, 1024]), ("sn_hbm", [4096, 1024]),
                ("lg_hbm", [_T, 4096]), ("lglo_hbm", [_T, 4096]),
                ("ce_hbm", [_T, 4096]), ("disp_hbm", [4096, _T]),
                ("slots_hbm", [1024, 4096])],
}
_L1A_NAMES = {
    "in": [("Wq", [1024, 1024]), ("Wk", [1024, 1024]), ("Wv", [1024, 1024]),
           ("Wg1", [128, 1024]), ("Wg2", [1024, 128]),
           ("g_norm_weight", [1024]), ("Wo", [1024, 1024])],
    "out": "x3",
    "scratch": [("x_bf", [_T, 1024]), ("wq_bf", [1024, 1024]),
                ("wk_bf", [1024, 1024]), ("wv_bf", [1024, 1024]),
                ("wo_bf", [1024, 1024]), ("wg1_bf", [128, 1024]),
                ("wg2_bf", [1024, 128]), ("qf_hbm", [1024, _T]),
                ("kf_hbm", [1024, _T]), ("vtok_hbm", [_T, 1024]),
                ("o_hbm", [_T, 1024]), ("og_hbm", [_T, 1024])],
}
_L1F_NAMES = {
    "in": [("q_w", [1024, 1024]), ("k_w", [1024, 1024]),
           ("v_w", [64, 1024, 128]), ("o_w", [64, 128, 1024]),
           ("sel_v_w", [64, 1024]), ("sel_o_w", [64, 1024]),
           ("route_scale", [1])],
    "out": "x4",
    "scratch": [("x_bf", [_T, 1024]), ("qw_bf", [1024, 1024]),
                ("kw_bf", [1024, 1024]), ("selv_bf", [64, 1024]),
                ("selo_bf", [64, 1024]), ("ow_bf", [64, 128, 1024]),
                ("qf2_hbm", [1024, _T]), ("kf2_hbm", [1024, _T]),
                ("vtok2_hbm", [_T, 1024]), ("ctx_hbm", [1024, _T])],
}


def _build(names, emit):
    nc = bass.Bass("TRN2", target_bir_lowering=False, debug=False)
    io = _decl(nc, names)
    with _ExitStack() as ex:
        sems = {}
        for s in ["ch", "cc"] + ["dm_" + e for e in ENGINES]:
            sems[s] = ex.enter_context(nc.semaphore(s))
        blk = ex.enter_context(nc.Block())
        P = Prog(nc)
        g = alloc_globals(P, nc, ex)
        g.row_scr = ex.enter_context(nc.sbuf_tensor("row_scr", [1, 1024], F32))
        emit(P, nc, ex, g, io, _T)
        P.emit(blk, sems)
    return nc


_CACHE = {}

def _get_nc(key):
    if key not in _CACHE:
        if key == "l0a":
            _CACHE[key] = _build(_L0A_NAMES, emit_l0_attn)
        elif key == "l0f":
            _CACHE[key] = _build(_L0F_NAMES, emit_l0_ffn)
        elif key == "l1a":
            _CACHE[key] = _build(_L1A_NAMES, emit_l1_attn)
        elif key == "l1f":
            _CACHE[key] = _build(_L1F_NAMES, emit_l1_ffn)
    return _CACHE[key]


def _launch(key, names, xs, winputs):
    nc = _get_nc(key)
    in_maps = []
    for c in range(8):
        m = {"x": np.ascontiguousarray(xs[c % 4])}
        for nm, _ in names["in"]:
            m[nm] = winputs[nm]
        in_maps.append(m)
    res = run_bass_kernel_spmd(nc, in_maps, core_ids=list(range(8)))
    return [res.results[c][names["out"]] for c in range(4)]


def kernel(**inputs):
    xs = [np.ascontiguousarray(inputs["L0_attn_x"][b]) for b in range(4)]
    w = lambda pre, names: {nm: np.ascontiguousarray(inputs[pre + nm])
                            for nm, _ in names["in"]}
    xs = _launch("l0a", _L0A_NAMES, xs, w("L0_attn_", _L0A_NAMES))
    xs = _launch("l0f", _L0F_NAMES, xs, w("L0_ffn_", _L0F_NAMES))
    xs = _launch("l1a", _L1A_NAMES, xs, w("L1_attn_", _L1A_NAMES))
    xs = _launch("l1f", _L1F_NAMES, xs, w("L1_ffn_", _L1F_NAMES))
    return np.stack(xs, axis=0).astype(np.float32)



def _decl_block(nc, names, pre, io_x_ap, out_kind_final):
    io = {"x": io_x_ap}
    for nm, shp in names["in"]:
        io[nm] = nc.dram_tensor(pre + nm, shp, F32, kind="ExternalInput").ap()
    if out_kind_final:
        io[names["out"]] = nc.dram_tensor("out", [_T, 1024], F32,
                                          kind="ExternalOutput").ap()
    else:
        io[names["out"]] = nc.dram_tensor(pre + names["out"], [_T, 1024], F32,
                                          kind="Internal").ap()
    for nm, shp in names["scratch"]:
        io[nm] = nc.dram_tensor(pre + nm, shp, BF16, kind="Internal").ap()
    return io


def _build_fused():
    nc = bass.Bass("TRN2", target_bir_lowering=False, debug=False)
    x0 = nc.dram_tensor("x", [_T, 1024], F32, kind="ExternalInput").ap()
    io_a = _decl_block(nc, _L0A_NAMES, "A_", x0, False)
    io_b = _decl_block(nc, _L0F_NAMES, "B_", io_a["x1"], False)
    io_c = _decl_block(nc, _L1A_NAMES, "C_", io_b["x2"], False)
    io_d = _decl_block(nc, _L1F_NAMES, "D_", io_c["x3"], True)
    with _ExitStack() as ex:
        sems = {}
        for s in ["ch", "cc"] + ["dm_" + e for e in ENGINES]:
            sems[s] = ex.enter_context(nc.semaphore(s))
        blk = ex.enter_context(nc.Block())
        P = Prog(nc)
        g = alloc_globals(P, nc, ex)
        g.row_scr = ex.enter_context(nc.sbuf_tensor("row_scr", [1, 1024], F32))
        for emit, io in ((emit_l0_attn, io_a), (emit_l0_ffn, io_b),
                         (emit_l1_attn, io_c), (emit_l1_ffn, io_d)):
            exb = _ExitStack()
            emit(P, nc, exb, g, io, _T)
            exb.close()
        P.emit(blk, sems)
    return nc


def kernel_fused(**inputs):
    if "fused" not in _CACHE:
        _CACHE["fused"] = _build_fused()
    nc = _CACHE["fused"]
    in_maps = []
    for c in range(8):
        m = {"x": np.ascontiguousarray(inputs["L0_attn_x"][c % 4])}
        for pre, names in (("A_", _L0A_NAMES), ("B_", _L0F_NAMES),
                           ("C_", _L1A_NAMES), ("D_", _L1F_NAMES)):
            kpre = {"A_": "L0_attn_", "B_": "L0_ffn_", "C_": "L1_attn_",
                    "D_": "L1_ffn_"}[pre]
            for nm, _ in names["in"]:
                m[pre + nm] = np.ascontiguousarray(inputs[kpre + nm])
        in_maps.append(m)
    res = run_bass_kernel_spmd(nc, in_maps, core_ids=list(range(8)))
    return np.stack([res.results[c]["out"] for c in range(4)], axis=0) \
        .astype(np.float32)


kernel = kernel_fused


# version 04 (final)
# speedup vs baseline: 1.7269x; 1.7269x over previous
import math
import numpy as np
import ml_dtypes
import concourse.bass as bass
import concourse.mybir as mybir
from concourse.bass_utils import run_bass_kernel_spmd

F32 = mybir.dt.float32
BF16 = mybir.dt.bfloat16
AF = mybir.ActivationFunctionType
ALU = mybir.AluOpType
AX = mybir.AxisListType

ENGINES = ("sync", "gpsimd", "tensor", "vector", "scalar")


class Prog:

    def __init__(self, nc):
        self.nc = nc
        self.steps = []
        self.meta = []
        self.eng_counts = {e: 0 for e in ENGINES}
        self.dmcnt = {e: 0 for e in ENGINES}
        self.cccnt = 0
        self.cur_eng = None
        self.sems = {}

    def step(self, eng, fn, deps=None):
        sid = len(self.steps)
        if deps is None:
            deps = [sid - 1] if sid > 0 else []
        else:
            deps = [d for d in deps if d is not None]
        self.steps.append((eng, fn, deps))
        self.meta.append((eng, self.eng_counts[eng]))
        self.eng_counts[eng] += 1
        return sid

    def dma(self, e, out, in_, **kw):
        n = self.cur_eng
        self.dmcnt[n] += 1
        return e.dma_start(out=out, in_=in_, **kw).then_inc(self.sems["dm_" + n], 16)

    def dmaT(self, e, out, in_):
        n = self.cur_eng
        self.dmcnt[n] += 1
        return e.dma_start_transpose(out, in_).then_inc(self.sems["dm_" + n], 16)

    def cc(self, e, kind, op, groups, ins, outs):
        self.cccnt += 1
        return e.collective_compute(kind, op, groups,
                                    [a.opt() for a in ins],
                                    [a.opt() for a in outs]).then_inc(self.sems["cc"])

    def emit(self, blk, sems):
        self.sems = sems
        def make_section(name):
            def sec(eng):
                self.cur_eng = name
                my_sem = sems["st_" + name]
                for sid, (ename, fn, deps) in enumerate(self.steps):
                    if ename != name:
                        continue
                    for d in deps:
                        deng, didx = self.meta[d]
                        eng.wait_ge(sems["st_" + deng], didx + 1)
                    dm_before = self.dmcnt[name]
                    cc_before = self.cccnt
                    last = fn(eng)
                    need_tail = False
                    if self.dmcnt[name] != dm_before:
                        eng.wait_ge(sems["dm_" + name], 16 * self.dmcnt[name])
                        need_tail = True
                    if self.cccnt != cc_before:
                        eng.wait_ge(sems["cc"], self.cccnt)
                        need_tail = True
                    if need_tail:
                        eng.nop().then_inc(my_sem, 1)
                    else:
                        assert last is not None, \
                            f"step {sid} ({name}) returned None with no dma/cc"
                        last.then_inc(my_sem, 1)
            return sec
        blk.sync(make_section("sync"))
        blk.gpsimd(make_section("gpsimd"))
        blk.tensor(make_section("tensor"))
        blk.vector(make_section("vector"))
        blk.scalar(make_section("scalar"))
        return len(self.steps)


def ceil_div(a, b):
    return (a + b - 1) // b



def load_vec_as_cols(P, sync_eng, sb_cols, dram_vec, n):
    nch = n // 128
    P.dma(sync_eng, sb_cols[:, 0:nch],
          dram_vec.rearrange("(c p) -> p c", p=128),
          allow_slow_non_contiguous=True)


class G:
    pass


def alloc_globals(P, nc, ex):
    g = G()
    g.ones_row = ex.enter_context(nc.sbuf_tensor("g_ones", [1, 128], F32))
    g.eps_col = ex.enter_context(nc.sbuf_tensor("g_eps", [128, 1], F32))
    g.negk0_col = ex.enter_context(nc.sbuf_tensor("g_negk0", [128, 1], F32))
    g.ones_colb = ex.enter_context(nc.sbuf_tensor("g_onescb", [128, 1], BF16))
    g.identb2 = ex.enter_context(nc.sbuf_tensor("g_identb2", [128, 128], BF16))
    def init(e):
        e.memset(g.ones_row[:], 1.0)
        e.memset(g.negk0_col[:], -8.0)
        e.memset(g.ones_colb[:], 1.0)
        return e.memset(g.eps_col[:], 1e-5)
    P.step("vector", init)
    def initg(e):
        return e.iota(g.identb2[:], [[1, 128]], base=0, channel_multiplier=-1,
                      allow_small_or_imprecise_dtypes=True)
    P.step("gpsimd", initg)
    def initg2(e):
        return e.tensor_single_scalar(g.identb2[:], g.identb2[:], 0.0,
                                      ALU.is_equal)
    P.step("vector", initg2)
    return g


def emit_postln(P, g, oacc, stats, out_sb, D=1024, scr=None):
    ssum = stats[:, 0:1]; negm = stats[:, 1:2]; ssq = stats[:, 2:3]
    lnv = stats[:, 3:4]; rstd = stats[:, 4:5]
    P.step("vector", lambda e: e.reduce_sum(ssum, oacc[:], axis=AX.X))
    P.step("vector", lambda e: e.tensor_scalar_mul(negm, ssum, -1.0 / D))
    def s2(e):
        return e.activation(scr[:, 0:D], oacc[:], AF.Square, bias=negm,
                            accum_out=ssq)
    P.step("scalar", s2)
    P.step("scalar", lambda e: e.activation(lnv, ssq, AF.Ln, scale=1.0 / D,
                                            bias=g.eps_col[:]))
    P.step("scalar", lambda e: e.activation(rstd, lnv, AF.Exp, scale=-0.5))
    def s4(e):
        return e.tensor_scalar(out_sb[:], oacc[:], negm, rstd,
                               ALU.add, ALU.mult)
    P.step("vector", s4)


def emit_l0_attn(P, nc, ex, g, io, T):
    D = 1024
    NT = T // 128
    NSUP = T // 1024
    NQB = T // 512

    from contextlib import ExitStack
    exB = ExitStack()
    sb = lambda n, s, d: ex.enter_context(nc.sbuf_tensor(n, s, d))
    ps = lambda n, s, d: ex.enter_context(nc.psum_tensor(n, s, d))
    sbB = lambda n, s, d: exB.enter_context(nc.sbuf_tensor(n, s, d))
    psB = lambda n, s, d: exB.enter_context(nc.psum_tensor(n, s, d))

    qhatT = sb("qhatT", [128, T], BF16)
    kT = sb("kT", [128, T], BF16)
    oscol = sb("oscol", [128, 4], F32)
    bout_row = sb("bout_row", [1, 1024], F32)
    sqscr = sb("sqscr", [128, 1024], F32)
    Whg = sbB("Whg", [128, 8, 4096], BF16)
    Wqkg = sbB("Wqkg", [128, 8, 128], BF16)
    beff_h = sbB("beff_h", [1, 4096], F32)
    beff_qk = sbB("beff_qk", [1, 128], F32)
    lng_c = sbB("lng_c", [128, 8], F32)
    lnb_c = sbB("lnb_c", [128, 8], BF16)
    bh_row = sbB("bh_row", [1, 4096], F32)
    bqk_row = sbB("bqk_row", [1, 128], F32)
    riota = sbB("riota", [1, 128], F32)
    pr0 = psB("pr0", [1, 512], F32)
    pr1 = psB("pr1", [1, 512], F32)

    def p0_sync(e):
        load_vec_as_cols(P, e, lng_c, io["ln_g"], D)
        P.dma(e, oscol[:, 0:1], io["os_gamma"][0:1, :].rearrange("a p -> p a"),
              allow_slow_non_contiguous=True)
        P.dma(e, oscol[:, 1:2], io["os_gamma"][1:2, :].rearrange("a p -> p a"),
              allow_slow_non_contiguous=True)
        P.dma(e, oscol[:, 2:3], io["os_beta"][0:1, :].rearrange("a p -> p a"),
              allow_slow_non_contiguous=True)
        P.dma(e, oscol[:, 3:4], io["os_beta"][1:2, :].rearrange("a p -> p a"),
              allow_slow_non_contiguous=True)
        P.dma(e, bh_row[:], io["b_hidden"].rearrange("(a f) -> a f", a=1))
        P.dma(e, bqk_row[:], io["b_qk"].rearrange("(a f) -> a f", a=1))
        P.dma(e, bout_row[:], io["b_out"].rearrange("(a f) -> a f", a=1))
    P.step("sync", p0_sync)

    def p0_gps(e):
        load_vec_as_cols(P, e, lnb_c, io["ln_b"], D)
        for c in range(8):
            P.dma(e, Whg[:, c, :], io["W_hidden"][c * 128:(c + 1) * 128, :])
            P.dma(e, Wqkg[:, c, :], io["W_qk"][c * 128:(c + 1) * 128, :])
        return None
    P.step("gpsimd", p0_gps)


    for fb in range(8):
        def tmm(e, fb=fb):
            p = pr0 if fb % 2 == 0 else pr1
            last = None
            for c in range(8):
                last = e.matmul(p[:], lnb_c[:, c:c + 1],
                                Whg[:, c, fb * 512:(fb + 1) * 512],
                                start=(c == 0), stop=(c == 7))
            return last
        def tev(e, fb=fb):
            p = pr0 if fb % 2 == 0 else pr1
            return e.tensor_add(beff_h[:, fb * 512:(fb + 1) * 512], p[:],
                                bh_row[:, fb * 512:(fb + 1) * 512])
        P.step("tensor", tmm)
        P.step("vector", tev)
    def tmm_qk(e):
        last = None
        for c in range(8):
            last = e.matmul(pr0[:, 0:128], lnb_c[:, c:c + 1], Wqkg[:, c, :],
                            start=(c == 0), stop=(c == 7))
        return last
    P.step("tensor", tmm_qk)
    def tev_qk(e):
        return e.tensor_add(beff_qk[:], pr0[:, 0:128], bqk_row[:])
    P.step("vector", tev_qk)

    def wscale(e):
        last = None
        for c in range(8):
            e.tensor_scalar_mul(Whg[:, c, :], Whg[:, c, :], lng_c[:, c:c + 1])
            last = e.tensor_scalar_mul(Wqkg[:, c, :], Wqkg[:, c, :],
                                       lng_c[:, c:c + 1])
        return last
    P.step("vector", wscale)

    exA = ExitStack()
    sbA = lambda n, s, d: exA.enter_context(nc.sbuf_tensor(n, s, d))
    NA = 4
    sxA = sbA("sxA", [128, NA, 1024], F32)
    snrmA = sbA("snrmA", [128, NA, 1024], BF16)
    statA = sbA("statA", [128, 32], F32)
    sq8 = sbA("sq8", [128, NA, 1024], F32)
    for sup in range(T // (128 * NA)):
        ts = sup * 128 * NA
        def a1(e, ts=ts):
            P.dma(e, sxA[:], io["x"][ts:ts + 128 * NA, :]
                  .rearrange("(a p) d -> p a d", p=128))
        P.step("sync", a1)
        P.step("vector", lambda e: e.tensor_reduce(statA[:, 0:NA], sxA[:],
                                                   AX.X, ALU.add))
        P.step("vector", lambda e: e.tensor_scalar_mul(statA[:, 8:8 + NA],
                                                       statA[:, 0:NA], -1.0 / D))
        def a3(e):
            last = None
            for a in range(NA):
                last = e.activation(sq8[:, a, :], sxA[:, a, :], AF.Square,
                                    bias=statA[:, 8 + a:9 + a],
                                    accum_out=statA[:, 16 + a:17 + a])
            return last
        P.step("scalar", a3)
        P.step("scalar", lambda e: e.activation(statA[:, 24:24 + NA],
                                                statA[:, 16:16 + NA], AF.Ln,
                                                scale=1.0 / D,
                                                bias=g.eps_col[:]))
        P.step("scalar", lambda e: e.activation(statA[:, 24:24 + NA],
                                                statA[:, 24:24 + NA], AF.Exp,
                                                scale=-0.5))
        def a5(e):
            last = None
            for a in range(NA):
                last = e.tensor_scalar(snrmA[:, a, :], sxA[:, a, :],
                                       statA[:, 8 + a:9 + a],
                                       statA[:, 24 + a:25 + a],
                                       ALU.add, ALU.mult)
            return last
        P.step("vector", a5)
        def a6(e, ts=ts):
            P.dma(e, io["nrm_hbm"][ts:ts + 128 * NA, :]
                  .rearrange("(a p) d -> p a d", p=128), snrmA[:])
        P.step("sync", a6)
    exA.close()
    nfm = sbB("nfm", [128, 8, 512], BF16)
    hid_tok = sbB("hid_tok", [128, 4096], BF16)
    qkf = sbB("qkf", [128, 128], F32)
    qtmp = sbB("qtmp", [128, 128], F32)
    pb = [psB(f"pb{i}", [128, 512], F32) for i in range(4)]
    pq = psB("pq", [128, 128], F32)
    pbc = psB("pbc", [128, 128], F32)

    for sup in range(T // 512):
        ts = sup * 512
        def b1(e, ts=ts):
            for c in range(8):
                P.dmaT(e, nfm[:, c, :],
                       io["nrm_hbm"][ts:ts + 512, c * 128:(c + 1) * 128])
        P.step("sync", b1)
        for tsub in range(4):
            tg = ts + tsub * 128
            for half in range(2):
                def bmm(e, tsub=tsub, half=half):
                    last = None
                    for fb in range(half * 4, half * 4 + 4):
                        p = pb[fb % 4]
                        for c in range(8):
                            e.matmul(p[:], nfm[:, c, tsub * 128:(tsub + 1) * 128],
                                     Whg[:, c, fb * 512:(fb + 1) * 512],
                                     start=(c == 0), stop=False)
                        last = e.matmul(p[:], g.ones_row[:],
                                        beff_h[:, fb * 512:(fb + 1) * 512],
                                        start=False, stop=True)
                    return last
                P.step("tensor", bmm)
                def bev(e, half=half):
                    last = None
                    for fb in range(half * 4, half * 4 + 4):
                        last = e.activation(hid_tok[:, fb * 512:(fb + 1) * 512],
                                            pb[fb % 4][:], AF.Silu)
                    return last
                P.step("scalar", bev)
            def biot(e, tg=tg):
                return e.iota(riota[:], [[1, 128]], base=1 + tg,
                              channel_multiplier=0,
                              allow_small_or_imprecise_dtypes=True)
            P.step("gpsimd", biot)
            P.step("vector", lambda e: e.reciprocal(riota[:], riota[:]))
            def bqk(e, tsub=tsub, tg=tg):
                for c in range(8):
                    e.matmul(pq[:], nfm[:, c, tsub * 128:(tsub + 1) * 128],
                             Wqkg[:, c, :], start=(c == 0), stop=False)
                e.matmul(pq[:], g.ones_row[:], beff_qk[:],
                         start=False, stop=True)
                return e.matmul(pbc[:], g.ones_row[:], riota[:],
                                start=True, stop=True)
            P.step("tensor", bqk)
            def bqe(e):
                return e.activation(qkf[:], pq[:], AF.Silu)
            P.step("scalar", bqe)
            def bqv(e, tg=tg):
                e.tensor_scalar(qtmp[:], qkf[:], oscol[:, 0:1], oscol[:, 2:3],
                                ALU.mult, ALU.add)
                return e.tensor_scalar(kT[:, tg:tg + 128], qkf[:],
                                       oscol[:, 1:2], oscol[:, 3:4],
                                       ALU.mult, ALU.add)
            P.step("vector", bqv)
            def bqv2(e, tg=tg):
                return e.tensor_mul(qhatT[:, tg:tg + 128], qtmp[:], pbc[:])
            P.step("vector", bqv2)
            def bhw(e, tg=tg):
                P.dma(e, io["hid_hbm"][tg:tg + 128, :], hid_tok[:])
            P.step("sync", bhw)

    exB.close()
    attnT = sb("attnT", [128, NT, 512], BF16)
    relu_s = [sb(f"relu{i}", [128, 512], F32) for i in range(2)]
    vslice = sb("vslice", [128, NT, 128], BF16)
    gatef = sb("gatef", [128, 512], BF16)
    gated = sb("gated", [128, 16, 512], BF16)
    Woutb = sb("Woutb", [128, 16, 1024], BF16)
    xres = sb("xres", [128, 4, 1024], F32)
    oacc = sb("oacc", [128, 1024], F32)
    onrm = sb("onrm", [128, 1024], F32)
    statC = sb("statC", [128, 8], F32)
    ps_sim = [ps(f"psim{i}", [128, 512], F32) for i in range(2)]
    ps_o = ps("ps_o", [128, 512], F32)
    ps_w = ps("ps_w", [128, 512], F32)

    def c0(e):
        for c in range(16):
            P.dma(e, Woutb[:, c, :], io["W_out"][c * 128:(c + 1) * 128, :])
    P.step("gpsimd", c0)

    for qb in range(NQB):
        nk = 4 * (qb + 1)
        qs = qb * 512
        for k0 in range(0, nk, 2):
            kk = [k for k in (k0, k0 + 1) if k < nk]
            def csim(e, kk=kk, qs=qs):
                for j, k in enumerate(kk):
                    last = e.matmul(ps_sim[j][:], kT[:, k * 128:(k + 1) * 128],
                                    qhatT[:, qs:qs + 512], start=True, stop=True)
                return last
            P.step("tensor", csim)
            def crel(e, kk=kk):
                last = None
                for j, k in enumerate(kk):
                    last = e.activation(relu_s[j][:], ps_sim[j][:], AF.Relu)
                return last
            P.step("scalar", crel)
            def csq(e, kk=kk):
                last = None
                for j, k in enumerate(kk):
                    last = e.tensor_mul(attnT[:, k, :], relu_s[j][:],
                                        relu_s[j][:])
                return last
            P.step("vector", csq)
        def cdiag(e, qb=qb, nk=nk):
            last = None
            for k in range(nk - 4, nk):
                rel = k * 128 - qb * 512
                last = e.affine_select(attnT[:, k, :], attnT[:, k, :],
                                       [[1, 512]], ALU.is_ge, 0.0,
                                       base=-rel, channel_multiplier=-1)
            return last
        P.step("gpsimd", cdiag)

        for vf in range(16):
            def cvl(e, vf=vf, nk=nk, qs=qs):
                P.dma(e, vslice[:, 0:nk, :],
                      io["hid_hbm"][0:nk * 128, vf * 128:(vf + 1) * 128]
                      .rearrange("(k p) f -> p k f", p=128))
                P.dmaT(e, gatef[:],
                       io["hid_hbm"][qs:qs + 512,
                                     2048 + vf * 128:2048 + (vf + 1) * 128])
            P.step("sync", cvl)
            def cav(e, vf=vf, nk=nk):
                last = None
                for k in range(nk):
                    last = e.matmul(ps_o[:], vslice[:, k, :], attnT[:, k, :],
                                    start=(k == 0), stop=(k == nk - 1))
                return last
            P.step("tensor", cav)
            def cgt(e, vf=vf):
                return e.tensor_mul(gated[:, vf, :], ps_o[:], gatef[:])
            P.step("vector", cgt)

        def cxl(e, qs=qs):
            P.dma(e, xres[:], io["x"][qs:qs + 512, :]
                  .rearrange("(a p) d -> p a d", p=128))
        P.step("sync", cxl)
        for tsub in range(4):
            for dh in range(2):
                def cwm(e, tsub=tsub, dh=dh):
                    for c in range(16):
                        e.matmul(ps_w[:], gated[:, c, tsub * 128:(tsub + 1) * 128],
                                 Woutb[:, c, dh * 512:(dh + 1) * 512],
                                 start=(c == 0), stop=False)
                    return e.matmul(ps_w[:], g.ones_row[:],
                                    bout_row[:, dh * 512:(dh + 1) * 512],
                                    start=False, stop=True)
                P.step("tensor", cwm)
                def cwe(e, tsub=tsub, dh=dh):
                    return e.scalar_tensor_tensor(
                        oacc[:, dh * 512:(dh + 1) * 512], xres[:, tsub, dh * 512:(dh + 1) * 512],
                        2.0, ps_w[:], ALU.mult, ALU.add)
                P.step("vector", cwe)
            emit_postln(P, g, oacc, statC, onrm, D=D, scr=sqscr)
            def cxw(e, tsub=tsub, qs=qs):
                P.dma(e, io["x1"][qs + tsub * 128:qs + (tsub + 1) * 128, :],
                      onrm[:])
            P.step("sync", cxw)


def emit_rms_spill(P, nc, g, io_in, io_out, T, gbc, scale_ln_col, tag):
    from contextlib import ExitStack
    ex = ExitStack()
    D = 1024
    NA = 4
    sx = ex.enter_context(nc.sbuf_tensor(f"rms_sx{tag}", [128, NA, 1024], F32))
    so = ex.enter_context(nc.sbuf_tensor(f"rms_so{tag}", [128, NA, 1024], BF16))
    st = ex.enter_context(nc.sbuf_tensor(f"rms_st{tag}", [128, 16], F32))
    sq = ex.enter_context(nc.sbuf_tensor(f"rms_sq{tag}", [128, NA, 1024], F32))
    for sup in range(T // (128 * NA)):
        ts = sup * 128 * NA
        def r1(e, ts=ts):
            P.dma(e, sx[:], io_in[ts:ts + 128 * NA, :]
                  .rearrange("(a p) d -> p a d", p=128))
        P.step("sync", r1)
        def r2(e):
            last = None
            for a in range(NA):
                last = e.activation(sq[:, a, :], sx[:, a, :], AF.Square,
                                    accum_out=st[:, a:a + 1])
            return last
        P.step("scalar", r2)
        P.step("vector", lambda e: e.tensor_scalar_max(st[:, 0:NA],
                                                       st[:, 0:NA], 1e-24))
        P.step("scalar", lambda e: e.activation(st[:, 8:8 + NA], st[:, 0:NA],
                                                AF.Ln))
        P.step("scalar", lambda e: e.activation(st[:, 8:8 + NA], st[:, 8:8 + NA],
                                                AF.Exp, scale=-0.5,
                                                bias=scale_ln_col[:]))
        def r5(e):
            last = None
            for a in range(NA):
                last = e.tensor_scalar_mul(sq[:, a, :], sx[:, a, :],
                                           st[:, 8 + a:9 + a])
            return last
        P.step("vector", r5)
        def r6(e):
            last = None
            for a in range(NA):
                last = e.tensor_mul(so[:, a, :], sq[:, a, :], gbc[:])
            return last
        P.step("vector", r6)
        def r7(e, ts=ts):
            P.dma(e, io_out[ts:ts + 128 * NA, :]
                  .rearrange("(a p) d -> p a d", p=128), so[:])
        P.step("sync", r7)
    ex.close()


def emit_bcast_row(P, nc, g, dram_vec, out_tile, psb, n):
    row = out_tile
    def s1(e):
        P.dma(e, g.row_scr[:, 0:n], dram_vec.rearrange("(a f) -> a f", a=1))
    P.step("sync", s1)
    for j0 in range(0, n, 512):
        w = min(512, n - j0)
        def s2(e, j0=j0, w=w):
            return e.matmul(psb[:, 0:w], g.ones_row[:], g.row_scr[:, j0:j0 + w],
                            start=True, stop=True)
        P.step("tensor", s2)
        def s3(e, j0=j0, w=w):
            return e.tensor_copy(out_tile[:, j0:j0 + w], psb[:, 0:w])
        P.step("vector", s3)


def emit_l0_ffn(P, nc, ex, g, io, T):
    from contextlib import ExitStack
    D = 1024
    ES = 4096
    NT = T // 128

    sb = lambda n, s, d: ex.enter_context(nc.sbuf_tensor(n, s, d))
    ps = lambda n, s, d: ex.enter_context(nc.psum_tensor(n, s, d))

    ex0 = ExitStack()
    gxbc = sb("gxbc", [128, 1024], F32)
    gsbc = sb("gsbc", [128, 1024], F32)
    ln32 = sb("ln32", [128, 1], F32)
    Mn_all = sb("Mn_all", [128, 64], F32)
    Sn_all = sb("Sn_all", [128, 64], F32)
    MnT = sb("MnT", [64, 128], F32)
    identf = sb("identf", [128, 128], F32)
    ps_pro = ex0.enter_context(nc.psum_tensor("ps_pro", [128, 512], F32))
    P.step("vector", lambda e: e.memset(ln32[:], math.log(32.0)))
    emit_bcast_row(P, nc, g, io["norm_gamma"], gxbc, ps_pro, 1024)
    emit_bcast_row(P, nc, g, io["slot_norm_gamma"], gsbc, ps_pro, 1024)
    def idgen(e):
        return e.iota(identf[:], [[1, 128]], base=0, channel_multiplier=-1,
                      allow_small_or_imprecise_dtypes=True)
    P.step("gpsimd", idgen)
    def idgen2(e):
        return e.tensor_single_scalar(identf[:], identf[:], 0.0, ALU.is_equal)
    P.step("vector", idgen2)

    emit_rms_spill(P, nc, g, io["x"], io["xn_hbm"], T, gxbc, ln32, "x")
    se_flat = io["slot_embeds"].rearrange("e s d -> (e s) d")
    emit_rms_spill(P, nc, g, se_flat, io["sn_hbm"], 4096, gsbc, ln32, "s")
    ex0.close()

    exF = ExitStack()
    sbF = lambda n, s, d: exF.enter_context(nc.sbuf_tensor(n, s, d))
    psF = lambda n, s, d: exF.enter_context(nc.psum_tensor(n, s, d))
    sn_fm = sbF("sn_fm", [128, 8, ES], BF16)
    xnf2 = [sbF(f"xnf{i}", [128, 8, 512], BF16) for i in range(2)]
    lgt = sbF("lgt", [128, ES], BF16)
    lgl = sbF("lgl", [128, ES], BF16)
    mcols = sbF("mcols", [128, 8], F32)
    plg = [psF(f"plg{i}", [128, 512], F32) for i in range(2)]
    f3id = None
    def f3(e):
        for c in range(8):
            P.dmaT(e, sn_fm[:, c, :], io["sn_hbm"][:, c * 128:(c + 1) * 128])
    f3id = P.step("sync", f3)
    fd = {}
    for sup in range(T // 512):
        ts = sup * 512
        def f4a(e, ts=ts, sup=sup):
            for c in range(8):
                P.dmaT(e, xnf2[sup % 2][:, c, :],
                       io["xn_hbm"][ts:ts + 512, c * 128:(c + 1) * 128])
        fd[("ld", sup)] = P.step("sync", f4a,
                                 deps=[fd.get(("mm", sup - 2, 3, 7))])
        for nsub in range(4):
            nt = (ts + nsub * 128) // 128
            for esb in range(8):
                def f4m(e, nsub=nsub, esb=esb, sup=sup):
                    p = plg[esb % 2]
                    last = None
                    for c in range(8):
                        last = e.matmul(p[:], xnf2[sup % 2][:, c, nsub * 128:(nsub + 1) * 128],
                                        sn_fm[:, c, esb * 512:(esb + 1) * 512],
                                        start=(c == 0), stop=(c == 7))
                    return last
                fd[("mm", sup, nsub, esb)] = P.step(
                    "tensor", f4m,
                    deps=[fd[("ld", sup)] if (nsub == 0 and esb == 0) else None,
                          f3id if sup == 0 and nsub == 0 and esb == 0 else None,
                          fd.get(("lo", sup, nsub, esb - 2)),
                          fd.get(("lo", sup, nsub - 1, 6 + esb)) if esb < 2 else None,
                          fd.get(("lo", sup - 1, 3, 6 + esb))
                          if (esb < 2 and nsub == 0) else None])
                def f4e(e, esb=esb):
                    e.tensor_copy(lgt[:, esb * 512:(esb + 1) * 512],
                                  plg[esb % 2][:])
                    return e.reduce_max(mcols[:, esb:esb + 1], plg[esb % 2][:],
                                        axis=AX.X)
                fd[("ev", sup, nsub, esb)] = P.step(
                    "vector", f4e, deps=[fd[("mm", sup, nsub, esb)],
                                         fd.get(("wr", sup, nsub - 1))
                                         if esb == 0 else None,
                                         fd.get(("wr", sup - 1, 3))
                                         if (esb == 0 and nsub == 0) else None])
                def f4l(e, esb=esb):
                    return e.tensor_sub(lgl[:, esb * 512:(esb + 1) * 512],
                                        plg[esb % 2][:],
                                        lgt[:, esb * 512:(esb + 1) * 512])
                fd[("lo", sup, nsub, esb)] = P.step("vector", f4l)
            def f4M(e, nt=nt):
                return e.reduce_max(Mn_all[:, nt:nt + 1], mcols[:], axis=AX.X)
            P.step("vector", f4M)
            def f4w(e, nt=nt):
                P.dma(e, io["lg_hbm"][nt * 128:(nt + 1) * 128, :], lgt[:])
                P.dma(e, io["lglo_hbm"][nt * 128:(nt + 1) * 128, :], lgl[:])
            fd[("wr", sup, nsub)] = P.step("sync", f4w,
                                           deps=[len(P.steps) - 1])
    exF.close()

    exG = ExitStack()
    sbG = lambda n, s, d: exG.enter_context(nc.sbuf_tensor(n, s, d))
    lgtT = sbG("lgtT", [128, T], BF16)
    lgtTl = sbG("lgtTl", [128, T], BF16)
    lgtTf = sbG("lgtTf", [128, T], F32)
    dspT = sbG("dspT", [128, T], BF16)
    mes = sbG("mes", [128, 4], F32)
    for et in range(32):
        def f5a(e, et=et):
            P.dmaT(e, lgtT[:], io["lg_hbm"][:, et * 128:(et + 1) * 128])
            P.dmaT(e, lgtTl[:], io["lglo_hbm"][:, et * 128:(et + 1) * 128])
        P.step("sync", f5a)
        P.step("vector", lambda e: e.tensor_add(lgtTf[:], lgtT[:], lgtTl[:]))
        P.step("vector", lambda e: e.reduce_max(mes[:, 0:1], lgtTf[:], axis=AX.X))
        P.step("vector", lambda e: e.tensor_scalar_mul(mes[:, 1:2], mes[:, 0:1],
                                                       -1.0))
        def f5e(e):
            return e.activation(dspT[:], lgtTf[:], AF.Exp, bias=mes[:, 1:2],
                                accum_out=mes[:, 2:3])
        P.step("scalar", f5e)
        P.step("vector", lambda e: e.reciprocal(mes[:, 3:4], mes[:, 2:3]))
        P.step("vector", lambda e: e.tensor_scalar_mul(dspT[:], dspT[:],
                                                       mes[:, 3:4]))
        def f5d(e, et=et):
            P.dma(e, io["disp_hbm"][et * 128:(et + 1) * 128, :], dspT[:])
        P.step("sync", f5d)
    exG.close()

    exH = ExitStack()
    sbH = lambda n, s, d: exH.enter_context(nc.sbuf_tensor(n, s, d))
    psH = lambda n, s, d: exH.enter_context(nc.psum_tensor(n, s, d))
    xtk2 = [sbH(f"xtk{i}", [128, 1024], BF16) for i in range(2)]
    dtk2 = [sbH(f"dtk{i}", [128, 512], BF16) for i in range(2)]
    slo = sbH("slo", [128, 8, 512], BF16)
    psl = [psH(f"psl{i}", [128, 512], F32) for i in range(8)]
    hd = {}
    for esb in range(8):
        for n in range(NT):
            it = esb * NT + n
            def f6a(e, esb=esb, n=n, it=it):
                P.dma(e, xtk2[it % 2][:], io["xn_hbm"][n * 128:(n + 1) * 128, :])
                P.dmaT(e, dtk2[it % 2][:],
                       io["disp_hbm"][esb * 512:(esb + 1) * 512,
                                      n * 128:(n + 1) * 128])
            hd[("ld", it)] = P.step("sync", f6a, deps=[hd.get(("mm", it - 2))])
            def f6m(e, esb=esb, n=n, it=it):
                last = None
                for dc in range(8):
                    last = e.matmul(psl[dc][:],
                                    xtk2[it % 2][:, dc * 128:(dc + 1) * 128],
                                    dtk2[it % 2][:], start=(n == 0),
                                    stop=(n == NT - 1))
                return last
            hd[("mm", it)] = P.step("tensor", f6m,
                                    deps=[hd[("ld", it)],
                                          hd.get(("ev", esb - 1)) if n == 0 else None])
        def f6e(e, esb=esb):
            last = None
            for dc in range(8):
                last = e.tensor_copy(slo[:, dc, :], psl[dc][:])
            return last
        hd[("ev", esb)] = P.step("vector", f6e,
                                 deps=[hd[("mm", esb * NT + NT - 1)],
                                       hd.get(("wr", esb - 1))])
        def f6w(e, esb=esb):
            P.dma(e, io["slots_hbm"].rearrange("(dc p) es -> p dc es", p=128)
                  [:, :, esb * 512:(esb + 1) * 512], slo[:])
        hd[("wr", esb)] = P.step("sync", f6w, deps=[hd[("ev", esb)]])
    exH.close()

    eo_all = sb("eo_all", [128, 32, 1024], BF16)
    exI = ExitStack()
    sbI = lambda n, s, d: exI.enter_context(nc.sbuf_tensor(n, s, d))
    psI = lambda n, s, d: exI.enter_context(nc.psum_tensor(n, s, d))
    w1e = sbI("w1e", [128, 8, 4096], BF16)
    slte = sbI("slte", [128, 8, 512], BF16)
    hfm = sbI("hfm", [128, 32, 512], BF16)
    w2c = [sbI(f"w2c{i}", [128, 1024], BF16) for i in range(2)]
    b1c = sbI("b1c", [128, 32], F32)
    b2r = sbI("b2r", [1, 1024], F32)
    f7_last = None
    for ee in range(8):
        exE = ExitStack()
        ph = [exE.enter_context(nc.psum_tensor(f"ph{ee}_{i}", [128, 512], F32))
              for i in range(2)]
        def f7a(e, ee=ee):
            for dc in range(8):
                P.dma(e, w1e[:, dc, :], io["w1"][ee, dc * 128:(dc + 1) * 128, :])
            P.dma(e, slte[:],
                  io["slots_hbm"].rearrange("(dc p) es -> p dc es", p=128)
                  [:, :, ee * 512:(ee + 1) * 512])
            load_vec_as_cols(P, e, b1c, io["b1"][ee, :], 4096)
        P.step("gpsimd", f7a)
        def f7b(e, ee=ee):
            P.dma(e, b2r[:], io["b2"][ee:ee + 1, :])
        P.step("sync", f7b)
        gd = {}
        for hc in range(32):
            def f7m(e, hc=hc):
                p = ph[hc % 2]
                last = None
                for dc in range(8):
                    last = e.matmul(p[:], w1e[:, dc, hc * 128:(hc + 1) * 128],
                                    slte[:, dc, :], start=(dc == 0),
                                    stop=(dc == 7))
                return last
            gd[("mm", hc)] = P.step("tensor", f7m,
                                    deps=[gd.get(("gl", hc - 2))] if hc >= 2
                                    else None)
            def f7g(e, hc=hc):
                return e.activation(hfm[:, hc, :], ph[hc % 2][:], AF.Gelu,
                                    bias=b1c[:, hc:hc + 1])
            gd[("gl", hc)] = P.step("scalar", f7g, deps=[gd[("mm", hc)]])
        for p in ph:
            pass
        exE.close()
        exE = ExitStack()
        peo = [exE.enter_context(nc.psum_tensor(f"peo{ee}_{i}", [128, 512], F32))
               for i in range(8)]
        for hc in range(32):
            def f7w2(e, ee=ee, hc=hc):
                P.dma(e, w2c[hc % 2][:], io["w2"][ee, hc * 128:(hc + 1) * 128, :])
            gd[("w2", hc)] = P.step("gpsimd", f7w2,
                                    deps=[gd.get(("eo", hc - 2)),
                                          gd[("gl", 31)] if hc == 0 else None])
            def f7e(e, hc=hc):
                last = None
                for ssub in range(4):
                    for dh in range(2):
                        last = e.matmul(
                            peo[ssub * 2 + dh][:],
                            hfm[:, hc, ssub * 128:(ssub + 1) * 128],
                            w2c[hc % 2][:, dh * 512:(dh + 1) * 512],
                            start=(hc == 0), stop=False)
                return last
            gd[("eo", hc)] = P.step("tensor", f7e,
                                    deps=[gd[("w2", hc)],
                                          gd[("gl", hc)]])
        def f7bias(e):
            last = None
            for ssub in range(4):
                for dh in range(2):
                    last = e.matmul(peo[ssub * 2 + dh][:], g.ones_row[:],
                                    b2r[:, dh * 512:(dh + 1) * 512],
                                    start=False, stop=True)
            return last
        P.step("tensor", f7bias)
        def f7v(e, ee=ee):
            last = None
            for ssub in range(4):
                for dh in range(2):
                    last = e.tensor_copy(
                        eo_all[:, ee * 4 + ssub, dh * 512:(dh + 1) * 512],
                        peo[ssub * 2 + dh][:])
            return last
        f7_last = P.step("vector", f7v)
        exE.close()
    exI.close()

    exJ = ExitStack()
    sbJ = lambda n, s, d: exJ.enter_context(nc.sbuf_tensor(n, s, d))
    psJ = lambda n, s, d: exJ.enter_context(nc.psum_tensor(n, s, d))
    lgn2 = [sbJ(f"lgn{i}", [128, ES], BF16) for i in range(2)]
    lgnl2 = [sbJ(f"lgnl{i}", [128, ES], BF16) for i in range(2)]
    lgnf2 = [sbJ(f"lgnf{i}", [128, ES], F32) for i in range(2)]
    cet2 = [sbJ(f"cet{i}", [128, ES], BF16) for i in range(2)]
    nmn = sbJ("nmn", [128, 2], F32)
    md = {}
    for n in range(NT):
        def h1(e, n=n):
            P.dma(e, lgn2[n % 2][:], io["lg_hbm"][n * 128:(n + 1) * 128, :])
            P.dma(e, lgnl2[n % 2][:], io["lglo_hbm"][n * 128:(n + 1) * 128, :])
        md[("ld", n)] = P.step("sync", h1,
                               deps=[f7_last if n == 0 else None,
                                     md.get(("ad", n - 2))])
        def had(e, n=n):
            e.tensor_add(lgnf2[n % 2][:], lgn2[n % 2][:], lgnl2[n % 2][:])
            return e.tensor_scalar_mul(nmn[:, n % 2:n % 2 + 1],
                                       Mn_all[:, n:n + 1], -1.0)
        md[("ad", n)] = P.step("vector", had,
                               deps=[md[("ld", n)], md.get(("ex", n - 2))])
        def h3(e, n=n):
            return e.activation(cet2[n % 2][:], lgnf2[n % 2][:], AF.Exp,
                                bias=nmn[:, n % 2:n % 2 + 1],
                                accum_out=Sn_all[:, n:n + 1])
        md[("ex", n)] = P.step("scalar", h3,
                               deps=[md[("ad", n)], md.get(("wr", n - 2))])
        def h4(e, n=n):
            P.dma(e, io["ce_hbm"][n * 128:(n + 1) * 128, :], cet2[n % 2][:])
        md[("wr", n)] = P.step("sync", h4, deps=[md[("ex", n)]])
    exJ.close()

    exK = ExitStack()
    sbK = lambda n, s, d: exK.enter_context(nc.sbuf_tensor(n, s, d))
    psK = lambda n, s, d: exK.enter_context(nc.psum_tensor(n, s, d))
    ceT2 = [sbK(f"ceT2{i}", [128, 32, 512], BF16) for i in range(2)]
    xre = sbK("xre", [128, 1024], F32)
    oac = sbK("oac", [128, 1024], F32)
    onr = sbK("onr", [128, 1024], F32)
    stJ = sbK("stJ", [128, 8], F32)
    snc = sbK("snc", [128, 1], F32)
    scrK = sbK("scrK", [128, 1024], F32)
    pb8 = [psK(f"pb8{i}", [128, 512], F32) for i in range(2)]
    nd = {}
    for nsup in range(ceil_div(NT, 4)):
      ns0 = nsup * 512
      nw = min(512, T - ns0)
      def k0(e, ns0=ns0, nw=nw, nsup=nsup):
        for k in range(32):
            P.dmaT(e, ceT2[nsup % 2][:, k, 0:nw],
                   io["ce_hbm"][ns0:ns0 + nw, k * 128:(k + 1) * 128])
      nd[("ld", nsup)] = P.step("sync", k0,
                                deps=[md[("wr", NT - 1)] if nsup == 0 else None,
                                      nd.get(("mm", nsup - 2, 3, 1))])
      for n in range(ns0 // 128, (ns0 + nw) // 128):
        def k1(e, n=n):
            P.dma(e, xre[:], io["x"][n * 128:(n + 1) * 128, :])
        P.step("sync", k1)
        def k2(e, n=n):
            return e.reciprocal(snc[:], Sn_all[:, n:n + 1])
        P.step("vector", k2)
        for dh in range(2):
            def k3(e, n=n, ns0=ns0, dh=dh, nsup=nsup):
                nn0 = n * 128 - ns0
                last = None
                for k in range(32):
                    last = e.matmul(pb8[dh][:],
                                    ceT2[nsup % 2][:, k, nn0:nn0 + 128],
                                    eo_all[:, k, dh * 512:(dh + 1) * 512],
                                    start=(k == 0), stop=(k == 31))
                return last
            nd[("mm", nsup, (n * 128 - ns0) // 128, dh)] = P.step(
                "tensor", k3, deps=[nd[("ld", nsup)], len(P.steps) - 1]
                if (n * 128 == ns0 and dh == 0) else None)
        def k4(e):
            e.tensor_scalar(oac[:, 0:512], pb8[0][:], snc[:], None, ALU.mult)
            return e.tensor_scalar(oac[:, 512:1024], pb8[1][:], snc[:], None,
                                   ALU.mult)
        P.step("vector", k4)
        P.step("vector", lambda e: e.tensor_add(oac[:], oac[:], xre[:]))
        emit_postln(P, g, oac, stJ, onr, D=D, scr=scrK)
        def k5(e, n=n):
            P.dma(e, io["x2"][n * 128:(n + 1) * 128, :], onr[:])
        P.step("sync", k5)
    exK.close()


def emit_l1_attn(P, nc, ex, g, io, T):
    from contextlib import ExitStack
    D = 1024
    H = 8
    NT = T // 128
    K0 = 8.0
    SC = 1.0 / math.sqrt(128.0)

    sb = lambda n, s, d: ex.enter_context(nc.sbuf_tensor(n, s, d))

    gnw_bc = sb("gnw_bc", [128, 1024], F32)
    g1_fm = sb("g1_fm", [128, T], BF16)
    Wg2T = sb("Wg2T", [128, 1024], BF16)
    identb = sb("identb", [128, 128], BF16)
    ln1 = sb("ln1", [128, 1], F32)

    ex0 = ExitStack()
    ps_pro = ex0.enter_context(nc.psum_tensor("ps_pro1", [128, 512], F32))
    emit_bcast_row(P, nc, g, io["g_norm_weight"], gnw_bc, ps_pro, 1024)
    def pid(e):
        return e.iota(ln1[:], [[1, 1]], base=0, channel_multiplier=0,
                      allow_small_or_imprecise_dtypes=True)
    P.step("gpsimd", pid)
    scp = ex0.enter_context(nc.sbuf_tensor("scp", [128, 512], F32))
    def pidb(e):
        return e.iota(scp[:, 0:128], [[1, 128]], base=0, channel_multiplier=-1,
                      allow_small_or_imprecise_dtypes=True)
    P.step("gpsimd", pidb)
    P.step("vector", lambda e: e.tensor_single_scalar(scp[:, 0:128],
                                                      scp[:, 0:128], 0.0,
                                                      ALU.is_equal))
    P.step("vector", lambda e: e.tensor_copy(identb[:], scp[:, 0:128]))
    def spill(e):
        for r in range(0, 1024, 128):
            P.dma(e, io["wq_bf"][r:r + 128, :], io["Wq"][r:r + 128, :])
            P.dma(e, io["wk_bf"][r:r + 128, :], io["Wk"][r:r + 128, :])
            P.dma(e, io["wv_bf"][r:r + 128, :], io["Wv"][r:r + 128, :])
            P.dma(e, io["wo_bf"][r:r + 128, :], io["Wo"][r:r + 128, :])
        P.dma(e, io["wg1_bf"][:, :], io["Wg1"][:, :])
        for r in range(0, 1024, 128):
            P.dma(e, io["wg2_bf"][r:r + 128, :], io["Wg2"][r:r + 128, :])
        for t0 in range(0, T, 512):
            P.dma(e, io["x_bf"][t0:t0 + 512, :], io["x"][t0:t0 + 512, :])
    P.step("gpsimd", spill)
    ex0.close()

    exB = ExitStack()
    sbB = lambda n, s, d: exB.enter_context(nc.sbuf_tensor(n, s, d))
    psB = lambda n, s, d: exB.enter_context(nc.psum_tensor(n, s, d))
    WqT = sbB("WqT", [128, 8, 1024], BF16)
    WkT = sbB("WkT", [128, 8, 1024], BF16)
    WvT = sbB("WvT", [128, 8, 1024], BF16)
    Wg1T = sbB("Wg1T", [128, 8, 128], BF16)
    xf = sbB("xf1", [128, 8, 512], BF16)
    prj = sbB("prj", [128, 1024], BF16)
    pp = [psB(f"pp1{i}", [128, 512], F32) for i in range(2)]
    def l1w(e):
        for c in range(8):
            for ob in range(2):
                P.dmaT(e, WqT[:, c, ob * 512:(ob + 1) * 512],
                       io["wq_bf"][ob * 512:(ob + 1) * 512, c * 128:(c + 1) * 128])
                P.dmaT(e, WkT[:, c, ob * 512:(ob + 1) * 512],
                       io["wk_bf"][ob * 512:(ob + 1) * 512, c * 128:(c + 1) * 128])
                P.dmaT(e, WvT[:, c, ob * 512:(ob + 1) * 512],
                       io["wv_bf"][ob * 512:(ob + 1) * 512, c * 128:(c + 1) * 128])
            P.dmaT(e, Wg1T[:, c, :], io["wg1_bf"][:, c * 128:(c + 1) * 128])
        for ob in range(2):
            P.dmaT(e, Wg2T[:, ob * 512:(ob + 1) * 512],
                   io["wg2_bf"][ob * 512:(ob + 1) * 512, :])
    P.step("sync", l1w)
    for sup in range(T // 512):
        ts = sup * 512
        def m1(e, ts=ts):
            for c in range(8):
                P.dmaT(e, xf[:, c, :], io["x_bf"][ts:ts + 512,
                                                  c * 128:(c + 1) * 128])
        P.step("sync", m1)
        for hh in range(8):
            def mq(e, hh=hh, ts=ts):
                for c in range(8):
                    e.matmul(pp[0][:], WqT[:, c, hh * 128:(hh + 1) * 128],
                             xf[:, c, :], start=(c == 0), stop=(c == 7))
                last = None
                for c in range(8):
                    last = e.matmul(pp[1][:], WkT[:, c, hh * 128:(hh + 1) * 128],
                                    xf[:, c, :], start=(c == 0), stop=(c == 7))
                return last
            P.step("tensor", mq)
            def mqe(e, hh=hh, ts=ts):
                return e.activation(prj[:, 0:512], pp[0][:], AF.Silu)
            P.step("scalar", mqe)
            def mqv(e, hh=hh, ts=ts):
                return e.tensor_copy(prj[:, 512:1024], pp[1][:])
            P.step("vector", mqv)
            def mqw(e, hh=hh, ts=ts):
                P.dma(e, io["qf_hbm"][hh * 128:(hh + 1) * 128, ts:ts + 512],
                      prj[:, 0:512])
                P.dma(e, io["kf_hbm"][hh * 128:(hh + 1) * 128, ts:ts + 512],
                      prj[:, 512:1024])
            P.step("sync", mqw)
        def mg(e, ts=ts):
            last = None
            for c in range(8):
                last = e.matmul(pp[0][:], Wg1T[:, c, :], xf[:, c, :],
                                start=(c == 0), stop=(c == 7))
            return last
        P.step("tensor", mg)
        def mge(e, ts=ts):
            return e.tensor_copy(g1_fm[:, ts:ts + 512], pp[0][:])
        P.step("vector", mge)
        for tsub in range(4):
            def mv(e, ts=ts, tsub=tsub):
                last = None
                for ob in range(2):
                    for c in range(8):
                        last = e.matmul(pp[ob][:],
                                        xf[:, c, tsub * 128:(tsub + 1) * 128],
                                        WvT[:, c, ob * 512:(ob + 1) * 512],
                                        start=(c == 0), stop=(c == 7))
                return last
            P.step("tensor", mv)
            def mve(e):
                e.tensor_copy(prj[:, 0:512], pp[0][:])
                return e.tensor_copy(prj[:, 512:1024], pp[1][:])
            P.step("vector", mve)
            def mvw(e, ts=ts, tsub=tsub):
                P.dma(e, io["vtok_hbm"][ts + tsub * 128:ts + (tsub + 1) * 128, :],
                      prj[:])
            P.step("sync", mvw)
    exB.close()

    exC = ExitStack()
    sbC = lambda n, s, d: exC.enter_context(nc.sbuf_tensor(n, s, d))
    psC = lambda n, s, d: exC.enter_context(nc.psum_tensor(n, s, d))
    kfh = sbC("kfh", [128, T], BF16)
    kap = sbC("kap", [128, T], BF16)
    csc = sbC("csc", [128, T], F32)
    qhh = sbC("qhh", [128, T], BF16)
    ktk = sbC("ktk", [128, NT, 128], BF16)
    vth = sbC("vth", [128, NT, 128], BF16)
    oth = sbC("oth", [128, NT, 128], BF16)
    zb = sbC("zb", [128, T], BF16)
    Sf = sbC("Sf", [128, 128], F32)
    Sb = sbC("Sb", [128, 128], BF16)
    ssc = sbC("ssc", [128, 128], BF16)
    ps1 = psC("ps1l", [128, 512], F32)
    ps2 = psC("ps2l", [128, 512], F32)
    psb = psC("psbl", [128, 512], BF16)
    P.step("vector", lambda e: e.memset(zb[:], 0.0))
    for hh in range(H):
        def n1(e, hh=hh):
            P.dma(e, kfh[:], io["kf_hbm"][hh * 128:(hh + 1) * 128, :])
            P.dma(e, qhh[:], io["qf_hbm"][hh * 128:(hh + 1) * 128, :])
            P.dma(e, vth[:], io["vtok_hbm"][:, hh * 128:(hh + 1) * 128]
                  .rearrange("(k p) f -> p k f", p=128))
        P.step("sync", n1)
        def n2(e):
            return e.activation(kap[:], kfh[:], AF.Exp, bias=g.negk0_col[:])
        P.step("scalar", n2)
        def n3(e):
            return e.tensor_tensor_scan(csc[:], kap[:], zb[:], 0.0,
                                        ALU.add, ALU.add)
        P.step("vector", n3)
        P.step("vector", lambda e: e.reciprocal(csc[:], csc[:]))
        def n5(e):
            return e.scalar_tensor_tensor(qhh[:], qhh[:], SC, csc[:],
                                          ALU.mult, ALU.mult)
        P.step("vector", n5)
        for c0 in range(0, NT, 4):
            def n6(e, c0=c0):
                last = None
                for j in range(4):
                    if c0 + j < NT:
                        last = e.transpose(psb[:, j * 128:(j + 1) * 128],
                                           kap[:, (c0 + j) * 128:(c0 + j + 1) * 128],
                                           identb[:])
                return last
            P.step("tensor", n6)
            def n7(e, c0=c0):
                last = None
                for j in range(4):
                    if c0 + j < NT:
                        last = e.tensor_copy(ktk[:, c0 + j, :],
                                             psb[:, j * 128:(j + 1) * 128])
                return last
            P.step("vector", n7)
        P.step("vector", lambda e: e.memset(Sf[:], 0.0))
        P.step("vector", lambda e: e.memset(Sb[:], 0.0))
        for ci in range(NT):
            def o1(e, ci=ci):
                return e.matmul(ps1[:, 0:128], kap[:, ci * 128:(ci + 1) * 128],
                                qhh[:, ci * 128:(ci + 1) * 128],
                                start=True, stop=True)
            P.step("tensor", o1)
            def o2(e, ci=ci):
                return e.tensor_copy(ssc[:], ps1[:, 0:128])
            P.step("vector", o2)
            def o3(e, ci=ci):
                return e.affine_select(ssc[:], ssc[:], [[1, 128]], ALU.is_ge,
                                       0.0, base=0, channel_multiplier=-1)
            P.step("gpsimd", o3)
            def o4(e, ci=ci):
                e.matmul(ps2[:, 0:128], ssc[:], vth[:, ci, :],
                         start=True, stop=False)
                e.matmul(ps2[:, 0:128], qhh[:, ci * 128:(ci + 1) * 128], Sb[:],
                         start=False, stop=True)
                return e.matmul(ps1[:, 128:256], ktk[:, ci, :], vth[:, ci, :],
                                start=True, stop=True)
            P.step("tensor", o4)
            def o5(e, ci=ci):
                e.tensor_copy(oth[:, ci, :], ps2[:, 0:128])
                return e.tensor_add(Sf[:], Sf[:], ps1[:, 128:256])
            P.step("vector", o5)
            def o6(e, ci=ci):
                return e.tensor_copy(Sb[:], Sf[:])
            P.step("vector", o6)
        def n8(e, hh=hh):
            P.dma(e, io["o_hbm"][:, hh * 128:(hh + 1) * 128]
                  .rearrange("(k p) f -> p k f", p=128), oth[:])
        P.step("sync", n8)
    exC.close()

    exD = ExitStack()
    sbD = lambda n, s, d: exD.enter_context(nc.sbuf_tensor(n, s, d))
    psD = lambda n, s, d: exD.enter_context(nc.psum_tensor(n, s, d))
    ost = sbD("ost", [128, 4, 1024], BF16)
    ogs = sbD("ogs", [128, 4, 1024], BF16)
    onn = sbD("onn", [128, 1024], F32)
    sgs = sbD("sgs", [128, 1024], F32)
    geff = sbD("geff", [128, 1024], F32)
    stD = sbD("stD", [128, 16], F32)
    sqD = sbD("sqD", [128, 4, 1024], F32)
    pg = [psD(f"pg{i}", [128, 512], F32) for i in range(2)]
    for sup in range(T // 512):
        ts = sup * 512
        def q1(e, ts=ts):
            P.dma(e, ost[:], io["o_hbm"][ts:ts + 512, :]
                  .rearrange("(a p) d -> p a d", p=128))
        P.step("sync", q1)
        def q2(e):
            last = None
            for a in range(4):
                last = e.activation(sqD[:, a, :], ost[:, a, :], AF.Square,
                                    accum_out=stD[:, a:a + 1])
            return last
        P.step("scalar", q2)
        P.step("scalar", lambda e: e.activation(stD[:, 8:12], stD[:, 0:4],
                                                AF.Ln, scale=1.0 / 1024,
                                                bias=g.eps_col[:]))
        P.step("scalar", lambda e: e.activation(stD[:, 8:12], stD[:, 8:12],
                                                AF.Exp, scale=-0.5))
        for a in range(4):
            tg = ts + a * 128
            def q3(e, a=a, tg=tg):
                last = None
                for ob in range(2):
                    last = e.matmul(pg[ob][:], g1_fm[:, tg:tg + 128],
                                    Wg2T[:, ob * 512:(ob + 1) * 512],
                                    start=True, stop=True)
                return last
            P.step("tensor", q3)
            def q4(e, a=a):
                e.activation(sgs[:, 0:512], pg[0][:], AF.Sigmoid)
                return e.activation(sgs[:, 512:1024], pg[1][:], AF.Sigmoid)
            P.step("scalar", q4)
            def q5(e, a=a):
                e.tensor_mul(geff[:, 0:512], pg[0][:], sgs[:, 0:512])
                return e.tensor_mul(geff[:, 512:1024], pg[1][:],
                                    sgs[:, 512:1024])
            P.step("vector", q5)
            def q6(e, a=a):
                return e.tensor_scalar_mul(onn[:], ost[:, a, :],
                                           stD[:, 8 + a:9 + a])
            P.step("vector", q6)
            def q7(e, a=a):
                return e.tensor_mul(onn[:], onn[:], gnw_bc[:])
            P.step("vector", q7)
            def q8(e, a=a):
                return e.tensor_mul(ogs[:, a, :], onn[:], geff[:])
            P.step("vector", q8)
        def q9(e, ts=ts):
            P.dma(e, io["og_hbm"][ts:ts + 512, :]
                  .rearrange("(a p) d -> p a d", p=128), ogs[:])
        P.step("sync", q9)
    exD.close()

    exE = ExitStack()
    sbE = lambda n, s, d: exE.enter_context(nc.sbuf_tensor(n, s, d))
    psE = lambda n, s, d: exE.enter_context(nc.psum_tensor(n, s, d))
    WoT = sbE("WoT", [128, 8, 1024], BF16)
    ogf = sbE("ogf", [128, 8, 512], BF16)
    xrE = sbE("xrE", [128, 1024], F32)
    oaE = sbE("oaE", [128, 1024], F32)
    onE = sbE("onE", [128, 1024], F32)
    stE = sbE("stE", [128, 8], F32)
    sqE = sbE("sqE", [128, 1024], F32)
    po = [psE(f"po{i}", [128, 512], F32) for i in range(2)]
    def r0(e):
        for c in range(8):
            for ob in range(2):
                P.dmaT(e, WoT[:, c, ob * 512:(ob + 1) * 512],
                       io["wo_bf"][ob * 512:(ob + 1) * 512,
                                   c * 128:(c + 1) * 128])
    P.step("sync", r0)
    for sup in range(T // 512):
        ts = sup * 512
        def r1(e, ts=ts):
            for c in range(8):
                P.dmaT(e, ogf[:, c, :], io["og_hbm"][ts:ts + 512,
                                                     c * 128:(c + 1) * 128])
        P.step("sync", r1)
        for tsub in range(4):
            tg = ts + tsub * 128
            def r2(e, tg=tg):
                P.dma(e, xrE[:], io["x"][tg:tg + 128, :])
            P.step("sync", r2)
            def r3(e, tsub=tsub):
                last = None
                for ob in range(2):
                    for c in range(8):
                        last = e.matmul(po[ob][:],
                                        ogf[:, c, tsub * 128:(tsub + 1) * 128],
                                        WoT[:, c, ob * 512:(ob + 1) * 512],
                                        start=(c == 0), stop=(c == 7))
                return last
            P.step("tensor", r3)
            def r4(e):
                e.tensor_add(oaE[:, 0:512], po[0][:], xrE[:, 0:512])
                return e.tensor_add(oaE[:, 512:1024], po[1][:],
                                    xrE[:, 512:1024])
            P.step("vector", r4)
            emit_postln(P, g, oaE, stE, onE, D=D, scr=sqE)
            def r5(e, tg=tg):
                P.dma(e, io["x3"][tg:tg + 128, :], onE[:])
            P.step("sync", r5)
    exE.close()


def emit_l1_ffn(P, nc, ex, g, io, T):
    from contextlib import ExitStack
    D = 1024
    H = 8
    NT = T // 128
    SC = 1.0 / math.sqrt(128.0)

    sb = lambda n, s, d: ex.enter_context(nc.sbuf_tensor(n, s, d))

    gv_all = sb("gv_all", [128, NT, 64], F32)
    go_all = sb("go_all", [128, NT, 64], BF16)
    rs_col = sb("rs_col", [128, 1], F32)

    ex0 = ExitStack()
    ps_pro = ex0.enter_context(nc.psum_tensor("ps_pro2", [128, 512], F32))
    emit_bcast_row(P, nc, g, io["route_scale"], rs_col, ps_pro, 1)
    def spill(e):
        for r in range(0, 1024, 128):
            P.dma(e, io["qw_bf"][r:r + 128, :], io["q_w"][r:r + 128, :])
            P.dma(e, io["kw_bf"][r:r + 128, :], io["k_w"][r:r + 128, :])
        P.dma(e, io["selv_bf"][:, :], io["sel_v_w"][:, :])
        P.dma(e, io["selo_bf"][:, :], io["sel_o_w"][:, :])
        for t0 in range(0, T, 512):
            P.dma(e, io["x_bf"][t0:t0 + 512, :], io["x"][t0:t0 + 512, :])
        for he in range(64):
            P.dma(e, io["ow_bf"][he, :, :], io["o_w"][he, :, :])
    spill_id = P.step("gpsimd", spill)
    ex0.close()

    exB = ExitStack()
    sbB = lambda n, s, d: exB.enter_context(nc.sbuf_tensor(n, s, d))
    psB = lambda n, s, d: exB.enter_context(nc.psum_tensor(n, s, d))
    QwT = sbB("QwT", [128, 8, 1024], BF16)
    KwT = sbB("KwT", [128, 8, 1024], BF16)
    SvT = sbB("SvT", [128, 8, 64], BF16)
    SoT = sbB("SoT", [128, 8, 64], BF16)
    vwh = sbB("vwh", [128, 8, 1024], BF16)
    xf2b = [sbB(f"xf2{i}", [128, 8, 512], BF16) for i in range(2)]
    prj2b = [sbB(f"prj2{i}", [128, 1024], BF16) for i in range(2)]
    probs = sbB("probs2", [128, 64], F32)
    gsc = sbB("gsc", [128, 64], F32)
    m12 = sbB("m12", [128, 32], F32)
    wvt = sbB("wvt", [128, 1024], F32)
    vtt = sbB("vtt", [128, 1024], BF16)
    vtf = sbB("vtf", [128, 128], F32)
    pp = [psB(f"pp2{i}", [128, 512], F32) for i in range(4)]
    psl = [psB(f"psl2{i}", [128, 64], F32) for i in range(2)]
    ids_sel = {}
    def s1w(e):
        for c in range(8):
            for ob in range(2):
                P.dmaT(e, QwT[:, c, ob * 512:(ob + 1) * 512],
                       io["qw_bf"][ob * 512:(ob + 1) * 512, c * 128:(c + 1) * 128])
                P.dmaT(e, KwT[:, c, ob * 512:(ob + 1) * 512],
                       io["kw_bf"][ob * 512:(ob + 1) * 512, c * 128:(c + 1) * 128])
            P.dmaT(e, SvT[:, c, :], io["selv_bf"][:, c * 128:(c + 1) * 128])
            P.dmaT(e, SoT[:, c, :], io["selo_bf"][:, c * 128:(c + 1) * 128])
    P.step("sync", s1w)
    qkd = {}
    for sup in range(T // 512):
        ts = sup * 512
        def s2a(e, ts=ts, sup=sup):
            for c in range(8):
                P.dmaT(e, xf2b[sup % 2][:, c, :],
                       io["x_bf"][ts:ts + 512, c * 128:(c + 1) * 128])
        qkd[("xf", sup)] = P.step(
            "sync", s2a,
            deps=[spill_id if sup == 0 else None,
                  qkd.get(("mm", sup - 2, 7)), ids_sel.get((sup - 2) * 4 + 3)])
        for hh in range(8):
            def s2q(e, hh=hh, sup=sup):
                for c in range(8):
                    e.matmul(pp[2 * (hh % 2)][:],
                             QwT[:, c, hh * 128:(hh + 1) * 128],
                             xf2b[sup % 2][:, c, :], start=(c == 0), stop=(c == 7))
                last = None
                for c in range(8):
                    last = e.matmul(pp[2 * (hh % 2) + 1][:],
                                    KwT[:, c, hh * 128:(hh + 1) * 128],
                                    xf2b[sup % 2][:, c, :], start=(c == 0),
                                    stop=(c == 7))
                return last
            qkd[("mm", sup, hh)] = P.step(
                "tensor", s2q,
                deps=[qkd[("xf", sup)] if hh == 0 else None,
                      qkd.get(("ev", sup, hh - 2)),
                      qkd.get(("ev", sup - 1, 6 + hh)) if hh < 2 else None])
            def s2e(e, hh=hh):
                e.tensor_scalar_mul(prj2b[hh % 2][:, 0:512],
                                    pp[2 * (hh % 2)][:], SC)
                return e.tensor_copy(prj2b[hh % 2][:, 512:1024],
                                     pp[2 * (hh % 2) + 1][:])
            qkd[("ev", sup, hh)] = P.step(
                "vector", s2e, deps=[qkd[("mm", sup, hh)],
                                     qkd.get(("wr", sup, hh - 2)),
                                     qkd.get(("wr", sup - 1, 6 + hh)) if hh < 2 else None])
            def s2w(e, hh=hh, ts=ts):
                P.dma(e, io["qf2_hbm"][hh * 128:(hh + 1) * 128, ts:ts + 512],
                      prj2b[hh % 2][:, 0:512])
                P.dma(e, io["kf2_hbm"][hh * 128:(hh + 1) * 128, ts:ts + 512],
                      prj2b[hh % 2][:, 512:1024])
            qkd[("wr", sup, hh)] = P.step("sync", s2w,
                                          deps=[qkd[("ev", sup, hh)]])
        for tsub in range(4):
            nt = (ts + tsub * 128) // 128
            def s3m(e, tsub=tsub, sup=sup):
                for c in range(8):
                    e.matmul(psl[0][:], xf2b[sup % 2][:, c, tsub * 128:(tsub + 1) * 128],
                             SvT[:, c, :], start=(c == 0), stop=(c == 7))
                last = None
                for c in range(8):
                    last = e.matmul(psl[1][:], xf2b[sup % 2][:, c, tsub * 128:(tsub + 1) * 128],
                                    SoT[:, c, :], start=(c == 0), stop=(c == 7))
                return last
            P.step("tensor", s3m)
            for which, psi in (("v", 0), ("o", 1)):
                gall = gv_all if which == "v" else go_all
                def s4a(e, psi=psi):
                    return e.activation(probs[:], psl[psi][:], AF.Sigmoid)
                P.step("scalar", s4a)
                def s4b(e):
                    p3 = probs[:].rearrange("q (h e2) -> q h e2", e2=8)
                    return e.tensor_reduce(m12[:, 0:8], p3, AX.X, ALU.max)
                P.step("vector", s4b)
                def s4c(e):
                    m1b = m12[:, 0:8].rearrange("q (h o) -> q h o", o=1) \
                        .broadcast_to([128, 8, 8])
                    p3 = probs[:].rearrange("q (h e2) -> q h e2", e2=8)
                    return e.tensor_tensor(gsc[:].rearrange("q (h e2) -> q h e2", e2=8),
                                           p3, m1b, ALU.is_ge)
                P.step("vector", s4c)
                def s4d(e):
                    return e.scalar_tensor_tensor(gsc[:], gsc[:], -1e30,
                                                  probs[:], ALU.mult, ALU.add)
                P.step("vector", s4d)
                def s4e(e):
                    g3 = gsc[:].rearrange("q (h e2) -> q h e2", e2=8)
                    return e.tensor_reduce(m12[:, 8:16], g3, AX.X, ALU.max)
                P.step("vector", s4e)
                def s4f(e):
                    m2b = m12[:, 8:16].rearrange("q (h o) -> q h o", o=1) \
                        .broadcast_to([128, 8, 8])
                    p3 = probs[:].rearrange("q (h e2) -> q h e2", e2=8)
                    return e.tensor_tensor(gsc[:].rearrange("q (h e2) -> q h e2", e2=8),
                                           p3, m2b, ALU.is_ge)
                P.step("vector", s4f)
                def s4g(e):
                    return e.tensor_add(m12[:, 16:24], m12[:, 0:8],
                                        m12[:, 8:16])
                P.step("vector", s4g)
                P.step("vector", lambda e: e.tensor_scalar_max(
                    m12[:, 16:24], m12[:, 16:24], 1e-9))
                P.step("vector", lambda e: e.reciprocal(m12[:, 24:32],
                                                        m12[:, 16:24]))
                def s4h(e):
                    return e.tensor_scalar_mul(m12[:, 24:32], m12[:, 24:32],
                                               rs_col[:])
                P.step("vector", s4h)
                P.step("vector", lambda e: e.tensor_mul(gsc[:], gsc[:],
                                                        probs[:]))
                def s4i(e, nt=nt, gall=gall):
                    fb = m12[:, 24:32].rearrange("q (h o) -> q h o", o=1) \
                        .broadcast_to([128, 8, 8])
                    return e.tensor_tensor(
                        gall[:, nt, :].rearrange("q (h e2) -> q h e2", e2=8),
                        gsc[:].rearrange("q (h e2) -> q h e2", e2=8), fb,
                        ALU.mult)
                sid4 = P.step("vector", s4i)
                if which == "v":
                    ids_sel[nt] = sid4
    xfv = [sbB(f"xfv{i}", [128, 8, 512], BF16) for i in range(2)]
    vwh2 = [sbB(f"vwh2{i}", [128, 8, 1024], BF16) for i in range(2)]
    ids = {}
    NSUP = T // 512
    for hh in range(8):
        def sv0(e, hh=hh):
            for dc in range(8):
                P.dma(e, vwh2[hh % 2][:, dc, :].rearrange("p (e p2) -> p e p2", p2=128),
                      io["v_w"][hh * 8:(hh + 1) * 8, dc * 128:(dc + 1) * 128, :]
                      .rearrange("e dp p2 -> dp e p2"))
        ids[("vw", hh)] = P.step("gpsimd", sv0,
                                 deps=[ids.get(("mm", hh - 2, NSUP - 1, 3))])
        for sup in range(NSUP):
            it = hh * NSUP + sup
            ts = sup * 512
            def sv1(e, ts=ts, it=it):
                for c in range(8):
                    P.dmaT(e, xfv[it % 2][:, c, :],
                           io["x_bf"][ts:ts + 512, c * 128:(c + 1) * 128])
            pit = it - 2
            ids[("ld", it)] = P.step("sync", sv1,
                                     deps=[spill_id,
                                           ids.get(("mm", pit // NSUP, pit % NSUP, 3))
                                           if pit >= 0 else None])
            for tsub in range(4):
                nt = (ts + tsub * 128) // 128
                def s5m(e, hh=hh, tsub=tsub, it=it):
                    last = None
                    for ob in range(2):
                        for c in range(8):
                            last = e.matmul(pp[ob][:],
                                            xfv[it % 2][:, c, tsub * 128:(tsub + 1) * 128],
                                            vwh2[hh % 2][:, c, ob * 512:(ob + 1) * 512],
                                            start=(c == 0), stop=(c == 7))
                    return last
                ids[("mm", hh, sup, tsub)] = P.step(
                    "tensor", s5m,
                    deps=[ids[("ld", it)], ids[("vw", hh)],
                          ids_sel[NT - 1] if (hh == 0 and sup == 0 and tsub == 0) else None,
                          ids.get(("gt", hh, sup, tsub - 1)),
                          ids.get(("gt", hh, sup - 1, 3)) if tsub == 0 else None])
                def s5g(e, hh=hh, nt=nt):
                    gb = gv_all[:, nt, hh * 8:(hh + 1) * 8] \
                        .rearrange("q (e2 o) -> q e2 o", o=1) \
                        .broadcast_to([128, 8, 128])
                    e.tensor_tensor(wvt[:, 0:512].rearrange("q (e2 p2) -> q e2 p2", p2=128),
                                    pp[0][:].rearrange("q (e2 p2) -> q e2 p2", p2=128),
                                    gb[:, 0:4, :], ALU.mult)
                    return e.tensor_tensor(
                        wvt[:, 512:1024].rearrange("q (e2 p2) -> q e2 p2", p2=128),
                        pp[1][:].rearrange("q (e2 p2) -> q e2 p2", p2=128),
                        gb[:, 4:8, :], ALU.mult)
                ids[("gt", hh, sup, tsub)] = P.step(
                    "vector", s5g, deps=[ids[("mm", hh, sup, tsub)],
                                         ids_sel[nt],
                                         ids.get(("wr", hh, sup, tsub - 1)),
                                         ids.get(("wr", hh, sup - 1, 3)) if tsub == 0 else None])
                def s5r(e):
                    wv3 = wvt[:].rearrange("q (e2 p2) -> q p2 e2", p2=128)
                    return e.tensor_reduce(vtf[:], wv3, AX.X, ALU.add)
                P.step("vector", s5r)
                P.step("vector", lambda e: e.tensor_copy(vtt[:, 0:128], vtf[:]))
                def s5w(e, hh=hh, ts=ts, tsub=tsub):
                    P.dma(e, io["vtok2_hbm"][ts + tsub * 128:ts + (tsub + 1) * 128,
                                             hh * 128:(hh + 1) * 128],
                          vtt[:, 0:128])
                ids[("wr", hh, sup, tsub)] = P.step(
                    "gpsimd", s5w, deps=[len(P.steps) - 1])
    exB.close()

    exC = ExitStack()
    sbC = lambda n, s, d: exC.enter_context(nc.sbuf_tensor(n, s, d))
    psC = lambda n, s, d: exC.enter_context(nc.psum_tensor(n, s, d))
    kfh = [sbC(f"kfh2{i}", [128, T], BF16) for i in range(2)]
    qfh = [sbC(f"qfh2{i}", [128, T], BF16) for i in range(2)]
    vth = [sbC(f"vth2{i}", [128, NT, 128], BF16) for i in range(2)]
    expT = sbC("expT2", [128, NT, 512], BF16)
    srow = sbC("srow2", [1, 512], F32)
    ctxs = sbC("ctxs", [128, 512], F32)
    ctxb = sbC("ctxb", [128, 512], BF16)
    psc = [psC(f"psc2{i}", [128, 1024], F32) for i in range(2)]
    pctx = psC("pctx2", [128, 512], F32)
    psum_s = psC("psum_s2", [1, 512], F32)
    pbc2 = psC("pbc2", [128, 512], F32)
    jd = {}
    NQB = T // 512
    for hh in range(H):
        def t1(e, hh=hh):
            P.dma(e, kfh[hh % 2][:], io["kf2_hbm"][hh * 128:(hh + 1) * 128, :])
            P.dma(e, qfh[hh % 2][:], io["qf2_hbm"][hh * 128:(hh + 1) * 128, :])
            P.dma(e, vth[hh % 2][:], io["vtok2_hbm"][:, hh * 128:(hh + 1) * 128]
                  .rearrange("(k p) f -> p k f", p=128))
        jd[("ld", hh)] = P.step("sync", t1,
                                deps=[ids[("wr", hh, NSUP - 1, 3)],
                                      ids[("wr", 7, NSUP - 1, 3)] if hh == 0 else None,
                                      jd.get(("ctx", hh - 2, NQB - 1))])
        for qb in range(NQB):
            qs = qb * 512
            NG = (NT + 1) // 2
            for gg in range(NG):
                kk = [k for k in (2 * gg, 2 * gg + 1) if k < NT]
                def t2(e, kk=kk, qs=qs, hh=hh, gg=gg):
                    last = None
                    for j, k in enumerate(kk):
                        last = e.matmul(psc[gg % 2][:, j * 512:(j + 1) * 512],
                                        kfh[hh % 2][:, k * 128:(k + 1) * 128],
                                        qfh[hh % 2][:, qs:qs + 512],
                                        start=True, stop=True)
                    return last
                jd[("mm", hh, qb, gg)] = P.step(
                    "tensor", t2,
                    deps=[jd[("ld", hh)] if gg == 0 and qb == 0 else None,
                          jd.get(("ex", hh, qb, gg - 2)),
                          jd.get(("ex", hh, qb - 1, NG - 2 + gg)) if gg < 2 else None])
                def t3(e, kk=kk, gg=gg):
                    last = None
                    for j, k in enumerate(kk):
                        last = e.activation(expT[:, k, :],
                                            psc[gg % 2][:, j * 512:(j + 1) * 512],
                                            AF.Exp)
                    return last
                jd[("ex", hh, qb, gg)] = P.step(
                    "scalar", t3,
                    deps=[jd[("mm", hh, qb, gg)],
                          jd.get(("ctx", hh, qb - 1))])
            def t4(e, qs=qs, hh=hh):
                for k in range(NT):
                    e.matmul(pctx[:], vth[hh % 2][:, k, :], expT[:, k, :],
                             start=(k == 0), stop=(k == NT - 1))
                last = None
                for k in range(NT):
                    last = e.matmul(psum_s[:], g.ones_colb[:], expT[:, k, :],
                                    start=(k == 0), stop=(k == NT - 1))
                return last
            jd[("ctxmm", hh, qb)] = P.step(
                "tensor", t4, deps=[jd[("ex", hh, qb, NG - 1)],
                                    jd.get(("cx7", hh, qb - 1))])
            jd[("sr", hh, qb)] = P.step(
                "vector", lambda e: e.reciprocal(srow[:], psum_s[:]),
                deps=[jd[("ctxmm", hh, qb)]])
            def t5(e):
                return e.matmul(pbc2[:], g.ones_row[:], srow[:],
                                start=True, stop=True)
            P.step("tensor", t5, deps=[jd[("sr", hh, qb)]])
            def t6(e):
                return e.tensor_copy(ctxs[:], pctx[:])
            P.step("vector", t6)
            def t7(e):
                return e.tensor_mul(ctxb[:], ctxs[:], pbc2[:])
            prev_ctx = jd.get(("ctx", hh, qb - 1)) if qb > 0 else \
                jd.get(("ctx", hh - 1, NQB - 1))
            jd[("cx7", hh, qb)] = P.step("vector", t7,
                                         deps=[len(P.steps) - 1, prev_ctx])
            def t8(e, hh=hh, qs=qs):
                P.dma(e, io["ctx_hbm"][hh * 128:(hh + 1) * 128, qs:qs + 512],
                      ctxb[:])
            jd[("ctx", hh, qb)] = P.step("sync", t8)
    exC.close()

    exD = ExitStack()
    sbD = lambda n, s, d: exD.enter_context(nc.sbuf_tensor(n, s, d))
    psD = lambda n, s, d: exD.enter_context(nc.psum_tensor(n, s, d))
    goT = sbD("goT", [64, T], BF16)
    Estr = sbD("Estr", [64, 8192], BF16)
    ctxsup = sbD("ctxsup", [128, 8, 512], BF16)
    wctx = sbD("wctx", [128, 64, 512], BF16)
    owg = [sbD(f"owg{i}", [128, 8, 1024], BF16) for i in range(2)]
    xrD = sbD("xrD", [128, 4, 1024], F32)
    oaD = sbD("oaD", [128, 1024], F32)
    onD = sbD("onD", [128, 1024], F32)
    stD2 = sbD("stD2", [128, 8], F32)
    sqD2 = sbD("sqD2", [128, 1024], F32)
    kd = {}
    def u0(e):
        return e.iota(Estr[:], [[-1, 64], [0, 128]], base=0,
                      channel_multiplier=1,
                      allow_small_or_imprecise_dtypes=True)
    P.step("gpsimd", u0)
    P.step("vector", lambda e: e.tensor_single_scalar(Estr[:], Estr[:], 0.0,
                                                      ALU.is_equal))
    u2_last = None
    with nc.psum_tensor("pgT3", [64, 128], BF16) as pgT:
        for n in range(NT):
            def u1(e, n=n):
                return e.transpose(pgT[:], go_all[:, n, :], g.identb2[:])
            P.step("tensor", u1)
            def u2(e, n=n):
                return e.tensor_copy(goT[:, n * 128:(n + 1) * 128], pgT[:])
            u2_last = P.step("vector", u2)
    for tb in range(T // 512):
        ts = tb * 512
        def u3(e, ts=ts):
            for hh2 in range(8):
                P.dma(e, ctxsup[:, hh2, :],
                      io["ctx_hbm"][hh2 * 128:(hh2 + 1) * 128, ts:ts + 512])
            P.dma(e, xrD[:], io["x"][ts:ts + 512, :]
                  .rearrange("(a p) d -> p a d", p=128))
        kd[("ld", tb)] = P.step("sync", u3,
                                deps=[jd[("ctx", H - 1, NQB - 1)]
                                      if tb == 0 else None,
                                      kd.get(("ln", tb - 1, 3))])
        with nc.psum_tensor(f"pwc4_{tb}", [128, 2, 2, 512], F32) as pwc4:
            NHG = 32
            for hg in range(NHG):
                def u5(e, hg=hg, ts=ts):
                    last = None
                    for j in range(2):
                        he = hg * 2 + j
                        last = e.matmul(pwc4[:, hg % 2, j, :],
                                        Estr[:, he * 128:(he + 1) * 128],
                                        goT[:, ts:ts + 512],
                                        start=True, stop=True)
                    return last
                kd[("bc", tb, hg)] = P.step(
                    "tensor", u5,
                    deps=[kd[("ld", tb)] if hg == 0 else None,
                          u2_last if (tb == 0 and hg == 0) else None,
                          kd.get(("wc", tb, hg - 2))])
                def u6(e, hg=hg):
                    last = None
                    for j in range(2):
                        he = hg * 2 + j
                        last = e.tensor_tensor(
                            wctx[:, he:he + 1, :],
                            ctxsup[:, he // 8:he // 8 + 1, :],
                            pwc4[:, hg % 2, j:j + 1, :], ALU.mult)
                    return last
                kd[("wc", tb, hg)] = P.step(
                    "vector", u6, deps=[kd[("bc", tb, hg)],
                                        kd.get(("omm", tb - 1, 7)) if hg < 2 else None])
        with nc.psum_tensor(f"pO_{tb}", [128, 8, 512], F32) as pO:
            for ho in range(8):
                def u8(e, ho=ho):
                    for dc2 in range(8):
                        P.dma(e, owg[ho % 2][:, dc2, :],
                              io["ow_bf"][ho * 8 + dc2, :, :])
                kd[("owl", tb, ho)] = P.step(
                    "gpsimd", u8, deps=[kd.get(("omm", tb, ho - 2)),
                                        kd.get(("omm", tb - 1, 7)) if ho < 2 else None])
                def u9(e, ho=ho):
                    last = None
                    for j in range(8):
                        he = ho * 8 + j
                        for tsub in range(4):
                            for dh in range(2):
                                last = e.matmul(
                                    pO[:, tsub * 2 + dh, :],
                                    wctx[:, he, tsub * 128:(tsub + 1) * 128],
                                    owg[ho % 2][:, j, dh * 512:(dh + 1) * 512],
                                    start=(he == 0), stop=(he == 63))
                    return last
                kd[("omm", tb, ho)] = P.step(
                    "tensor", u9, deps=[kd[("owl", tb, ho)],
                                        kd[("wc", tb, 31)] if ho == 0 else None])
            for tsub in range(4):
                tg = ts + tsub * 128
                def u10(e, tsub=tsub):
                    e.tensor_add(oaD[:, 0:512], pO[:, tsub * 2, :],
                                 xrD[:, tsub, 0:512])
                    return e.tensor_add(oaD[:, 512:1024], pO[:, tsub * 2 + 1, :],
                                        xrD[:, tsub, 512:1024])
                P.step("vector", u10, deps=[kd[("omm", tb, 7)],
                                            len(P.steps) - 1 if tsub > 0 else kd[("omm", tb, 7)]])
                emit_postln(P, g, oaD, stD2, onD, D=D, scr=sqD2)
                def u11(e, tg=tg):
                    P.dma(e, io["x4"][tg:tg + 128, :], onD[:])
                kd[("ln", tb, tsub)] = P.step("sync", u11)
    exD.close()


from contextlib import ExitStack as _ExitStack

_T, _D = 8192, 1024


def _decl(nc, names, T=_T):
    io = {}
    io["x"] = nc.dram_tensor("x", [T, 1024], F32, kind="ExternalInput").ap()
    for nm, shp in names["in"]:
        io[nm] = nc.dram_tensor(nm, shp, F32, kind="ExternalInput").ap()
    io[names["out"]] = nc.dram_tensor(names["out"], [T, 1024], F32,
                                      kind="ExternalOutput").ap()
    for nm, shp in names["scratch"]:
        io[nm] = nc.dram_tensor(nm, shp, BF16, kind="Internal").ap()
    return io


_L0A_NAMES = {
    "in": [("ln_g", [1024]), ("ln_b", [1024]), ("W_hidden", [1024, 4096]),
           ("b_hidden", [4096]), ("W_qk", [1024, 128]), ("b_qk", [128]),
           ("os_gamma", [2, 128]), ("os_beta", [2, 128]),
           ("W_out", [2048, 1024]), ("b_out", [1024])],
    "out": "x1",
    "scratch": [("nrm_hbm", [_T, 1024]), ("hid_hbm", [_T, 4096])],
}
_L0F_NAMES = {
    "in": [("norm_gamma", [1024]), ("slot_norm_gamma", [1024]),
           ("slot_embeds", [8, 512, 1024]), ("w1", [8, 1024, 4096]),
           ("b1", [8, 4096]), ("w2", [8, 4096, 1024]), ("b2", [8, 1024])],
    "out": "x2",
    "scratch": [("xn_hbm", [_T, 1024]), ("sn_hbm", [4096, 1024]),
                ("lg_hbm", [_T, 4096]), ("lglo_hbm", [_T, 4096]),
                ("ce_hbm", [_T, 4096]), ("disp_hbm", [4096, _T]),
                ("slots_hbm", [1024, 4096])],
}
_L1A_NAMES = {
    "in": [("Wq", [1024, 1024]), ("Wk", [1024, 1024]), ("Wv", [1024, 1024]),
           ("Wg1", [128, 1024]), ("Wg2", [1024, 128]),
           ("g_norm_weight", [1024]), ("Wo", [1024, 1024])],
    "out": "x3",
    "scratch": [("x_bf", [_T, 1024]), ("wq_bf", [1024, 1024]),
                ("wk_bf", [1024, 1024]), ("wv_bf", [1024, 1024]),
                ("wo_bf", [1024, 1024]), ("wg1_bf", [128, 1024]),
                ("wg2_bf", [1024, 128]), ("qf_hbm", [1024, _T]),
                ("kf_hbm", [1024, _T]), ("vtok_hbm", [_T, 1024]),
                ("o_hbm", [_T, 1024]), ("og_hbm", [_T, 1024])],
}
_L1F_NAMES = {
    "in": [("q_w", [1024, 1024]), ("k_w", [1024, 1024]),
           ("v_w", [64, 1024, 128]), ("o_w", [64, 128, 1024]),
           ("sel_v_w", [64, 1024]), ("sel_o_w", [64, 1024]),
           ("route_scale", [1])],
    "out": "x4",
    "scratch": [("x_bf", [_T, 1024]), ("qw_bf", [1024, 1024]),
                ("kw_bf", [1024, 1024]), ("selv_bf", [64, 1024]),
                ("selo_bf", [64, 1024]), ("ow_bf", [64, 128, 1024]),
                ("qf2_hbm", [1024, _T]), ("kf2_hbm", [1024, _T]),
                ("vtok2_hbm", [_T, 1024]), ("ctx_hbm", [1024, _T])],
}


def _build(names, emit):
    nc = bass.Bass("TRN2", target_bir_lowering=False, debug=False)
    io = _decl(nc, names)
    with _ExitStack() as ex:
        sems = {}
        for s in ["cc"] + ["dm_" + e for e in ENGINES] + ["st_" + e for e in ENGINES]:
            sems[s] = ex.enter_context(nc.semaphore(s))
        blk = ex.enter_context(nc.Block())
        P = Prog(nc)
        g = alloc_globals(P, nc, ex)
        g.row_scr = ex.enter_context(nc.sbuf_tensor("row_scr", [1, 1024], F32))
        emit(P, nc, ex, g, io, _T)
        P.emit(blk, sems)
    return nc


_CACHE = {}

def _get_nc(key):
    if key not in _CACHE:
        if key == "l0a":
            _CACHE[key] = _build(_L0A_NAMES, emit_l0_attn)
        elif key == "l0f":
            _CACHE[key] = _build(_L0F_NAMES, emit_l0_ffn)
        elif key == "l1a":
            _CACHE[key] = _build(_L1A_NAMES, emit_l1_attn)
        elif key == "l1f":
            _CACHE[key] = _build(_L1F_NAMES, emit_l1_ffn)
    return _CACHE[key]


def _launch(key, names, xs, winputs):
    nc = _get_nc(key)
    in_maps = []
    for c in range(8):
        m = {"x": np.ascontiguousarray(xs[c % 4])}
        for nm, _ in names["in"]:
            m[nm] = winputs[nm]
        in_maps.append(m)
    res = run_bass_kernel_spmd(nc, in_maps, core_ids=list(range(8)))
    return [res.results[c][names["out"]] for c in range(4)]


def kernel(**inputs):
    xs = [np.ascontiguousarray(inputs["L0_attn_x"][b]) for b in range(4)]
    w = lambda pre, names: {nm: np.ascontiguousarray(inputs[pre + nm])
                            for nm, _ in names["in"]}
    xs = _launch("l0a", _L0A_NAMES, xs, w("L0_attn_", _L0A_NAMES))
    xs = _launch("l0f", _L0F_NAMES, xs, w("L0_ffn_", _L0F_NAMES))
    xs = _launch("l1a", _L1A_NAMES, xs, w("L1_attn_", _L1A_NAMES))
    xs = _launch("l1f", _L1F_NAMES, xs, w("L1_ffn_", _L1F_NAMES))
    return np.stack(xs, axis=0).astype(np.float32)



def _decl_block(nc, names, pre, io_x_ap, out_kind_final):
    io = {"x": io_x_ap}
    for nm, shp in names["in"]:
        io[nm] = nc.dram_tensor(pre + nm, shp, F32, kind="ExternalInput").ap()
    if out_kind_final:
        io[names["out"]] = nc.dram_tensor("out", [_T, 1024], F32,
                                          kind="ExternalOutput").ap()
    else:
        io[names["out"]] = nc.dram_tensor(pre + names["out"], [_T, 1024], F32,
                                          kind="Internal").ap()
    for nm, shp in names["scratch"]:
        io[nm] = nc.dram_tensor(pre + nm, shp, BF16, kind="Internal").ap()
    return io


def _build_fused():
    nc = bass.Bass("TRN2", target_bir_lowering=False, debug=False)
    x0 = nc.dram_tensor("x", [_T, 1024], F32, kind="ExternalInput").ap()
    io_a = _decl_block(nc, _L0A_NAMES, "A_", x0, False)
    io_b = _decl_block(nc, _L0F_NAMES, "B_", io_a["x1"], False)
    io_c = _decl_block(nc, _L1A_NAMES, "C_", io_b["x2"], False)
    io_d = _decl_block(nc, _L1F_NAMES, "D_", io_c["x3"], True)
    with _ExitStack() as ex:
        sems = {}
        for s in ["cc"] + ["dm_" + e for e in ENGINES] + ["st_" + e for e in ENGINES]:
            sems[s] = ex.enter_context(nc.semaphore(s))
        blk = ex.enter_context(nc.Block())
        P = Prog(nc)
        g = alloc_globals(P, nc, ex)
        g.row_scr = ex.enter_context(nc.sbuf_tensor("row_scr", [1, 1024], F32))
        for emit, io in ((emit_l0_attn, io_a), (emit_l0_ffn, io_b),
                         (emit_l1_attn, io_c), (emit_l1_ffn, io_d)):
            exb = _ExitStack()
            emit(P, nc, exb, g, io, _T)
            exb.close()
        P.emit(blk, sems)
    return nc


def kernel_fused(**inputs):
    if "fused" not in _CACHE:
        _CACHE["fused"] = _build_fused()
    nc = _CACHE["fused"]
    in_maps = []
    for c in range(8):
        m = {"x": np.ascontiguousarray(inputs["L0_attn_x"][c % 4])}
        for pre, names in (("A_", _L0A_NAMES), ("B_", _L0F_NAMES),
                           ("C_", _L1A_NAMES), ("D_", _L1F_NAMES)):
            kpre = {"A_": "L0_attn_", "B_": "L0_ffn_", "C_": "L1_attn_",
                    "D_": "L1_ffn_"}[pre]
            for nm, _ in names["in"]:
                m[pre + nm] = np.ascontiguousarray(inputs[kpre + nm])
        in_maps.append(m)
    res = run_bass_kernel_spmd(nc, in_maps, core_ids=list(range(8)))
    return np.stack([res.results[c]["out"] for c in range(4)], axis=0) \
        .astype(np.float32)


kernel = kernel_fused


# version 05
# speedup vs baseline: 1.7386x; 1.0068x over previous
import math
import numpy as np
import ml_dtypes
import concourse.bass as bass
import concourse.mybir as mybir
from concourse.bass_utils import run_bass_kernel_spmd

F32 = mybir.dt.float32
BF16 = mybir.dt.bfloat16
AF = mybir.ActivationFunctionType
ALU = mybir.AluOpType
AX = mybir.AxisListType

ENGINES = ("sync", "gpsimd", "tensor", "vector", "scalar")


class Prog:

    def __init__(self, nc):
        self.nc = nc
        self.steps = []
        self.meta = []
        self.eng_counts = {e: 0 for e in ENGINES}
        self.dmcnt = {e: 0 for e in ENGINES}
        self.cccnt = 0
        self.cur_eng = None
        self.sems = {}

    def step(self, eng, fn, deps=None):
        sid = len(self.steps)
        if deps is None:
            deps = [sid - 1] if sid > 0 else []
        else:
            deps = [d for d in deps if d is not None]
        self.steps.append((eng, fn, deps))
        self.meta.append((eng, self.eng_counts[eng]))
        self.eng_counts[eng] += 1
        return sid

    def dma(self, e, out, in_, **kw):
        n = self.cur_eng
        self.dmcnt[n] += 1
        return e.dma_start(out=out, in_=in_, **kw).then_inc(self.sems["dm_" + n], 16)

    def dmaT(self, e, out, in_):
        n = self.cur_eng
        self.dmcnt[n] += 1
        return e.dma_start_transpose(out, in_).then_inc(self.sems["dm_" + n], 16)

    def cc(self, e, kind, op, groups, ins, outs):
        self.cccnt += 1
        return e.collective_compute(kind, op, groups,
                                    [a.opt() for a in ins],
                                    [a.opt() for a in outs]).then_inc(self.sems["cc"])

    def emit(self, blk, sems):
        self.sems = sems
        def make_section(name):
            def sec(eng):
                self.cur_eng = name
                my_sem = sems["st_" + name]
                for sid, (ename, fn, deps) in enumerate(self.steps):
                    if ename != name:
                        continue
                    for d in deps:
                        deng, didx = self.meta[d]
                        eng.wait_ge(sems["st_" + deng], didx + 1)
                    dm_before = self.dmcnt[name]
                    cc_before = self.cccnt
                    last = fn(eng)
                    need_tail = False
                    if self.dmcnt[name] != dm_before:
                        eng.wait_ge(sems["dm_" + name], 16 * self.dmcnt[name])
                        need_tail = True
                    if self.cccnt != cc_before:
                        eng.wait_ge(sems["cc"], self.cccnt)
                        need_tail = True
                    if need_tail:
                        eng.nop().then_inc(my_sem, 1)
                    else:
                        assert last is not None, \
                            f"step {sid} ({name}) returned None with no dma/cc"
                        last.then_inc(my_sem, 1)
            return sec
        blk.sync(make_section("sync"))
        blk.gpsimd(make_section("gpsimd"))
        blk.tensor(make_section("tensor"))
        blk.vector(make_section("vector"))
        blk.scalar(make_section("scalar"))
        return len(self.steps)


def ceil_div(a, b):
    return (a + b - 1) // b



def load_vec_as_cols(P, sync_eng, sb_cols, dram_vec, n):
    nch = n // 128
    P.dma(sync_eng, sb_cols[:, 0:nch],
          dram_vec.rearrange("(c p) -> p c", p=128),
          allow_slow_non_contiguous=True)


class G:
    pass


def alloc_globals(P, nc, ex):
    g = G()
    g.ones_row = ex.enter_context(nc.sbuf_tensor("g_ones", [1, 128], F32))
    g.eps_col = ex.enter_context(nc.sbuf_tensor("g_eps", [128, 1], F32))
    g.negk0_col = ex.enter_context(nc.sbuf_tensor("g_negk0", [128, 1], F32))
    g.ones_colb = ex.enter_context(nc.sbuf_tensor("g_onescb", [128, 1], BF16))
    g.identb2 = ex.enter_context(nc.sbuf_tensor("g_identb2", [128, 128], BF16))
    def init(e):
        e.memset(g.ones_row[:], 1.0)
        e.memset(g.negk0_col[:], -8.0)
        e.memset(g.ones_colb[:], 1.0)
        return e.memset(g.eps_col[:], 1e-5)
    P.step("vector", init)
    def initg(e):
        return e.iota(g.identb2[:], [[1, 128]], base=0, channel_multiplier=-1,
                      allow_small_or_imprecise_dtypes=True)
    P.step("gpsimd", initg)
    def initg2(e):
        return e.tensor_single_scalar(g.identb2[:], g.identb2[:], 0.0,
                                      ALU.is_equal)
    P.step("vector", initg2)
    return g


def emit_postln(P, g, oacc, stats, out_sb, D=1024, scr=None):
    ssum = stats[:, 0:1]; negm = stats[:, 1:2]; ssq = stats[:, 2:3]
    lnv = stats[:, 3:4]; rstd = stats[:, 4:5]
    P.step("vector", lambda e: e.reduce_sum(ssum, oacc[:], axis=AX.X))
    P.step("vector", lambda e: e.tensor_scalar_mul(negm, ssum, -1.0 / D))
    def s2(e):
        return e.activation(scr[:, 0:D], oacc[:], AF.Square, bias=negm,
                            accum_out=ssq)
    P.step("scalar", s2)
    P.step("scalar", lambda e: e.activation(lnv, ssq, AF.Ln, scale=1.0 / D,
                                            bias=g.eps_col[:]))
    P.step("scalar", lambda e: e.activation(rstd, lnv, AF.Exp, scale=-0.5))
    def s4(e):
        return e.tensor_scalar(out_sb[:], oacc[:], negm, rstd,
                               ALU.add, ALU.mult)
    P.step("vector", s4)


def emit_l0_attn(P, nc, ex, g, io, T):
    D = 1024
    NT = T // 128
    NSUP = T // 1024
    NQB = T // 512

    from contextlib import ExitStack
    exB = ExitStack()
    sb = lambda n, s, d: ex.enter_context(nc.sbuf_tensor(n, s, d))
    ps = lambda n, s, d: ex.enter_context(nc.psum_tensor(n, s, d))
    sbB = lambda n, s, d: exB.enter_context(nc.sbuf_tensor(n, s, d))
    psB = lambda n, s, d: exB.enter_context(nc.psum_tensor(n, s, d))

    qhatT = sb("qhatT", [128, T], BF16)
    kT = sb("kT", [128, T], BF16)
    oscol = sb("oscol", [128, 4], F32)
    bout_row = sb("bout_row", [1, 1024], F32)
    sqscr = sb("sqscr", [128, 1024], F32)
    Whg = sbB("Whg", [128, 8, 4096], BF16)
    Wqkg = sbB("Wqkg", [128, 8, 128], BF16)
    beff_h = sbB("beff_h", [1, 4096], F32)
    beff_qk = sbB("beff_qk", [1, 128], F32)
    lng_c = sbB("lng_c", [128, 8], F32)
    lnb_c = sbB("lnb_c", [128, 8], BF16)
    bh_row = sbB("bh_row", [1, 4096], F32)
    bqk_row = sbB("bqk_row", [1, 128], F32)
    riota = sbB("riota", [1, 128], F32)
    pr0 = psB("pr0", [1, 512], F32)
    pr1 = psB("pr1", [1, 512], F32)

    def p0_sync(e):
        load_vec_as_cols(P, e, lng_c, io["ln_g"], D)
        P.dma(e, oscol[:, 0:1], io["os_gamma"][0:1, :].rearrange("a p -> p a"),
              allow_slow_non_contiguous=True)
        P.dma(e, oscol[:, 1:2], io["os_gamma"][1:2, :].rearrange("a p -> p a"),
              allow_slow_non_contiguous=True)
        P.dma(e, oscol[:, 2:3], io["os_beta"][0:1, :].rearrange("a p -> p a"),
              allow_slow_non_contiguous=True)
        P.dma(e, oscol[:, 3:4], io["os_beta"][1:2, :].rearrange("a p -> p a"),
              allow_slow_non_contiguous=True)
        P.dma(e, bh_row[:], io["b_hidden"].rearrange("(a f) -> a f", a=1))
        P.dma(e, bqk_row[:], io["b_qk"].rearrange("(a f) -> a f", a=1))
        P.dma(e, bout_row[:], io["b_out"].rearrange("(a f) -> a f", a=1))
    P.step("sync", p0_sync)

    def p0_gps(e):
        load_vec_as_cols(P, e, lnb_c, io["ln_b"], D)
        for c in range(8):
            P.dma(e, Whg[:, c, :], io["W_hidden"][c * 128:(c + 1) * 128, :])
            P.dma(e, Wqkg[:, c, :], io["W_qk"][c * 128:(c + 1) * 128, :])
        return None
    P.step("gpsimd", p0_gps)


    for fb in range(8):
        def tmm(e, fb=fb):
            p = pr0 if fb % 2 == 0 else pr1
            last = None
            for c in range(8):
                last = e.matmul(p[:], lnb_c[:, c:c + 1],
                                Whg[:, c, fb * 512:(fb + 1) * 512],
                                start=(c == 0), stop=(c == 7))
            return last
        def tev(e, fb=fb):
            p = pr0 if fb % 2 == 0 else pr1
            return e.tensor_add(beff_h[:, fb * 512:(fb + 1) * 512], p[:],
                                bh_row[:, fb * 512:(fb + 1) * 512])
        P.step("tensor", tmm)
        P.step("vector", tev)
    def tmm_qk(e):
        last = None
        for c in range(8):
            last = e.matmul(pr0[:, 0:128], lnb_c[:, c:c + 1], Wqkg[:, c, :],
                            start=(c == 0), stop=(c == 7))
        return last
    P.step("tensor", tmm_qk)
    def tev_qk(e):
        return e.tensor_add(beff_qk[:], pr0[:, 0:128], bqk_row[:])
    P.step("vector", tev_qk)

    def wscale(e):
        last = None
        for c in range(8):
            e.tensor_scalar_mul(Whg[:, c, :], Whg[:, c, :], lng_c[:, c:c + 1])
            last = e.tensor_scalar_mul(Wqkg[:, c, :], Wqkg[:, c, :],
                                       lng_c[:, c:c + 1])
        return last
    P.step("vector", wscale)

    exA = ExitStack()
    sbA = lambda n, s, d: exA.enter_context(nc.sbuf_tensor(n, s, d))
    NA = 4
    sxA = sbA("sxA", [128, NA, 1024], F32)
    snrmA = sbA("snrmA", [128, NA, 1024], BF16)
    statA = sbA("statA", [128, 32], F32)
    sq8 = sbA("sq8", [128, NA, 1024], F32)
    for sup in range(T // (128 * NA)):
        ts = sup * 128 * NA
        def a1(e, ts=ts):
            P.dma(e, sxA[:], io["x"][ts:ts + 128 * NA, :]
                  .rearrange("(a p) d -> p a d", p=128))
        P.step("sync", a1)
        P.step("vector", lambda e: e.tensor_reduce(statA[:, 0:NA], sxA[:],
                                                   AX.X, ALU.add))
        P.step("vector", lambda e: e.tensor_scalar_mul(statA[:, 8:8 + NA],
                                                       statA[:, 0:NA], -1.0 / D))
        def a3(e):
            last = None
            for a in range(NA):
                last = e.activation(sq8[:, a, :], sxA[:, a, :], AF.Square,
                                    bias=statA[:, 8 + a:9 + a],
                                    accum_out=statA[:, 16 + a:17 + a])
            return last
        P.step("scalar", a3)
        P.step("scalar", lambda e: e.activation(statA[:, 24:24 + NA],
                                                statA[:, 16:16 + NA], AF.Ln,
                                                scale=1.0 / D,
                                                bias=g.eps_col[:]))
        P.step("scalar", lambda e: e.activation(statA[:, 24:24 + NA],
                                                statA[:, 24:24 + NA], AF.Exp,
                                                scale=-0.5))
        def a5(e):
            last = None
            for a in range(NA):
                last = e.tensor_scalar(snrmA[:, a, :], sxA[:, a, :],
                                       statA[:, 8 + a:9 + a],
                                       statA[:, 24 + a:25 + a],
                                       ALU.add, ALU.mult)
            return last
        P.step("vector", a5)
        def a6(e, ts=ts):
            P.dma(e, io["nrm_hbm"][ts:ts + 128 * NA, :]
                  .rearrange("(a p) d -> p a d", p=128), snrmA[:])
        P.step("sync", a6)
    exA.close()
    nfm = sbB("nfm", [128, 8, 512], BF16)
    hid_tok = sbB("hid_tok", [128, 4096], BF16)
    qkf = sbB("qkf", [128, 128], F32)
    qtmp = sbB("qtmp", [128, 128], F32)
    pb = [psB(f"pb{i}", [128, 512], F32) for i in range(4)]
    pq = psB("pq", [128, 128], F32)
    pbc = psB("pbc", [128, 128], F32)

    for sup in range(T // 512):
        ts = sup * 512
        def b1(e, ts=ts):
            for c in range(8):
                P.dmaT(e, nfm[:, c, :],
                       io["nrm_hbm"][ts:ts + 512, c * 128:(c + 1) * 128])
        P.step("sync", b1)
        for tsub in range(4):
            tg = ts + tsub * 128
            for half in range(2):
                def bmm(e, tsub=tsub, half=half):
                    last = None
                    for fb in range(half * 4, half * 4 + 4):
                        p = pb[fb % 4]
                        for c in range(8):
                            e.matmul(p[:], nfm[:, c, tsub * 128:(tsub + 1) * 128],
                                     Whg[:, c, fb * 512:(fb + 1) * 512],
                                     start=(c == 0), stop=False)
                        last = e.matmul(p[:], g.ones_row[:],
                                        beff_h[:, fb * 512:(fb + 1) * 512],
                                        start=False, stop=True)
                    return last
                P.step("tensor", bmm)
                def bev(e, half=half):
                    last = None
                    for fb in range(half * 4, half * 4 + 4):
                        last = e.activation(hid_tok[:, fb * 512:(fb + 1) * 512],
                                            pb[fb % 4][:], AF.Silu)
                    return last
                P.step("scalar", bev)
            def biot(e, tg=tg):
                return e.iota(riota[:], [[1, 128]], base=1 + tg,
                              channel_multiplier=0,
                              allow_small_or_imprecise_dtypes=True)
            P.step("gpsimd", biot)
            P.step("vector", lambda e: e.reciprocal(riota[:], riota[:]))
            def bqk(e, tsub=tsub, tg=tg):
                for c in range(8):
                    e.matmul(pq[:], nfm[:, c, tsub * 128:(tsub + 1) * 128],
                             Wqkg[:, c, :], start=(c == 0), stop=False)
                e.matmul(pq[:], g.ones_row[:], beff_qk[:],
                         start=False, stop=True)
                return e.matmul(pbc[:], g.ones_row[:], riota[:],
                                start=True, stop=True)
            P.step("tensor", bqk)
            def bqe(e):
                return e.activation(qkf[:], pq[:], AF.Silu)
            P.step("scalar", bqe)
            def bqv(e, tg=tg):
                e.tensor_scalar(qtmp[:], qkf[:], oscol[:, 0:1], oscol[:, 2:3],
                                ALU.mult, ALU.add)
                return e.tensor_scalar(kT[:, tg:tg + 128], qkf[:],
                                       oscol[:, 1:2], oscol[:, 3:4],
                                       ALU.mult, ALU.add)
            P.step("vector", bqv)
            def bqv2(e, tg=tg):
                return e.tensor_mul(qhatT[:, tg:tg + 128], qtmp[:], pbc[:])
            P.step("vector", bqv2)
            def bhw(e, tg=tg):
                P.dma(e, io["hid_hbm"][tg:tg + 128, :], hid_tok[:])
            P.step("sync", bhw)

    exB.close()
    attnT = sb("attnT", [128, NT, 512], BF16)
    relu_s = [sb(f"relu{i}", [128, 512], F32) for i in range(2)]
    vslice = sb("vslice", [128, NT, 128], BF16)
    gatef = sb("gatef", [128, 512], BF16)
    gated = sb("gated", [128, 16, 512], BF16)
    Woutb = sb("Woutb", [128, 16, 1024], BF16)
    xres = sb("xres", [128, 4, 1024], F32)
    oacc = sb("oacc", [128, 1024], F32)
    onrm = sb("onrm", [128, 1024], F32)
    statC = sb("statC", [128, 8], F32)
    ps_sim = [ps(f"psim{i}", [128, 512], F32) for i in range(2)]
    ps_o = ps("ps_o", [128, 512], F32)
    ps_w = ps("ps_w", [128, 512], F32)

    def c0(e):
        for c in range(16):
            P.dma(e, Woutb[:, c, :], io["W_out"][c * 128:(c + 1) * 128, :])
    P.step("gpsimd", c0)

    for qb in range(NQB):
        nk = 4 * (qb + 1)
        qs = qb * 512
        for k0 in range(0, nk, 2):
            kk = [k for k in (k0, k0 + 1) if k < nk]
            def csim(e, kk=kk, qs=qs):
                for j, k in enumerate(kk):
                    last = e.matmul(ps_sim[j][:], kT[:, k * 128:(k + 1) * 128],
                                    qhatT[:, qs:qs + 512], start=True, stop=True)
                return last
            P.step("tensor", csim)
            def crel(e, kk=kk):
                last = None
                for j, k in enumerate(kk):
                    last = e.activation(relu_s[j][:], ps_sim[j][:], AF.Relu)
                return last
            P.step("scalar", crel)
            def csq(e, kk=kk):
                last = None
                for j, k in enumerate(kk):
                    last = e.tensor_mul(attnT[:, k, :], relu_s[j][:],
                                        relu_s[j][:])
                return last
            P.step("vector", csq)
        def cdiag(e, qb=qb, nk=nk):
            last = None
            for k in range(nk - 4, nk):
                rel = k * 128 - qb * 512
                last = e.affine_select(attnT[:, k, :], attnT[:, k, :],
                                       [[1, 512]], ALU.is_ge, 0.0,
                                       base=-rel, channel_multiplier=-1)
            return last
        P.step("gpsimd", cdiag)

        for vf in range(16):
            def cvl(e, vf=vf, nk=nk, qs=qs):
                P.dma(e, vslice[:, 0:nk, :],
                      io["hid_hbm"][0:nk * 128, vf * 128:(vf + 1) * 128]
                      .rearrange("(k p) f -> p k f", p=128))
                P.dmaT(e, gatef[:],
                       io["hid_hbm"][qs:qs + 512,
                                     2048 + vf * 128:2048 + (vf + 1) * 128])
            P.step("sync", cvl)
            def cav(e, vf=vf, nk=nk):
                last = None
                for k in range(nk):
                    last = e.matmul(ps_o[:], vslice[:, k, :], attnT[:, k, :],
                                    start=(k == 0), stop=(k == nk - 1))
                return last
            P.step("tensor", cav)
            def cgt(e, vf=vf):
                return e.tensor_mul(gated[:, vf, :], ps_o[:], gatef[:])
            P.step("vector", cgt)

        def cxl(e, qs=qs):
            P.dma(e, xres[:], io["x"][qs:qs + 512, :]
                  .rearrange("(a p) d -> p a d", p=128))
        P.step("sync", cxl)
        for tsub in range(4):
            for dh in range(2):
                def cwm(e, tsub=tsub, dh=dh):
                    for c in range(16):
                        e.matmul(ps_w[:], gated[:, c, tsub * 128:(tsub + 1) * 128],
                                 Woutb[:, c, dh * 512:(dh + 1) * 512],
                                 start=(c == 0), stop=False)
                    return e.matmul(ps_w[:], g.ones_row[:],
                                    bout_row[:, dh * 512:(dh + 1) * 512],
                                    start=False, stop=True)
                P.step("tensor", cwm)
                def cwe(e, tsub=tsub, dh=dh):
                    return e.scalar_tensor_tensor(
                        oacc[:, dh * 512:(dh + 1) * 512], xres[:, tsub, dh * 512:(dh + 1) * 512],
                        2.0, ps_w[:], ALU.mult, ALU.add)
                P.step("vector", cwe)
            emit_postln(P, g, oacc, statC, onrm, D=D, scr=sqscr)
            def cxw(e, tsub=tsub, qs=qs):
                P.dma(e, io["x1"][qs + tsub * 128:qs + (tsub + 1) * 128, :],
                      onrm[:])
            P.step("sync", cxw)


def emit_rms_spill(P, nc, g, io_in, io_out, T, gbc, scale_ln_col, tag):
    from contextlib import ExitStack
    ex = ExitStack()
    D = 1024
    NA = 4
    sx = ex.enter_context(nc.sbuf_tensor(f"rms_sx{tag}", [128, NA, 1024], F32))
    so = ex.enter_context(nc.sbuf_tensor(f"rms_so{tag}", [128, NA, 1024], BF16))
    st = ex.enter_context(nc.sbuf_tensor(f"rms_st{tag}", [128, 16], F32))
    sq = ex.enter_context(nc.sbuf_tensor(f"rms_sq{tag}", [128, NA, 1024], F32))
    for sup in range(T // (128 * NA)):
        ts = sup * 128 * NA
        def r1(e, ts=ts):
            P.dma(e, sx[:], io_in[ts:ts + 128 * NA, :]
                  .rearrange("(a p) d -> p a d", p=128))
        P.step("sync", r1)
        def r2(e):
            last = None
            for a in range(NA):
                last = e.activation(sq[:, a, :], sx[:, a, :], AF.Square,
                                    accum_out=st[:, a:a + 1])
            return last
        P.step("scalar", r2)
        P.step("vector", lambda e: e.tensor_scalar_max(st[:, 0:NA],
                                                       st[:, 0:NA], 1e-24))
        P.step("scalar", lambda e: e.activation(st[:, 8:8 + NA], st[:, 0:NA],
                                                AF.Ln))
        P.step("scalar", lambda e: e.activation(st[:, 8:8 + NA], st[:, 8:8 + NA],
                                                AF.Exp, scale=-0.5,
                                                bias=scale_ln_col[:]))
        def r5(e):
            last = None
            for a in range(NA):
                last = e.tensor_scalar_mul(sq[:, a, :], sx[:, a, :],
                                           st[:, 8 + a:9 + a])
            return last
        P.step("vector", r5)
        def r6(e):
            last = None
            for a in range(NA):
                last = e.tensor_mul(so[:, a, :], sq[:, a, :], gbc[:])
            return last
        P.step("vector", r6)
        def r7(e, ts=ts):
            P.dma(e, io_out[ts:ts + 128 * NA, :]
                  .rearrange("(a p) d -> p a d", p=128), so[:])
        P.step("sync", r7)
    ex.close()


def emit_bcast_row(P, nc, g, dram_vec, out_tile, psb, n):
    row = out_tile
    def s1(e):
        P.dma(e, g.row_scr[:, 0:n], dram_vec.rearrange("(a f) -> a f", a=1))
    P.step("sync", s1)
    for j0 in range(0, n, 512):
        w = min(512, n - j0)
        def s2(e, j0=j0, w=w):
            return e.matmul(psb[:, 0:w], g.ones_row[:], g.row_scr[:, j0:j0 + w],
                            start=True, stop=True)
        P.step("tensor", s2)
        def s3(e, j0=j0, w=w):
            return e.tensor_copy(out_tile[:, j0:j0 + w], psb[:, 0:w])
        P.step("vector", s3)


def emit_l0_ffn(P, nc, ex, g, io, T):
    from contextlib import ExitStack
    D = 1024
    ES = 4096
    NT = T // 128

    sb = lambda n, s, d: ex.enter_context(nc.sbuf_tensor(n, s, d))
    ps = lambda n, s, d: ex.enter_context(nc.psum_tensor(n, s, d))

    ex0 = ExitStack()
    gxbc = sb("gxbc", [128, 1024], F32)
    gsbc = sb("gsbc", [128, 1024], F32)
    ln32 = sb("ln32", [128, 1], F32)
    Mn_all = sb("Mn_all", [128, 64], F32)
    Sn_all = sb("Sn_all", [128, 64], F32)
    MnT = sb("MnT", [64, 128], F32)
    identf = sb("identf", [128, 128], F32)
    ps_pro = ex0.enter_context(nc.psum_tensor("ps_pro", [128, 512], F32))
    P.step("vector", lambda e: e.memset(ln32[:], math.log(32.0)))
    emit_bcast_row(P, nc, g, io["norm_gamma"], gxbc, ps_pro, 1024)
    emit_bcast_row(P, nc, g, io["slot_norm_gamma"], gsbc, ps_pro, 1024)
    def idgen(e):
        return e.iota(identf[:], [[1, 128]], base=0, channel_multiplier=-1,
                      allow_small_or_imprecise_dtypes=True)
    P.step("gpsimd", idgen)
    def idgen2(e):
        return e.tensor_single_scalar(identf[:], identf[:], 0.0, ALU.is_equal)
    P.step("vector", idgen2)

    emit_rms_spill(P, nc, g, io["x"], io["xn_hbm"], T, gxbc, ln32, "x")
    se_flat = io["slot_embeds"].rearrange("e s d -> (e s) d")
    emit_rms_spill(P, nc, g, se_flat, io["sn_hbm"], 4096, gsbc, ln32, "s")
    ex0.close()

    exF = ExitStack()
    sbF = lambda n, s, d: exF.enter_context(nc.sbuf_tensor(n, s, d))
    psF = lambda n, s, d: exF.enter_context(nc.psum_tensor(n, s, d))
    sn_fm = sbF("sn_fm", [128, 8, ES], BF16)
    xnf2 = [sbF(f"xnf{i}", [128, 8, 512], BF16) for i in range(2)]
    lgt = sbF("lgt", [128, ES], BF16)
    lgl = sbF("lgl", [128, ES], BF16)
    mcols = sbF("mcols", [128, 8], F32)
    plg = [psF(f"plg{i}", [128, 512], F32) for i in range(2)]
    f3id = None
    def f3(e):
        for c in range(8):
            P.dmaT(e, sn_fm[:, c, :], io["sn_hbm"][:, c * 128:(c + 1) * 128])
    f3id = P.step("sync", f3)
    fd = {}
    for sup in range(T // 512):
        ts = sup * 512
        def f4a(e, ts=ts, sup=sup):
            for c in range(8):
                P.dmaT(e, xnf2[sup % 2][:, c, :],
                       io["xn_hbm"][ts:ts + 512, c * 128:(c + 1) * 128])
        fd[("ld", sup)] = P.step("sync", f4a,
                                 deps=[fd.get(("mm", sup - 2, 3, 7))])
        for nsub in range(4):
            nt = (ts + nsub * 128) // 128
            for esb in range(8):
                def f4m(e, nsub=nsub, esb=esb, sup=sup):
                    p = plg[esb % 2]
                    last = None
                    for c in range(8):
                        last = e.matmul(p[:], xnf2[sup % 2][:, c, nsub * 128:(nsub + 1) * 128],
                                        sn_fm[:, c, esb * 512:(esb + 1) * 512],
                                        start=(c == 0), stop=(c == 7))
                    return last
                fd[("mm", sup, nsub, esb)] = P.step(
                    "tensor", f4m,
                    deps=[fd[("ld", sup)] if (nsub == 0 and esb == 0) else None,
                          f3id if sup == 0 and nsub == 0 and esb == 0 else None,
                          fd.get(("lo", sup, nsub, esb - 2)),
                          fd.get(("lo", sup, nsub - 1, 6 + esb)) if esb < 2 else None,
                          fd.get(("lo", sup - 1, 3, 6 + esb))
                          if (esb < 2 and nsub == 0) else None])
                def f4e(e, esb=esb):
                    e.tensor_copy(lgt[:, esb * 512:(esb + 1) * 512],
                                  plg[esb % 2][:])
                    return e.reduce_max(mcols[:, esb:esb + 1], plg[esb % 2][:],
                                        axis=AX.X)
                fd[("ev", sup, nsub, esb)] = P.step(
                    "vector", f4e, deps=[fd[("mm", sup, nsub, esb)],
                                         fd.get(("wr", sup, nsub - 1))
                                         if esb == 0 else None,
                                         fd.get(("wr", sup - 1, 3))
                                         if (esb == 0 and nsub == 0) else None])
                def f4l(e, esb=esb):
                    return e.tensor_sub(lgl[:, esb * 512:(esb + 1) * 512],
                                        plg[esb % 2][:],
                                        lgt[:, esb * 512:(esb + 1) * 512])
                fd[("lo", sup, nsub, esb)] = P.step("vector", f4l)
            def f4M(e, nt=nt):
                return e.reduce_max(Mn_all[:, nt:nt + 1], mcols[:], axis=AX.X)
            P.step("vector", f4M)
            def f4w(e, nt=nt):
                P.dma(e, io["lg_hbm"][nt * 128:(nt + 1) * 128, :], lgt[:])
                P.dma(e, io["lglo_hbm"][nt * 128:(nt + 1) * 128, :], lgl[:])
            fd[("wr", sup, nsub)] = P.step("sync", f4w,
                                           deps=[len(P.steps) - 1])
    exF.close()

    exG = ExitStack()
    sbG = lambda n, s, d: exG.enter_context(nc.sbuf_tensor(n, s, d))
    lgtT2 = [sbG(f"lgtT{i}", [128, T], BF16) for i in range(2)]
    lgtTl2 = [sbG(f"lgtTl{i}", [128, T], BF16) for i in range(2)]
    lgtTf = sbG("lgtTf", [128, T], F32)
    dspT = sbG("dspT", [128, T], BF16)
    mes = sbG("mes", [128, 4], F32)
    gdd = {}
    for et in range(32):
        def f5a(e, et=et):
            P.dmaT(e, lgtT2[et % 2][:], io["lg_hbm"][:, et * 128:(et + 1) * 128])
            P.dmaT(e, lgtTl2[et % 2][:], io["lglo_hbm"][:, et * 128:(et + 1) * 128])
        gdd[("ld", et)] = P.step("sync", f5a, deps=[gdd.get(("ad", et - 2))]
                                 if et >= 2 else None)
        def f5ad(e, et=et):
            return e.tensor_add(lgtTf[:], lgtT2[et % 2][:], lgtTl2[et % 2][:])
        gdd[("ad", et)] = P.step("vector", f5ad,
                                 deps=[gdd[("ld", et)], len(P.steps) - 1])
        P.step("vector", lambda e: e.reduce_max(mes[:, 0:1], lgtTf[:], axis=AX.X))
        P.step("vector", lambda e: e.tensor_scalar_mul(mes[:, 1:2], mes[:, 0:1],
                                                       -1.0))
        def f5e(e):
            return e.activation(dspT[:], lgtTf[:], AF.Exp, bias=mes[:, 1:2],
                                accum_out=mes[:, 2:3])
        P.step("scalar", f5e)
        P.step("vector", lambda e: e.reciprocal(mes[:, 3:4], mes[:, 2:3]))
        P.step("vector", lambda e: e.tensor_scalar_mul(dspT[:], dspT[:],
                                                       mes[:, 3:4]))
        def f5d(e, et=et):
            P.dma(e, io["disp_hbm"][et * 128:(et + 1) * 128, :], dspT[:])
        P.step("sync", f5d)
    exG.close()

    exH = ExitStack()
    sbH = lambda n, s, d: exH.enter_context(nc.sbuf_tensor(n, s, d))
    psH = lambda n, s, d: exH.enter_context(nc.psum_tensor(n, s, d))
    xtk2 = [sbH(f"xtk{i}", [128, 1024], BF16) for i in range(2)]
    dtk2 = [sbH(f"dtk{i}", [128, 512], BF16) for i in range(2)]
    slo = sbH("slo", [128, 8, 512], BF16)
    psl = [psH(f"psl{i}", [128, 512], F32) for i in range(8)]
    hd = {}
    for esb in range(8):
        for n in range(NT):
            it = esb * NT + n
            def f6a(e, esb=esb, n=n, it=it):
                P.dma(e, xtk2[it % 2][:], io["xn_hbm"][n * 128:(n + 1) * 128, :])
                P.dmaT(e, dtk2[it % 2][:],
                       io["disp_hbm"][esb * 512:(esb + 1) * 512,
                                      n * 128:(n + 1) * 128])
            hd[("ld", it)] = P.step("sync", f6a, deps=[hd.get(("mm", it - 2))])
            def f6m(e, esb=esb, n=n, it=it):
                last = None
                for dc in range(8):
                    last = e.matmul(psl[dc][:],
                                    xtk2[it % 2][:, dc * 128:(dc + 1) * 128],
                                    dtk2[it % 2][:], start=(n == 0),
                                    stop=(n == NT - 1))
                return last
            hd[("mm", it)] = P.step("tensor", f6m,
                                    deps=[hd[("ld", it)],
                                          hd.get(("ev", esb - 1)) if n == 0 else None])
        def f6e(e, esb=esb):
            last = None
            for dc in range(8):
                last = e.tensor_copy(slo[:, dc, :], psl[dc][:])
            return last
        hd[("ev", esb)] = P.step("vector", f6e,
                                 deps=[hd[("mm", esb * NT + NT - 1)],
                                       hd.get(("wr", esb - 1))])
        def f6w(e, esb=esb):
            P.dma(e, io["slots_hbm"].rearrange("(dc p) es -> p dc es", p=128)
                  [:, :, esb * 512:(esb + 1) * 512], slo[:])
        hd[("wr", esb)] = P.step("sync", f6w, deps=[hd[("ev", esb)]])
    exH.close()

    eo_all = sb("eo_all", [128, 32, 1024], BF16)
    exI = ExitStack()
    sbI = lambda n, s, d: exI.enter_context(nc.sbuf_tensor(n, s, d))
    psI = lambda n, s, d: exI.enter_context(nc.psum_tensor(n, s, d))
    w1e = sbI("w1e", [128, 8, 4096], BF16)
    slte = sbI("slte", [128, 8, 512], BF16)
    hfm = sbI("hfm", [128, 32, 512], BF16)
    w2c = [sbI(f"w2c{i}", [128, 1024], BF16) for i in range(2)]
    b1c = sbI("b1c", [128, 32], F32)
    b2r = sbI("b2r", [1, 1024], F32)
    f7_last = None
    for ee in range(8):
        exE = ExitStack()
        ph = [exE.enter_context(nc.psum_tensor(f"ph{ee}_{i}", [128, 512], F32))
              for i in range(2)]
        def f7a(e, ee=ee):
            for dc in range(8):
                P.dma(e, w1e[:, dc, :], io["w1"][ee, dc * 128:(dc + 1) * 128, :])
            P.dma(e, slte[:],
                  io["slots_hbm"].rearrange("(dc p) es -> p dc es", p=128)
                  [:, :, ee * 512:(ee + 1) * 512])
            load_vec_as_cols(P, e, b1c, io["b1"][ee, :], 4096)
        P.step("gpsimd", f7a)
        def f7b(e, ee=ee):
            P.dma(e, b2r[:], io["b2"][ee:ee + 1, :])
        P.step("sync", f7b)
        gd = {}
        for hc in range(32):
            def f7m(e, hc=hc):
                p = ph[hc % 2]
                last = None
                for dc in range(8):
                    last = e.matmul(p[:], w1e[:, dc, hc * 128:(hc + 1) * 128],
                                    slte[:, dc, :], start=(dc == 0),
                                    stop=(dc == 7))
                return last
            gd[("mm", hc)] = P.step("tensor", f7m,
                                    deps=[gd.get(("gl", hc - 2))] if hc >= 2
                                    else None)
            def f7g(e, hc=hc):
                return e.activation(hfm[:, hc, :], ph[hc % 2][:], AF.Gelu,
                                    bias=b1c[:, hc:hc + 1])
            gd[("gl", hc)] = P.step("scalar", f7g, deps=[gd[("mm", hc)]])
        for p in ph:
            pass
        exE.close()
        exE = ExitStack()
        peo = [exE.enter_context(nc.psum_tensor(f"peo{ee}_{i}", [128, 512], F32))
               for i in range(8)]
        for hc in range(32):
            def f7w2(e, ee=ee, hc=hc):
                P.dma(e, w2c[hc % 2][:], io["w2"][ee, hc * 128:(hc + 1) * 128, :])
            gd[("w2", hc)] = P.step("gpsimd", f7w2,
                                    deps=[gd.get(("eo", hc - 2)),
                                          gd[("gl", 31)] if hc == 0 else None])
            def f7e(e, hc=hc):
                last = None
                for ssub in range(4):
                    for dh in range(2):
                        last = e.matmul(
                            peo[ssub * 2 + dh][:],
                            hfm[:, hc, ssub * 128:(ssub + 1) * 128],
                            w2c[hc % 2][:, dh * 512:(dh + 1) * 512],
                            start=(hc == 0), stop=False)
                return last
            gd[("eo", hc)] = P.step("tensor", f7e,
                                    deps=[gd[("w2", hc)],
                                          gd[("gl", hc)]])
        def f7bias(e):
            last = None
            for ssub in range(4):
                for dh in range(2):
                    last = e.matmul(peo[ssub * 2 + dh][:], g.ones_row[:],
                                    b2r[:, dh * 512:(dh + 1) * 512],
                                    start=False, stop=True)
            return last
        P.step("tensor", f7bias)
        def f7v(e, ee=ee):
            last = None
            for ssub in range(4):
                for dh in range(2):
                    last = e.tensor_copy(
                        eo_all[:, ee * 4 + ssub, dh * 512:(dh + 1) * 512],
                        peo[ssub * 2 + dh][:])
            return last
        f7_last = P.step("vector", f7v)
        exE.close()
    exI.close()

    exJ = ExitStack()
    sbJ = lambda n, s, d: exJ.enter_context(nc.sbuf_tensor(n, s, d))
    psJ = lambda n, s, d: exJ.enter_context(nc.psum_tensor(n, s, d))
    lgn2 = [sbJ(f"lgn{i}", [128, ES], BF16) for i in range(2)]
    lgnl2 = [sbJ(f"lgnl{i}", [128, ES], BF16) for i in range(2)]
    lgnf2 = [sbJ(f"lgnf{i}", [128, ES], F32) for i in range(2)]
    cet2 = [sbJ(f"cet{i}", [128, ES], BF16) for i in range(2)]
    nmn = sbJ("nmn", [128, 2], F32)
    md = {}
    for n in range(NT):
        def h1(e, n=n):
            P.dma(e, lgn2[n % 2][:], io["lg_hbm"][n * 128:(n + 1) * 128, :])
            P.dma(e, lgnl2[n % 2][:], io["lglo_hbm"][n * 128:(n + 1) * 128, :])
        md[("ld", n)] = P.step("sync", h1,
                               deps=[f7_last if n == 0 else None,
                                     md.get(("ad", n - 2))])
        def had(e, n=n):
            e.tensor_add(lgnf2[n % 2][:], lgn2[n % 2][:], lgnl2[n % 2][:])
            return e.tensor_scalar_mul(nmn[:, n % 2:n % 2 + 1],
                                       Mn_all[:, n:n + 1], -1.0)
        md[("ad", n)] = P.step("vector", had,
                               deps=[md[("ld", n)], md.get(("ex", n - 2))])
        def h3(e, n=n):
            return e.activation(cet2[n % 2][:], lgnf2[n % 2][:], AF.Exp,
                                bias=nmn[:, n % 2:n % 2 + 1],
                                accum_out=Sn_all[:, n:n + 1])
        md[("ex", n)] = P.step("scalar", h3,
                               deps=[md[("ad", n)], md.get(("wr", n - 2))])
        def h4(e, n=n):
            P.dma(e, io["ce_hbm"][n * 128:(n + 1) * 128, :], cet2[n % 2][:])
        md[("wr", n)] = P.step("sync", h4, deps=[md[("ex", n)]])
    exJ.close()

    exK = ExitStack()
    sbK = lambda n, s, d: exK.enter_context(nc.sbuf_tensor(n, s, d))
    psK = lambda n, s, d: exK.enter_context(nc.psum_tensor(n, s, d))
    ceT2 = [sbK(f"ceT2{i}", [128, 32, 512], BF16) for i in range(2)]
    xre = sbK("xre", [128, 1024], F32)
    oac = sbK("oac", [128, 1024], F32)
    onr = sbK("onr", [128, 1024], F32)
    stJ = sbK("stJ", [128, 8], F32)
    snc = sbK("snc", [128, 1], F32)
    scrK = sbK("scrK", [128, 1024], F32)
    pb8 = [psK(f"pb8{i}", [128, 512], F32) for i in range(2)]
    nd = {}
    for nsup in range(ceil_div(NT, 4)):
      ns0 = nsup * 512
      nw = min(512, T - ns0)
      def k0(e, ns0=ns0, nw=nw, nsup=nsup):
        for k in range(32):
            P.dmaT(e, ceT2[nsup % 2][:, k, 0:nw],
                   io["ce_hbm"][ns0:ns0 + nw, k * 128:(k + 1) * 128])
      nd[("ld", nsup)] = P.step("sync", k0,
                                deps=[md[("wr", NT - 1)] if nsup == 0 else None,
                                      nd.get(("mm", nsup - 2, 3, 1))])
      for n in range(ns0 // 128, (ns0 + nw) // 128):
        def k1(e, n=n):
            P.dma(e, xre[:], io["x"][n * 128:(n + 1) * 128, :])
        P.step("sync", k1)
        def k2(e, n=n):
            return e.reciprocal(snc[:], Sn_all[:, n:n + 1])
        P.step("vector", k2)
        for dh in range(2):
            def k3(e, n=n, ns0=ns0, dh=dh, nsup=nsup):
                nn0 = n * 128 - ns0
                last = None
                for k in range(32):
                    last = e.matmul(pb8[dh][:],
                                    ceT2[nsup % 2][:, k, nn0:nn0 + 128],
                                    eo_all[:, k, dh * 512:(dh + 1) * 512],
                                    start=(k == 0), stop=(k == 31))
                return last
            nd[("mm", nsup, (n * 128 - ns0) // 128, dh)] = P.step(
                "tensor", k3, deps=[nd[("ld", nsup)], len(P.steps) - 1]
                if (n * 128 == ns0 and dh == 0) else None)
        def k4(e):
            e.tensor_scalar(oac[:, 0:512], pb8[0][:], snc[:], None, ALU.mult)
            return e.tensor_scalar(oac[:, 512:1024], pb8[1][:], snc[:], None,
                                   ALU.mult)
        P.step("vector", k4)
        P.step("vector", lambda e: e.tensor_add(oac[:], oac[:], xre[:]))
        emit_postln(P, g, oac, stJ, onr, D=D, scr=scrK)
        def k5(e, n=n):
            P.dma(e, io["x2"][n * 128:(n + 1) * 128, :], onr[:])
        P.step("sync", k5)
    exK.close()


def emit_l1_attn(P, nc, ex, g, io, T):
    from contextlib import ExitStack
    D = 1024
    H = 8
    NT = T // 128
    K0 = 8.0
    SC = 1.0 / math.sqrt(128.0)

    sb = lambda n, s, d: ex.enter_context(nc.sbuf_tensor(n, s, d))

    gnw_bc = sb("gnw_bc", [128, 1024], F32)
    g1_fm = sb("g1_fm", [128, T], BF16)
    Wg2T = sb("Wg2T", [128, 1024], BF16)
    identb = sb("identb", [128, 128], BF16)
    ln1 = sb("ln1", [128, 1], F32)

    ex0 = ExitStack()
    ps_pro = ex0.enter_context(nc.psum_tensor("ps_pro1", [128, 512], F32))
    emit_bcast_row(P, nc, g, io["g_norm_weight"], gnw_bc, ps_pro, 1024)
    def pid(e):
        return e.iota(ln1[:], [[1, 1]], base=0, channel_multiplier=0,
                      allow_small_or_imprecise_dtypes=True)
    P.step("gpsimd", pid)
    scp = ex0.enter_context(nc.sbuf_tensor("scp", [128, 512], F32))
    def pidb(e):
        return e.iota(scp[:, 0:128], [[1, 128]], base=0, channel_multiplier=-1,
                      allow_small_or_imprecise_dtypes=True)
    P.step("gpsimd", pidb)
    P.step("vector", lambda e: e.tensor_single_scalar(scp[:, 0:128],
                                                      scp[:, 0:128], 0.0,
                                                      ALU.is_equal))
    P.step("vector", lambda e: e.tensor_copy(identb[:], scp[:, 0:128]))
    def spill(e):
        for r in range(0, 1024, 128):
            P.dma(e, io["wq_bf"][r:r + 128, :], io["Wq"][r:r + 128, :])
            P.dma(e, io["wk_bf"][r:r + 128, :], io["Wk"][r:r + 128, :])
            P.dma(e, io["wv_bf"][r:r + 128, :], io["Wv"][r:r + 128, :])
            P.dma(e, io["wo_bf"][r:r + 128, :], io["Wo"][r:r + 128, :])
        P.dma(e, io["wg1_bf"][:, :], io["Wg1"][:, :])
        for r in range(0, 1024, 128):
            P.dma(e, io["wg2_bf"][r:r + 128, :], io["Wg2"][r:r + 128, :])
        for t0 in range(0, T, 512):
            P.dma(e, io["x_bf"][t0:t0 + 512, :], io["x"][t0:t0 + 512, :])
    P.step("gpsimd", spill)
    ex0.close()

    exB = ExitStack()
    sbB = lambda n, s, d: exB.enter_context(nc.sbuf_tensor(n, s, d))
    psB = lambda n, s, d: exB.enter_context(nc.psum_tensor(n, s, d))
    WqT = sbB("WqT", [128, 8, 1024], BF16)
    WkT = sbB("WkT", [128, 8, 1024], BF16)
    WvT = sbB("WvT", [128, 8, 1024], BF16)
    Wg1T = sbB("Wg1T", [128, 8, 128], BF16)
    xf = sbB("xf1", [128, 8, 512], BF16)
    prj = sbB("prj", [128, 1024], BF16)
    pp = [psB(f"pp1{i}", [128, 512], F32) for i in range(2)]
    def l1w(e):
        for c in range(8):
            for ob in range(2):
                P.dmaT(e, WqT[:, c, ob * 512:(ob + 1) * 512],
                       io["wq_bf"][ob * 512:(ob + 1) * 512, c * 128:(c + 1) * 128])
                P.dmaT(e, WkT[:, c, ob * 512:(ob + 1) * 512],
                       io["wk_bf"][ob * 512:(ob + 1) * 512, c * 128:(c + 1) * 128])
                P.dmaT(e, WvT[:, c, ob * 512:(ob + 1) * 512],
                       io["wv_bf"][ob * 512:(ob + 1) * 512, c * 128:(c + 1) * 128])
            P.dmaT(e, Wg1T[:, c, :], io["wg1_bf"][:, c * 128:(c + 1) * 128])
        for ob in range(2):
            P.dmaT(e, Wg2T[:, ob * 512:(ob + 1) * 512],
                   io["wg2_bf"][ob * 512:(ob + 1) * 512, :])
    P.step("sync", l1w)
    for sup in range(T // 512):
        ts = sup * 512
        def m1(e, ts=ts):
            for c in range(8):
                P.dmaT(e, xf[:, c, :], io["x_bf"][ts:ts + 512,
                                                  c * 128:(c + 1) * 128])
        P.step("sync", m1)
        for hh in range(8):
            def mq(e, hh=hh, ts=ts):
                for c in range(8):
                    e.matmul(pp[0][:], WqT[:, c, hh * 128:(hh + 1) * 128],
                             xf[:, c, :], start=(c == 0), stop=(c == 7))
                last = None
                for c in range(8):
                    last = e.matmul(pp[1][:], WkT[:, c, hh * 128:(hh + 1) * 128],
                                    xf[:, c, :], start=(c == 0), stop=(c == 7))
                return last
            P.step("tensor", mq)
            def mqe(e, hh=hh, ts=ts):
                return e.activation(prj[:, 0:512], pp[0][:], AF.Silu)
            P.step("scalar", mqe)
            def mqv(e, hh=hh, ts=ts):
                return e.tensor_copy(prj[:, 512:1024], pp[1][:])
            P.step("vector", mqv)
            def mqw(e, hh=hh, ts=ts):
                P.dma(e, io["qf_hbm"][hh * 128:(hh + 1) * 128, ts:ts + 512],
                      prj[:, 0:512])
                P.dma(e, io["kf_hbm"][hh * 128:(hh + 1) * 128, ts:ts + 512],
                      prj[:, 512:1024])
            P.step("sync", mqw)
        def mg(e, ts=ts):
            last = None
            for c in range(8):
                last = e.matmul(pp[0][:], Wg1T[:, c, :], xf[:, c, :],
                                start=(c == 0), stop=(c == 7))
            return last
        P.step("tensor", mg)
        def mge(e, ts=ts):
            return e.tensor_copy(g1_fm[:, ts:ts + 512], pp[0][:])
        P.step("vector", mge)
        for tsub in range(4):
            def mv(e, ts=ts, tsub=tsub):
                last = None
                for ob in range(2):
                    for c in range(8):
                        last = e.matmul(pp[ob][:],
                                        xf[:, c, tsub * 128:(tsub + 1) * 128],
                                        WvT[:, c, ob * 512:(ob + 1) * 512],
                                        start=(c == 0), stop=(c == 7))
                return last
            P.step("tensor", mv)
            def mve(e):
                e.tensor_copy(prj[:, 0:512], pp[0][:])
                return e.tensor_copy(prj[:, 512:1024], pp[1][:])
            P.step("vector", mve)
            def mvw(e, ts=ts, tsub=tsub):
                P.dma(e, io["vtok_hbm"][ts + tsub * 128:ts + (tsub + 1) * 128, :],
                      prj[:])
            P.step("sync", mvw)
    exB.close()

    exC = ExitStack()
    sbC = lambda n, s, d: exC.enter_context(nc.sbuf_tensor(n, s, d))
    psC = lambda n, s, d: exC.enter_context(nc.psum_tensor(n, s, d))
    kfh = sbC("kfh", [128, T], BF16)
    kap = sbC("kap", [128, T], BF16)
    csc = sbC("csc", [128, T], F32)
    qhh = sbC("qhh", [128, T], BF16)
    ktk = sbC("ktk", [128, NT, 128], BF16)
    vth = sbC("vth", [128, NT, 128], BF16)
    oth = sbC("oth", [128, NT, 128], BF16)
    zb = sbC("zb", [128, T], BF16)
    Sf = sbC("Sf", [128, 128], F32)
    Sb = sbC("Sb", [128, 128], BF16)
    ssc = sbC("ssc", [128, 128], BF16)
    ps1 = psC("ps1l", [128, 512], F32)
    ps2 = psC("ps2l", [128, 512], F32)
    psb = psC("psbl", [128, 512], BF16)
    P.step("vector", lambda e: e.memset(zb[:], 0.0))
    for hh in range(H):
        def n1(e, hh=hh):
            P.dma(e, kfh[:], io["kf_hbm"][hh * 128:(hh + 1) * 128, :])
            P.dma(e, qhh[:], io["qf_hbm"][hh * 128:(hh + 1) * 128, :])
            P.dma(e, vth[:], io["vtok_hbm"][:, hh * 128:(hh + 1) * 128]
                  .rearrange("(k p) f -> p k f", p=128))
        P.step("sync", n1)
        def n2(e):
            return e.activation(kap[:], kfh[:], AF.Exp, bias=g.negk0_col[:])
        P.step("scalar", n2)
        def n3(e):
            return e.tensor_tensor_scan(csc[:], kap[:], zb[:], 0.0,
                                        ALU.add, ALU.add)
        P.step("vector", n3)
        P.step("vector", lambda e: e.reciprocal(csc[:], csc[:]))
        def n5(e):
            return e.scalar_tensor_tensor(qhh[:], qhh[:], SC, csc[:],
                                          ALU.mult, ALU.mult)
        P.step("vector", n5)
        for c0 in range(0, NT, 4):
            def n6(e, c0=c0):
                last = None
                for j in range(4):
                    if c0 + j < NT:
                        last = e.transpose(psb[:, j * 128:(j + 1) * 128],
                                           kap[:, (c0 + j) * 128:(c0 + j + 1) * 128],
                                           identb[:])
                return last
            P.step("tensor", n6)
            def n7(e, c0=c0):
                last = None
                for j in range(4):
                    if c0 + j < NT:
                        last = e.tensor_copy(ktk[:, c0 + j, :],
                                             psb[:, j * 128:(j + 1) * 128])
                return last
            P.step("vector", n7)
        P.step("vector", lambda e: e.memset(Sf[:], 0.0))
        P.step("vector", lambda e: e.memset(Sb[:], 0.0))
        for ci in range(NT):
            def o1(e, ci=ci):
                return e.matmul(ps1[:, 0:128], kap[:, ci * 128:(ci + 1) * 128],
                                qhh[:, ci * 128:(ci + 1) * 128],
                                start=True, stop=True)
            P.step("tensor", o1)
            def o2(e, ci=ci):
                return e.tensor_copy(ssc[:], ps1[:, 0:128])
            P.step("vector", o2)
            def o3(e, ci=ci):
                return e.affine_select(ssc[:], ssc[:], [[1, 128]], ALU.is_ge,
                                       0.0, base=0, channel_multiplier=-1)
            P.step("gpsimd", o3)
            def o4(e, ci=ci):
                e.matmul(ps2[:, 0:128], ssc[:], vth[:, ci, :],
                         start=True, stop=False)
                e.matmul(ps2[:, 0:128], qhh[:, ci * 128:(ci + 1) * 128], Sb[:],
                         start=False, stop=True)
                return e.matmul(ps1[:, 128:256], ktk[:, ci, :], vth[:, ci, :],
                                start=True, stop=True)
            P.step("tensor", o4)
            def o5(e, ci=ci):
                e.tensor_copy(oth[:, ci, :], ps2[:, 0:128])
                e.tensor_add(Sf[:], Sf[:], ps1[:, 128:256])
                return e.tensor_copy(Sb[:], Sf[:])
            P.step("vector", o5)
        def n8(e, hh=hh):
            P.dma(e, io["o_hbm"][:, hh * 128:(hh + 1) * 128]
                  .rearrange("(k p) f -> p k f", p=128), oth[:])
        P.step("sync", n8)
    exC.close()

    exD = ExitStack()
    sbD = lambda n, s, d: exD.enter_context(nc.sbuf_tensor(n, s, d))
    psD = lambda n, s, d: exD.enter_context(nc.psum_tensor(n, s, d))
    ost = sbD("ost", [128, 4, 1024], BF16)
    ogs = sbD("ogs", [128, 4, 1024], BF16)
    onn = sbD("onn", [128, 1024], F32)
    sgs = sbD("sgs", [128, 1024], F32)
    geff = sbD("geff", [128, 1024], F32)
    stD = sbD("stD", [128, 16], F32)
    sqD = sbD("sqD", [128, 4, 1024], F32)
    pg = [psD(f"pg{i}", [128, 512], F32) for i in range(2)]
    for sup in range(T // 512):
        ts = sup * 512
        def q1(e, ts=ts):
            P.dma(e, ost[:], io["o_hbm"][ts:ts + 512, :]
                  .rearrange("(a p) d -> p a d", p=128))
        P.step("sync", q1)
        def q2(e):
            last = None
            for a in range(4):
                last = e.activation(sqD[:, a, :], ost[:, a, :], AF.Square,
                                    accum_out=stD[:, a:a + 1])
            return last
        P.step("scalar", q2)
        P.step("scalar", lambda e: e.activation(stD[:, 8:12], stD[:, 0:4],
                                                AF.Ln, scale=1.0 / 1024,
                                                bias=g.eps_col[:]))
        P.step("scalar", lambda e: e.activation(stD[:, 8:12], stD[:, 8:12],
                                                AF.Exp, scale=-0.5))
        for a in range(4):
            tg = ts + a * 128
            def q3(e, a=a, tg=tg):
                last = None
                for ob in range(2):
                    last = e.matmul(pg[ob][:], g1_fm[:, tg:tg + 128],
                                    Wg2T[:, ob * 512:(ob + 1) * 512],
                                    start=True, stop=True)
                return last
            P.step("tensor", q3)
            def q4(e, a=a):
                e.activation(sgs[:, 0:512], pg[0][:], AF.Sigmoid)
                return e.activation(sgs[:, 512:1024], pg[1][:], AF.Sigmoid)
            P.step("scalar", q4)
            def q5(e, a=a):
                e.tensor_mul(geff[:, 0:512], pg[0][:], sgs[:, 0:512])
                return e.tensor_mul(geff[:, 512:1024], pg[1][:],
                                    sgs[:, 512:1024])
            P.step("vector", q5)
            def q6(e, a=a):
                return e.tensor_scalar_mul(onn[:], ost[:, a, :],
                                           stD[:, 8 + a:9 + a])
            P.step("vector", q6)
            def q7(e, a=a):
                return e.tensor_mul(onn[:], onn[:], gnw_bc[:])
            P.step("vector", q7)
            def q8(e, a=a):
                return e.tensor_mul(ogs[:, a, :], onn[:], geff[:])
            P.step("vector", q8)
        def q9(e, ts=ts):
            P.dma(e, io["og_hbm"][ts:ts + 512, :]
                  .rearrange("(a p) d -> p a d", p=128), ogs[:])
        P.step("sync", q9)
    exD.close()

    exE = ExitStack()
    sbE = lambda n, s, d: exE.enter_context(nc.sbuf_tensor(n, s, d))
    psE = lambda n, s, d: exE.enter_context(nc.psum_tensor(n, s, d))
    WoT = sbE("WoT", [128, 8, 1024], BF16)
    ogf = sbE("ogf", [128, 8, 512], BF16)
    xrE = sbE("xrE", [128, 1024], F32)
    oaE = sbE("oaE", [128, 1024], F32)
    onE = sbE("onE", [128, 1024], F32)
    stE = sbE("stE", [128, 8], F32)
    sqE = sbE("sqE", [128, 1024], F32)
    po = [psE(f"po{i}", [128, 512], F32) for i in range(2)]
    def r0(e):
        for c in range(8):
            for ob in range(2):
                P.dmaT(e, WoT[:, c, ob * 512:(ob + 1) * 512],
                       io["wo_bf"][ob * 512:(ob + 1) * 512,
                                   c * 128:(c + 1) * 128])
    P.step("sync", r0)
    for sup in range(T // 512):
        ts = sup * 512
        def r1(e, ts=ts):
            for c in range(8):
                P.dmaT(e, ogf[:, c, :], io["og_hbm"][ts:ts + 512,
                                                     c * 128:(c + 1) * 128])
        P.step("sync", r1)
        for tsub in range(4):
            tg = ts + tsub * 128
            def r2(e, tg=tg):
                P.dma(e, xrE[:], io["x"][tg:tg + 128, :])
            P.step("sync", r2)
            def r3(e, tsub=tsub):
                last = None
                for ob in range(2):
                    for c in range(8):
                        last = e.matmul(po[ob][:],
                                        ogf[:, c, tsub * 128:(tsub + 1) * 128],
                                        WoT[:, c, ob * 512:(ob + 1) * 512],
                                        start=(c == 0), stop=(c == 7))
                return last
            P.step("tensor", r3)
            def r4(e):
                e.tensor_add(oaE[:, 0:512], po[0][:], xrE[:, 0:512])
                return e.tensor_add(oaE[:, 512:1024], po[1][:],
                                    xrE[:, 512:1024])
            P.step("vector", r4)
            emit_postln(P, g, oaE, stE, onE, D=D, scr=sqE)
            def r5(e, tg=tg):
                P.dma(e, io["x3"][tg:tg + 128, :], onE[:])
            P.step("sync", r5)
    exE.close()


def emit_l1_ffn(P, nc, ex, g, io, T):
    from contextlib import ExitStack
    D = 1024
    H = 8
    NT = T // 128
    SC = 1.0 / math.sqrt(128.0)

    sb = lambda n, s, d: ex.enter_context(nc.sbuf_tensor(n, s, d))

    gv_all = sb("gv_all", [128, NT, 64], F32)
    go_all = sb("go_all", [128, NT, 64], BF16)
    rs_col = sb("rs_col", [128, 1], F32)

    ex0 = ExitStack()
    ps_pro = ex0.enter_context(nc.psum_tensor("ps_pro2", [128, 512], F32))
    emit_bcast_row(P, nc, g, io["route_scale"], rs_col, ps_pro, 1)
    def spill(e):
        for r in range(0, 1024, 128):
            P.dma(e, io["qw_bf"][r:r + 128, :], io["q_w"][r:r + 128, :])
            P.dma(e, io["kw_bf"][r:r + 128, :], io["k_w"][r:r + 128, :])
        P.dma(e, io["selv_bf"][:, :], io["sel_v_w"][:, :])
        P.dma(e, io["selo_bf"][:, :], io["sel_o_w"][:, :])
        for t0 in range(0, T, 512):
            P.dma(e, io["x_bf"][t0:t0 + 512, :], io["x"][t0:t0 + 512, :])
        for he in range(64):
            P.dma(e, io["ow_bf"][he, :, :], io["o_w"][he, :, :])
    spill_id = P.step("gpsimd", spill)
    ex0.close()

    exB = ExitStack()
    sbB = lambda n, s, d: exB.enter_context(nc.sbuf_tensor(n, s, d))
    psB = lambda n, s, d: exB.enter_context(nc.psum_tensor(n, s, d))
    QwT = sbB("QwT", [128, 8, 1024], BF16)
    KwT = sbB("KwT", [128, 8, 1024], BF16)
    SvT = sbB("SvT", [128, 8, 64], BF16)
    SoT = sbB("SoT", [128, 8, 64], BF16)
    vwh = sbB("vwh", [128, 8, 1024], BF16)
    xf2b = [sbB(f"xf2{i}", [128, 8, 512], BF16) for i in range(2)]
    prj2b = [sbB(f"prj2{i}", [128, 1024], BF16) for i in range(2)]
    probs = sbB("probs2", [128, 64], F32)
    gsc = sbB("gsc", [128, 64], F32)
    m12 = sbB("m12", [128, 32], F32)
    wvt = sbB("wvt", [128, 1024], F32)
    vtt = sbB("vtt", [128, 1024], BF16)
    vtf = sbB("vtf", [128, 128], F32)
    pp = [psB(f"pp2{i}", [128, 512], F32) for i in range(4)]
    psl = [psB(f"psl2{i}", [128, 64], F32) for i in range(2)]
    ids_sel = {}
    def s1w(e):
        for c in range(8):
            for ob in range(2):
                P.dmaT(e, QwT[:, c, ob * 512:(ob + 1) * 512],
                       io["qw_bf"][ob * 512:(ob + 1) * 512, c * 128:(c + 1) * 128])
                P.dmaT(e, KwT[:, c, ob * 512:(ob + 1) * 512],
                       io["kw_bf"][ob * 512:(ob + 1) * 512, c * 128:(c + 1) * 128])
            P.dmaT(e, SvT[:, c, :], io["selv_bf"][:, c * 128:(c + 1) * 128])
            P.dmaT(e, SoT[:, c, :], io["selo_bf"][:, c * 128:(c + 1) * 128])
    P.step("sync", s1w)
    qkd = {}
    for sup in range(T // 512):
        ts = sup * 512
        def s2a(e, ts=ts, sup=sup):
            for c in range(8):
                P.dmaT(e, xf2b[sup % 2][:, c, :],
                       io["x_bf"][ts:ts + 512, c * 128:(c + 1) * 128])
        qkd[("xf", sup)] = P.step(
            "sync", s2a,
            deps=[spill_id if sup == 0 else None,
                  qkd.get(("mm", sup - 2, 7)), ids_sel.get((sup - 2) * 4 + 3)])
        for hh in range(8):
            def s2q(e, hh=hh, sup=sup):
                for c in range(8):
                    e.matmul(pp[2 * (hh % 2)][:],
                             QwT[:, c, hh * 128:(hh + 1) * 128],
                             xf2b[sup % 2][:, c, :], start=(c == 0), stop=(c == 7))
                last = None
                for c in range(8):
                    last = e.matmul(pp[2 * (hh % 2) + 1][:],
                                    KwT[:, c, hh * 128:(hh + 1) * 128],
                                    xf2b[sup % 2][:, c, :], start=(c == 0),
                                    stop=(c == 7))
                return last
            qkd[("mm", sup, hh)] = P.step(
                "tensor", s2q,
                deps=[qkd[("xf", sup)] if hh == 0 else None,
                      qkd.get(("ev", sup, hh - 2)),
                      qkd.get(("ev", sup - 1, 6 + hh)) if hh < 2 else None])
            def s2e(e, hh=hh):
                e.tensor_scalar_mul(prj2b[hh % 2][:, 0:512],
                                    pp[2 * (hh % 2)][:], SC)
                return e.tensor_copy(prj2b[hh % 2][:, 512:1024],
                                     pp[2 * (hh % 2) + 1][:])
            qkd[("ev", sup, hh)] = P.step(
                "vector", s2e, deps=[qkd[("mm", sup, hh)],
                                     qkd.get(("wr", sup, hh - 2)),
                                     qkd.get(("wr", sup - 1, 6 + hh)) if hh < 2 else None])
            def s2w(e, hh=hh, ts=ts):
                P.dma(e, io["qf2_hbm"][hh * 128:(hh + 1) * 128, ts:ts + 512],
                      prj2b[hh % 2][:, 0:512])
                P.dma(e, io["kf2_hbm"][hh * 128:(hh + 1) * 128, ts:ts + 512],
                      prj2b[hh % 2][:, 512:1024])
            qkd[("wr", sup, hh)] = P.step("sync", s2w,
                                          deps=[qkd[("ev", sup, hh)]])
        for tsub in range(4):
            nt = (ts + tsub * 128) // 128
            def s3m(e, tsub=tsub, sup=sup):
                for c in range(8):
                    e.matmul(psl[0][:], xf2b[sup % 2][:, c, tsub * 128:(tsub + 1) * 128],
                             SvT[:, c, :], start=(c == 0), stop=(c == 7))
                last = None
                for c in range(8):
                    last = e.matmul(psl[1][:], xf2b[sup % 2][:, c, tsub * 128:(tsub + 1) * 128],
                                    SoT[:, c, :], start=(c == 0), stop=(c == 7))
                return last
            P.step("tensor", s3m)
            for which, psi in (("v", 0), ("o", 1)):
                gall = gv_all if which == "v" else go_all
                def s4a(e, psi=psi):
                    return e.activation(probs[:], psl[psi][:], AF.Sigmoid)
                P.step("scalar", s4a)
                def s4b(e):
                    p3 = probs[:].rearrange("q (h e2) -> q h e2", e2=8)
                    return e.tensor_reduce(m12[:, 0:8], p3, AX.X, ALU.max)
                P.step("vector", s4b)
                def s4c(e):
                    m1b = m12[:, 0:8].rearrange("q (h o) -> q h o", o=1) \
                        .broadcast_to([128, 8, 8])
                    p3 = probs[:].rearrange("q (h e2) -> q h e2", e2=8)
                    return e.tensor_tensor(gsc[:].rearrange("q (h e2) -> q h e2", e2=8),
                                           p3, m1b, ALU.is_ge)
                P.step("vector", s4c)
                def s4d(e):
                    return e.scalar_tensor_tensor(gsc[:], gsc[:], -1e30,
                                                  probs[:], ALU.mult, ALU.add)
                P.step("vector", s4d)
                def s4e(e):
                    g3 = gsc[:].rearrange("q (h e2) -> q h e2", e2=8)
                    return e.tensor_reduce(m12[:, 8:16], g3, AX.X, ALU.max)
                P.step("vector", s4e)
                def s4f(e):
                    m2b = m12[:, 8:16].rearrange("q (h o) -> q h o", o=1) \
                        .broadcast_to([128, 8, 8])
                    p3 = probs[:].rearrange("q (h e2) -> q h e2", e2=8)
                    return e.tensor_tensor(gsc[:].rearrange("q (h e2) -> q h e2", e2=8),
                                           p3, m2b, ALU.is_ge)
                P.step("vector", s4f)
                def s4g(e):
                    return e.tensor_add(m12[:, 16:24], m12[:, 0:8],
                                        m12[:, 8:16])
                P.step("vector", s4g)
                P.step("vector", lambda e: e.tensor_scalar_max(
                    m12[:, 16:24], m12[:, 16:24], 1e-9))
                P.step("vector", lambda e: e.reciprocal(m12[:, 24:32],
                                                        m12[:, 16:24]))
                def s4h(e):
                    return e.tensor_scalar_mul(m12[:, 24:32], m12[:, 24:32],
                                               rs_col[:])
                P.step("vector", s4h)
                P.step("vector", lambda e: e.tensor_mul(gsc[:], gsc[:],
                                                        probs[:]))
                def s4i(e, nt=nt, gall=gall):
                    fb = m12[:, 24:32].rearrange("q (h o) -> q h o", o=1) \
                        .broadcast_to([128, 8, 8])
                    return e.tensor_tensor(
                        gall[:, nt, :].rearrange("q (h e2) -> q h e2", e2=8),
                        gsc[:].rearrange("q (h e2) -> q h e2", e2=8), fb,
                        ALU.mult)
                sid4 = P.step("vector", s4i)
                if which == "v":
                    ids_sel[nt] = sid4
    xfv = [sbB(f"xfv{i}", [128, 8, 512], BF16) for i in range(2)]
    vwh2 = [sbB(f"vwh2{i}", [128, 8, 1024], BF16) for i in range(2)]
    ids = {}
    NSUP = T // 512
    for hh in range(8):
        def sv0(e, hh=hh):
            for dc in range(8):
                P.dma(e, vwh2[hh % 2][:, dc, :].rearrange("p (e p2) -> p e p2", p2=128),
                      io["v_w"][hh * 8:(hh + 1) * 8, dc * 128:(dc + 1) * 128, :]
                      .rearrange("e dp p2 -> dp e p2"))
        ids[("vw", hh)] = P.step("gpsimd", sv0,
                                 deps=[ids.get(("mm", hh - 2, NSUP - 1, 3))])
        for sup in range(NSUP):
            it = hh * NSUP + sup
            ts = sup * 512
            def sv1(e, ts=ts, it=it):
                for c in range(8):
                    P.dmaT(e, xfv[it % 2][:, c, :],
                           io["x_bf"][ts:ts + 512, c * 128:(c + 1) * 128])
            pit = it - 2
            ids[("ld", it)] = P.step("sync", sv1,
                                     deps=[spill_id,
                                           ids.get(("mm", pit // NSUP, pit % NSUP, 3))
                                           if pit >= 0 else None])
            for tsub in range(4):
                nt = (ts + tsub * 128) // 128
                def s5m(e, hh=hh, tsub=tsub, it=it):
                    last = None
                    for ob in range(2):
                        for c in range(8):
                            last = e.matmul(pp[ob][:],
                                            xfv[it % 2][:, c, tsub * 128:(tsub + 1) * 128],
                                            vwh2[hh % 2][:, c, ob * 512:(ob + 1) * 512],
                                            start=(c == 0), stop=(c == 7))
                    return last
                ids[("mm", hh, sup, tsub)] = P.step(
                    "tensor", s5m,
                    deps=[ids[("ld", it)], ids[("vw", hh)],
                          ids_sel[NT - 1] if (hh == 0 and sup == 0 and tsub == 0) else None,
                          ids.get(("gt", hh, sup, tsub - 1)),
                          ids.get(("gt", hh, sup - 1, 3)) if tsub == 0 else None])
                def s5g(e, hh=hh, nt=nt):
                    gb = gv_all[:, nt, hh * 8:(hh + 1) * 8] \
                        .rearrange("q (e2 o) -> q e2 o", o=1) \
                        .broadcast_to([128, 8, 128])
                    e.tensor_tensor(wvt[:, 0:512].rearrange("q (e2 p2) -> q e2 p2", p2=128),
                                    pp[0][:].rearrange("q (e2 p2) -> q e2 p2", p2=128),
                                    gb[:, 0:4, :], ALU.mult)
                    return e.tensor_tensor(
                        wvt[:, 512:1024].rearrange("q (e2 p2) -> q e2 p2", p2=128),
                        pp[1][:].rearrange("q (e2 p2) -> q e2 p2", p2=128),
                        gb[:, 4:8, :], ALU.mult)
                ids[("gt", hh, sup, tsub)] = P.step(
                    "vector", s5g, deps=[ids[("mm", hh, sup, tsub)],
                                         ids_sel[nt],
                                         ids.get(("wr", hh, sup, tsub - 1)),
                                         ids.get(("wr", hh, sup - 1, 3)) if tsub == 0 else None])
                def s5r(e):
                    wv3 = wvt[:].rearrange("q (e2 p2) -> q p2 e2", p2=128)
                    return e.tensor_reduce(vtf[:], wv3, AX.X, ALU.add)
                P.step("vector", s5r)
                P.step("vector", lambda e: e.tensor_copy(vtt[:, 0:128], vtf[:]))
                def s5w(e, hh=hh, ts=ts, tsub=tsub):
                    P.dma(e, io["vtok2_hbm"][ts + tsub * 128:ts + (tsub + 1) * 128,
                                             hh * 128:(hh + 1) * 128],
                          vtt[:, 0:128])
                ids[("wr", hh, sup, tsub)] = P.step(
                    "gpsimd", s5w, deps=[len(P.steps) - 1])
    exB.close()

    exC = ExitStack()
    sbC = lambda n, s, d: exC.enter_context(nc.sbuf_tensor(n, s, d))
    psC = lambda n, s, d: exC.enter_context(nc.psum_tensor(n, s, d))
    kfh = [sbC(f"kfh2{i}", [128, T], BF16) for i in range(2)]
    qfh = [sbC(f"qfh2{i}", [128, T], BF16) for i in range(2)]
    vth = [sbC(f"vth2{i}", [128, NT, 128], BF16) for i in range(2)]
    expT = sbC("expT2", [128, NT, 512], BF16)
    srow = sbC("srow2", [1, 512], F32)
    ctxs = sbC("ctxs", [128, 512], F32)
    ctxb = sbC("ctxb", [128, 512], BF16)
    psc = [psC(f"psc2{i}", [128, 1024], F32) for i in range(2)]
    pctx = psC("pctx2", [128, 512], F32)
    psum_s = psC("psum_s2", [1, 512], F32)
    pbc2 = psC("pbc2", [128, 512], F32)
    jd = {}
    NQB = T // 512
    for hh in range(H):
        def t1(e, hh=hh):
            P.dma(e, kfh[hh % 2][:], io["kf2_hbm"][hh * 128:(hh + 1) * 128, :])
            P.dma(e, qfh[hh % 2][:], io["qf2_hbm"][hh * 128:(hh + 1) * 128, :])
            P.dma(e, vth[hh % 2][:], io["vtok2_hbm"][:, hh * 128:(hh + 1) * 128]
                  .rearrange("(k p) f -> p k f", p=128))
        jd[("ld", hh)] = P.step("sync", t1,
                                deps=[ids[("wr", hh, NSUP - 1, 3)],
                                      ids[("wr", 7, NSUP - 1, 3)] if hh == 0 else None,
                                      jd.get(("ctx", hh - 2, NQB - 1))])
        for qb in range(NQB):
            qs = qb * 512
            NG = (NT + 1) // 2
            for gg in range(NG):
                kk = [k for k in (2 * gg, 2 * gg + 1) if k < NT]
                def t2(e, kk=kk, qs=qs, hh=hh, gg=gg):
                    last = None
                    for j, k in enumerate(kk):
                        last = e.matmul(psc[gg % 2][:, j * 512:(j + 1) * 512],
                                        kfh[hh % 2][:, k * 128:(k + 1) * 128],
                                        qfh[hh % 2][:, qs:qs + 512],
                                        start=True, stop=True)
                    return last
                jd[("mm", hh, qb, gg)] = P.step(
                    "tensor", t2,
                    deps=[jd[("ld", hh)] if gg == 0 and qb == 0 else None,
                          jd.get(("ex", hh, qb, gg - 2)),
                          jd.get(("ex", hh, qb - 1, NG - 2 + gg)) if gg < 2 else None])
                def t3(e, kk=kk, gg=gg):
                    last = None
                    for j, k in enumerate(kk):
                        last = e.activation(expT[:, k, :],
                                            psc[gg % 2][:, j * 512:(j + 1) * 512],
                                            AF.Exp)
                    return last
                jd[("ex", hh, qb, gg)] = P.step(
                    "scalar", t3,
                    deps=[jd[("mm", hh, qb, gg)],
                          jd.get(("ctx", hh, qb - 1))])
            def t4(e, qs=qs, hh=hh):
                for k in range(NT):
                    e.matmul(pctx[:], vth[hh % 2][:, k, :], expT[:, k, :],
                             start=(k == 0), stop=(k == NT - 1))
                last = None
                for k in range(NT):
                    last = e.matmul(psum_s[:], g.ones_colb[:], expT[:, k, :],
                                    start=(k == 0), stop=(k == NT - 1))
                return last
            jd[("ctxmm", hh, qb)] = P.step(
                "tensor", t4, deps=[jd[("ex", hh, qb, NG - 1)],
                                    jd.get(("cx7", hh, qb - 1))])
            jd[("sr", hh, qb)] = P.step(
                "vector", lambda e: e.reciprocal(srow[:], psum_s[:]),
                deps=[jd[("ctxmm", hh, qb)]])
            def t5(e):
                return e.matmul(pbc2[:], g.ones_row[:], srow[:],
                                start=True, stop=True)
            P.step("tensor", t5, deps=[jd[("sr", hh, qb)]])
            def t6(e):
                return e.tensor_copy(ctxs[:], pctx[:])
            P.step("vector", t6)
            def t7(e):
                return e.tensor_mul(ctxb[:], ctxs[:], pbc2[:])
            prev_ctx = jd.get(("ctx", hh, qb - 1)) if qb > 0 else \
                jd.get(("ctx", hh - 1, NQB - 1))
            jd[("cx7", hh, qb)] = P.step("vector", t7,
                                         deps=[len(P.steps) - 1, prev_ctx])
            def t8(e, hh=hh, qs=qs):
                P.dma(e, io["ctx_hbm"][hh * 128:(hh + 1) * 128, qs:qs + 512],
                      ctxb[:])
            jd[("ctx", hh, qb)] = P.step("sync", t8)
    exC.close()

    exD = ExitStack()
    sbD = lambda n, s, d: exD.enter_context(nc.sbuf_tensor(n, s, d))
    psD = lambda n, s, d: exD.enter_context(nc.psum_tensor(n, s, d))
    goT = sbD("goT", [64, T], BF16)
    Estr = sbD("Estr", [64, 8192], BF16)
    ctxsup = sbD("ctxsup", [128, 8, 512], BF16)
    wctx = sbD("wctx", [128, 64, 512], BF16)
    owg = [sbD(f"owg{i}", [128, 8, 1024], BF16) for i in range(2)]
    xrD = sbD("xrD", [128, 4, 1024], F32)
    oaD = sbD("oaD", [128, 1024], F32)
    onD = sbD("onD", [128, 1024], F32)
    stD2 = sbD("stD2", [128, 8], F32)
    sqD2 = sbD("sqD2", [128, 1024], F32)
    kd = {}
    def u0(e):
        return e.iota(Estr[:], [[-1, 64], [0, 128]], base=0,
                      channel_multiplier=1,
                      allow_small_or_imprecise_dtypes=True)
    P.step("gpsimd", u0)
    P.step("vector", lambda e: e.tensor_single_scalar(Estr[:], Estr[:], 0.0,
                                                      ALU.is_equal))
    u2_last = None
    with nc.psum_tensor("pgT3", [64, 128], BF16) as pgT:
        for n in range(NT):
            def u1(e, n=n):
                return e.transpose(pgT[:], go_all[:, n, :], g.identb2[:])
            P.step("tensor", u1)
            def u2(e, n=n):
                return e.tensor_copy(goT[:, n * 128:(n + 1) * 128], pgT[:])
            u2_last = P.step("vector", u2)
    for tb in range(T // 512):
        ts = tb * 512
        def u3(e, ts=ts):
            for hh2 in range(8):
                P.dma(e, ctxsup[:, hh2, :],
                      io["ctx_hbm"][hh2 * 128:(hh2 + 1) * 128, ts:ts + 512])
            P.dma(e, xrD[:], io["x"][ts:ts + 512, :]
                  .rearrange("(a p) d -> p a d", p=128))
        kd[("ld", tb)] = P.step("sync", u3,
                                deps=[jd[("ctx", H - 1, NQB - 1)]
                                      if tb == 0 else None,
                                      kd.get(("ln", tb - 1, 3))])
        with nc.psum_tensor(f"pwc4_{tb}", [128, 2, 2, 512], F32) as pwc4:
            NHG = 32
            for hg in range(NHG):
                def u5(e, hg=hg, ts=ts):
                    last = None
                    for j in range(2):
                        he = hg * 2 + j
                        last = e.matmul(pwc4[:, hg % 2, j, :],
                                        Estr[:, he * 128:(he + 1) * 128],
                                        goT[:, ts:ts + 512],
                                        start=True, stop=True)
                    return last
                kd[("bc", tb, hg)] = P.step(
                    "tensor", u5,
                    deps=[kd[("ld", tb)] if hg == 0 else None,
                          u2_last if (tb == 0 and hg == 0) else None,
                          kd.get(("wc", tb, hg - 2))])
                def u6(e, hg=hg):
                    last = None
                    for j in range(2):
                        he = hg * 2 + j
                        last = e.tensor_tensor(
                            wctx[:, he:he + 1, :],
                            ctxsup[:, he // 8:he // 8 + 1, :],
                            pwc4[:, hg % 2, j:j + 1, :], ALU.mult)
                    return last
                kd[("wc", tb, hg)] = P.step(
                    "vector", u6, deps=[kd[("bc", tb, hg)],
                                        kd.get(("omm", tb - 1, 7)) if hg < 2 else None])
        with nc.psum_tensor(f"pO_{tb}", [128, 8, 512], F32) as pO:
            for ho in range(8):
                def u8(e, ho=ho):
                    for dc2 in range(8):
                        P.dma(e, owg[ho % 2][:, dc2, :],
                              io["ow_bf"][ho * 8 + dc2, :, :])
                kd[("owl", tb, ho)] = P.step(
                    "gpsimd", u8, deps=[kd.get(("omm", tb, ho - 2)),
                                        kd.get(("omm", tb - 1, 7)) if ho < 2 else None])
                def u9(e, ho=ho):
                    last = None
                    for j in range(8):
                        he = ho * 8 + j
                        for tsub in range(4):
                            for dh in range(2):
                                last = e.matmul(
                                    pO[:, tsub * 2 + dh, :],
                                    wctx[:, he, tsub * 128:(tsub + 1) * 128],
                                    owg[ho % 2][:, j, dh * 512:(dh + 1) * 512],
                                    start=(he == 0), stop=(he == 63))
                    return last
                kd[("omm", tb, ho)] = P.step(
                    "tensor", u9, deps=[kd[("owl", tb, ho)],
                                        kd[("wc", tb, 31)] if ho == 0 else None])
            for tsub in range(4):
                tg = ts + tsub * 128
                def u10(e, tsub=tsub):
                    e.tensor_add(oaD[:, 0:512], pO[:, tsub * 2, :],
                                 xrD[:, tsub, 0:512])
                    return e.tensor_add(oaD[:, 512:1024], pO[:, tsub * 2 + 1, :],
                                        xrD[:, tsub, 512:1024])
                P.step("vector", u10, deps=[kd[("omm", tb, 7)],
                                            len(P.steps) - 1 if tsub > 0 else kd[("omm", tb, 7)]])
                emit_postln(P, g, oaD, stD2, onD, D=D, scr=sqD2)
                def u11(e, tg=tg):
                    P.dma(e, io["x4"][tg:tg + 128, :], onD[:])
                kd[("ln", tb, tsub)] = P.step("sync", u11)
    exD.close()


from contextlib import ExitStack as _ExitStack

_T, _D = 8192, 1024


def _decl(nc, names, T=_T):
    io = {}
    io["x"] = nc.dram_tensor("x", [T, 1024], F32, kind="ExternalInput").ap()
    for nm, shp in names["in"]:
        io[nm] = nc.dram_tensor(nm, shp, F32, kind="ExternalInput").ap()
    io[names["out"]] = nc.dram_tensor(names["out"], [T, 1024], F32,
                                      kind="ExternalOutput").ap()
    for nm, shp in names["scratch"]:
        io[nm] = nc.dram_tensor(nm, shp, BF16, kind="Internal").ap()
    return io


_L0A_NAMES = {
    "in": [("ln_g", [1024]), ("ln_b", [1024]), ("W_hidden", [1024, 4096]),
           ("b_hidden", [4096]), ("W_qk", [1024, 128]), ("b_qk", [128]),
           ("os_gamma", [2, 128]), ("os_beta", [2, 128]),
           ("W_out", [2048, 1024]), ("b_out", [1024])],
    "out": "x1",
    "scratch": [("nrm_hbm", [_T, 1024]), ("hid_hbm", [_T, 4096])],
}
_L0F_NAMES = {
    "in": [("norm_gamma", [1024]), ("slot_norm_gamma", [1024]),
           ("slot_embeds", [8, 512, 1024]), ("w1", [8, 1024, 4096]),
           ("b1", [8, 4096]), ("w2", [8, 4096, 1024]), ("b2", [8, 1024])],
    "out": "x2",
    "scratch": [("xn_hbm", [_T, 1024]), ("sn_hbm", [4096, 1024]),
                ("lg_hbm", [_T, 4096]), ("lglo_hbm", [_T, 4096]),
                ("ce_hbm", [_T, 4096]), ("disp_hbm", [4096, _T]),
                ("slots_hbm", [1024, 4096])],
}
_L1A_NAMES = {
    "in": [("Wq", [1024, 1024]), ("Wk", [1024, 1024]), ("Wv", [1024, 1024]),
           ("Wg1", [128, 1024]), ("Wg2", [1024, 128]),
           ("g_norm_weight", [1024]), ("Wo", [1024, 1024])],
    "out": "x3",
    "scratch": [("x_bf", [_T, 1024]), ("wq_bf", [1024, 1024]),
                ("wk_bf", [1024, 1024]), ("wv_bf", [1024, 1024]),
                ("wo_bf", [1024, 1024]), ("wg1_bf", [128, 1024]),
                ("wg2_bf", [1024, 128]), ("qf_hbm", [1024, _T]),
                ("kf_hbm", [1024, _T]), ("vtok_hbm", [_T, 1024]),
                ("o_hbm", [_T, 1024]), ("og_hbm", [_T, 1024])],
}
_L1F_NAMES = {
    "in": [("q_w", [1024, 1024]), ("k_w", [1024, 1024]),
           ("v_w", [64, 1024, 128]), ("o_w", [64, 128, 1024]),
           ("sel_v_w", [64, 1024]), ("sel_o_w", [64, 1024]),
           ("route_scale", [1])],
    "out": "x4",
    "scratch": [("x_bf", [_T, 1024]), ("qw_bf", [1024, 1024]),
                ("kw_bf", [1024, 1024]), ("selv_bf", [64, 1024]),
                ("selo_bf", [64, 1024]), ("ow_bf", [64, 128, 1024]),
                ("qf2_hbm", [1024, _T]), ("kf2_hbm", [1024, _T]),
                ("vtok2_hbm", [_T, 1024]), ("ctx_hbm", [1024, _T])],
}


def _build(names, emit):
    nc = bass.Bass("TRN2", target_bir_lowering=False, debug=False)
    io = _decl(nc, names)
    with _ExitStack() as ex:
        sems = {}
        for s in ["cc"] + ["dm_" + e for e in ENGINES] + ["st_" + e for e in ENGINES]:
            sems[s] = ex.enter_context(nc.semaphore(s))
        blk = ex.enter_context(nc.Block())
        P = Prog(nc)
        g = alloc_globals(P, nc, ex)
        g.row_scr = ex.enter_context(nc.sbuf_tensor("row_scr", [1, 1024], F32))
        emit(P, nc, ex, g, io, _T)
        P.emit(blk, sems)
    return nc


_CACHE = {}

def _get_nc(key):
    if key not in _CACHE:
        if key == "l0a":
            _CACHE[key] = _build(_L0A_NAMES, emit_l0_attn)
        elif key == "l0f":
            _CACHE[key] = _build(_L0F_NAMES, emit_l0_ffn)
        elif key == "l1a":
            _CACHE[key] = _build(_L1A_NAMES, emit_l1_attn)
        elif key == "l1f":
            _CACHE[key] = _build(_L1F_NAMES, emit_l1_ffn)
    return _CACHE[key]


def _launch(key, names, xs, winputs):
    nc = _get_nc(key)
    in_maps = []
    for c in range(8):
        m = {"x": np.ascontiguousarray(xs[c % 4])}
        for nm, _ in names["in"]:
            m[nm] = winputs[nm]
        in_maps.append(m)
    res = run_bass_kernel_spmd(nc, in_maps, core_ids=list(range(8)))
    return [res.results[c][names["out"]] for c in range(4)]


def kernel(**inputs):
    xs = [np.ascontiguousarray(inputs["L0_attn_x"][b]) for b in range(4)]
    w = lambda pre, names: {nm: np.ascontiguousarray(inputs[pre + nm])
                            for nm, _ in names["in"]}
    xs = _launch("l0a", _L0A_NAMES, xs, w("L0_attn_", _L0A_NAMES))
    xs = _launch("l0f", _L0F_NAMES, xs, w("L0_ffn_", _L0F_NAMES))
    xs = _launch("l1a", _L1A_NAMES, xs, w("L1_attn_", _L1A_NAMES))
    xs = _launch("l1f", _L1F_NAMES, xs, w("L1_ffn_", _L1F_NAMES))
    return np.stack(xs, axis=0).astype(np.float32)



def _decl_block(nc, names, pre, io_x_ap, out_kind_final):
    io = {"x": io_x_ap}
    for nm, shp in names["in"]:
        io[nm] = nc.dram_tensor(pre + nm, shp, F32, kind="ExternalInput").ap()
    if out_kind_final:
        io[names["out"]] = nc.dram_tensor("out", [_T, 1024], F32,
                                          kind="ExternalOutput").ap()
    else:
        io[names["out"]] = nc.dram_tensor(pre + names["out"], [_T, 1024], F32,
                                          kind="Internal").ap()
    for nm, shp in names["scratch"]:
        io[nm] = nc.dram_tensor(pre + nm, shp, BF16, kind="Internal").ap()
    return io


def _build_fused():
    nc = bass.Bass("TRN2", target_bir_lowering=False, debug=False)
    x0 = nc.dram_tensor("x", [_T, 1024], F32, kind="ExternalInput").ap()
    io_a = _decl_block(nc, _L0A_NAMES, "A_", x0, False)
    io_b = _decl_block(nc, _L0F_NAMES, "B_", io_a["x1"], False)
    io_c = _decl_block(nc, _L1A_NAMES, "C_", io_b["x2"], False)
    io_d = _decl_block(nc, _L1F_NAMES, "D_", io_c["x3"], True)
    with _ExitStack() as ex:
        sems = {}
        for s in ["cc"] + ["dm_" + e for e in ENGINES] + ["st_" + e for e in ENGINES]:
            sems[s] = ex.enter_context(nc.semaphore(s))
        blk = ex.enter_context(nc.Block())
        P = Prog(nc)
        g = alloc_globals(P, nc, ex)
        g.row_scr = ex.enter_context(nc.sbuf_tensor("row_scr", [1, 1024], F32))
        for emit, io in ((emit_l0_attn, io_a), (emit_l0_ffn, io_b),
                         (emit_l1_attn, io_c), (emit_l1_ffn, io_d)):
            exb = _ExitStack()
            emit(P, nc, exb, g, io, _T)
            exb.close()
        P.emit(blk, sems)
    return nc


def kernel_fused(**inputs):
    if "fused" not in _CACHE:
        _CACHE["fused"] = _build_fused()
    nc = _CACHE["fused"]
    in_maps = []
    for c in range(8):
        m = {"x": np.ascontiguousarray(inputs["L0_attn_x"][c % 4])}
        for pre, names in (("A_", _L0A_NAMES), ("B_", _L0F_NAMES),
                           ("C_", _L1A_NAMES), ("D_", _L1F_NAMES)):
            kpre = {"A_": "L0_attn_", "B_": "L0_ffn_", "C_": "L1_attn_",
                    "D_": "L1_ffn_"}[pre]
            for nm, _ in names["in"]:
                m[pre + nm] = np.ascontiguousarray(inputs[kpre + nm])
        in_maps.append(m)
    res = run_bass_kernel_spmd(nc, in_maps, core_ids=list(range(8)))
    return np.stack([res.results[c]["out"] for c in range(4)], axis=0) \
        .astype(np.float32)


kernel = kernel_fused
